# version 26
# baseline (speedup 1.0000x reference)
import contextlib
import numpy as np
import ml_dtypes

import concourse.bass as bass
import concourse.mybir as mybir
from concourse.bass_utils import run_bass_kernel_spmd

F32 = mybir.dt.float32
BF16 = mybir.dt.bfloat16
AF = mybir.ActivationFunctionType
ALU = mybir.AluOpType
AX = mybir.AxisListType

B, T, C = 4, 4096, 1024
NH, HD, FF = 16, 64, 2816
NT = T // 128
OWN = 16
LN6 = float(np.log(np.float64(6.0)))
EPS = 1e-5
_QT = [0.0625, 0.5625, 1.5625, 3.0625, 6.25, 12.25, 25.0]

DEBUG = False
PHLIM = 3
ATTLIM = 5
NQT = 4
NQC = 4
EXPON = True


def _decay_values():
    WM_DECAY_LENGTH, WM_ECHOIC = 2048, 16
    dl = WM_DECAY_LENGTH - WM_ECHOIC + 1
    nums = np.linspace(0.0, 1.0, dl, dtype=np.float64)
    dv = 1.0 - np.power(nums, 1.0 / np.e)
    dv = np.concatenate([np.ones(WM_ECHOIC - 1), dv]).astype(np.float32)
    out = np.zeros(T + 256, dtype=np.float32)
    out[:2048] = dv
    return out


def _decay_table(p):
    dv = _decay_values()
    r = np.arange(128)[:, None]
    c = np.arange(128)[None, :]
    tab = np.zeros((18, 128, 128), dtype=np.float32)
    for idx in range(18):
        d = idx - 1 + p
        dist = d * 128 + (c - r)
        tab[idx] = dv[np.clip(dist, 0, T - 1)] * (dist >= 0)
    tab = tab.transpose(1, 0, 2).reshape(128, 18 * 128)
    return np.ascontiguousarray(tab.astype(ml_dtypes.bfloat16))


def _masks(p):
    r = np.arange(128)[:, None]
    c = np.arange(128)[None, :]
    tri = (c >= r).astype(np.float32)
    if p == 0:
        ma, mb = tri, np.zeros((128, 128), np.float32)
    else:
        ma, mb = np.ones((128, 128), np.float32), tri
    return (np.ascontiguousarray(ma.astype(ml_dtypes.bfloat16)),
            np.ascontiguousarray(mb.astype(ml_dtypes.bfloat16)))


ENGS = ("tensor", "vector", "scalar", "gpsimd", "sync")


class Prog:

    def __init__(self, nc, ctx):
        self.ops = []
        self.ctr = {}
        self.sem = {}
        for e in ENGS:
            for k in (e, e + "_d"):
                self.sem[k] = ctx.enter_context(nc.semaphore("c_" + k))
                self.ctr[k] = 0

    @staticmethod
    def _flat(waits):
        out = []
        for w in waits:
            if w is None:
                continue
            if isinstance(w, list):
                out.extend(x for x in w if x is not None and x[1] > 0)
            elif w[1] > 0:
                out.append(w)
        return out

    def emit(self, eng, fn, waits=(), mark=False, dma=False):
        key = eng + "_d" if dma else eng
        inc = 16 if dma else (1 if mark else 0)
        if inc:
            self.ctr[key] += inc
        self.ops.append([eng, fn, self._flat(waits), inc, key])
        return (key, self.ctr[key])

    def tok(self, eng):
        return [(eng, self.ctr[eng]), (eng + "_d", self.ctr[eng + "_d"])]

    def finalize(self):
        waited = {}
        for op in self.ops:
            for (we, wv) in op[2]:
                if we.endswith("_d"):
                    waited.setdefault(we, set()).add(wv)
        run = {k: 0 for k in self.ctr}
        for op in self.ops:
            eng, fn, waits, inc, key = op
            if inc and key.endswith("_d"):
                pre = run[key]
                fences = [w for w in waited.get(key, ()) if pre >= w > pre - 16
                          or (pre >= w and any(True for _ in ()))]
                run[key] = pre + inc
        last_fenced = {k: 0 for k in self.ctr}
        run = {k: 0 for k in self.ctr}
        for op in self.ops:
            eng, fn, waits, inc, key = op
            if inc and key.endswith("_d"):
                pre = run[key]
                ws = sorted(w for w in waited.get(key, ())
                            if last_fenced[key] < w <= pre)
                if ws:
                    op[2] = [(key, ws[-1])] + op[2]
                    last_fenced[key] = ws[-1]
                run[key] = pre + inc

    def replay(self, name, e):
        for eng, fn, waits, inc, key in self.ops:
            if eng != name:
                continue
            for (we, wv) in waits:
                e.wait_ge(self.sem[we], wv)
            inst = fn(e)
            if inc:
                inst.then_inc(self.sem[key], inc)


def build_program(phlim=None):
    if phlim is None:
        phlim = PHLIM
    attlim = ATTLIM
    nqt, nqc, expon = NQT, NQC, EXPON
    nc = bass.Bass("TRN2", target_bir_lowering=False)
    okind = "ExternalOutput" if DEBUG else "Internal"

    d_x = nc.dram_tensor("x_full", [T, C], F32, kind="ExternalInput").ap()
    d_xo = nc.dram_tensor("x_own", [OWN * 128, C], F32, kind="ExternalInput").ap()
    d_wa = nc.dram_tensor("w_attn", [C, 3 * C], F32, kind="ExternalInput").ap()
    d_wp = nc.dram_tensor("w_proj", [C, C], F32, kind="ExternalInput").ap()
    d_wg = nc.dram_tensor("w_gate_sh", [FF // 8, C], F32,
                          kind="ExternalInput").ap()
    d_wu = nc.dram_tensor("w_up_sh", [FF // 8, C], F32,
                          kind="ExternalInput").ap()
    d_wd = nc.dram_tensor("w_down_sh", [C // 8, FF], F32,
                          kind="ExternalInput").ap()
    d_dec = nc.dram_tensor("decay_tab", [128, 18 * 128], BF16,
                           kind="ExternalInput").ap()
    d_ma = nc.dram_tensor("mask_a", [128, 128], BF16, kind="ExternalInput").ap()
    d_mb = nc.dram_tensor("mask_b", [128, 128], BF16, kind="ExternalInput").ap()
    d_id = nc.dram_tensor("ident", [128, 128], BF16, kind="ExternalInput").ap()
    d_out = nc.dram_tensor("out", [OWN * 128, C], F32, kind="ExternalOutput").ap()
    s_xb = nc.dram_tensor("s_xb", [T, C], BF16, kind="Internal").ap()
    s_xob = nc.dram_tensor("s_xob", [OWN * 128, C], BF16, kind="Internal").ap()
    s_wgq = nc.dram_tensor("s_wgq", [FF, C], BF16, kind="Internal").ap()
    s_wgi = nc.dram_tensor("s_wgi", [FF // 8, C], BF16, kind="Internal").ap()
    s_wui = nc.dram_tensor("s_wui", [FF // 8, C], BF16, kind="Internal").ap()
    s_wdi = nc.dram_tensor("s_wdi", [C // 8, FF], BF16, kind="Internal").ap()
    s_wuq = nc.dram_tensor("s_wuq", [FF, C], BF16, kind="Internal").ap()
    s_wdq = nc.dram_tensor("s_wdq", [C, FF], BF16, kind="Internal").ap()
    s_xq = nc.dram_tensor("s_xq", [OWN * 128, C], BF16, kind=okind).ap()
    s_x1 = nc.dram_tensor("s_x1", [OWN * 128, C], F32, kind=okind).ap()
    s_y = (nc.dram_tensor("s_y", [1024, 2048], BF16, kind="ExternalOutput").ap()
           if DEBUG else None)

    ctx = contextlib.ExitStack()
    en = ctx.enter_context
    P = Prog(nc, ctx)

    mska = en(nc.sbuf_tensor("mska", [128, 128], BF16))
    mskb = en(nc.sbuf_tensor("mskb", [128, 128], BF16))
    onesc = en(nc.sbuf_tensor("onesc", [128, 1], BF16))
    onesr = en(nc.sbuf_tensor("onesr", [128, 64], BF16))
    idn = en(nc.sbuf_tensor("idn", [128, 128], BF16))
    lns = en(nc.sbuf_tensor("lns", [128, 2, 10], F32))
    qw = en(nc.sbuf_tensor("qw", [128, 512], F32))
    qb = en(nc.sbuf_tensor("qb", [128, 11, 512], BF16))
    qm = en(nc.sbuf_tensor("qm", [128, 4, 32], F32))
    qout = en(nc.sbuf_tensor("qout", [128, 2, 512], BF16))
    qws0 = {"qm": qm, "qw": qw, "qb": qb, "prev": None}
    ps = en(nc.psum_tensor("ps", [128, 8, 512], F32))
    psx = ps[:, 6, :]
    pst = ps[:, 7, :].bitcast(BF16)

    

    def emit_quant_v(src_ap, dst_ap, fd, src_tok=None, dst_tok=None,
                     np_=128, ws=None):
        if ws is None:
            ws = qws0
        qm, qw, qb = ws["qm"], ws["qw"], ws["qb"]
        nb = fd // 16
        t0 = P.emit("vector", lambda e: e.tensor_reduce(
            qm[0:np_, 0, 0:nb], src_ap.rearrange("p (b s) -> p b s", s=16),
            axis=AX.X, op=ALU.max, apply_absolute_value=True),
            waits=[src_tok], mark=True)
        vt2 = P.emit("vector", lambda e: e.tensor_single_scalar(
            qm[0:np_, 3, 0:nb], qm[0:np_, 0, 0:nb], 1.0 / 12.0, ALU.mult),
            waits=[t0], mark=True)
        m2 = P.emit("vector", lambda e: e.tensor_tensor(
            qm[0:np_, 1, 0:nb], qm[0:np_, 0, 0:nb], qm[0:np_, 0, 0:nb],
            op=ALU.mult), waits=[t0], mark=True)
        x2 = P.emit("scalar", lambda e: e.square(
            qw[0:np_, 0:fd], src_ap), waits=[src_tok, ws.get("cl")],
            mark=True)
        m2in = qm[0:np_, 1, 0:nb].unsqueeze(2).broadcast_to([np_, nb, 16])
        qw3 = qw[0:np_, 0:fd].rearrange("p (b s) -> p b s", s=16)
        cl = None
        for i, thr in enumerate(_QT):
            cl = P.emit("vector", lambda e, i=i, k=thr / 36.0: e.scalar_tensor_tensor(
                qb[0:np_, i, 0:fd].rearrange("p (b s) -> p b s", s=16),
                m2in, k, qw3, ALU.mult, ALU.is_lt),
                waits=[m2, x2] if i == 0 else (), mark=(i == 6))
        cm = P.emit("vector", lambda e: e.tensor_tensor(
            qb[0:np_, 7, 0:fd], qb[0:np_, 0, 0:fd],
            qb[0:np_, 1, 0:fd], op=ALU.add), waits=[cl], mark=True)
        a2 = P.emit("vector", lambda e: e.tensor_tensor(
            qb[0:np_, 8, 0:fd], qb[0:np_, 2, 0:fd], qb[0:np_, 3, 0:fd],
            op=ALU.add), waits=[cl], mark=True)
        a3 = P.emit("vector", lambda e: e.tensor_tensor(
            qb[0:np_, 9, 0:fd], qb[0:np_, 7, 0:fd], qb[0:np_, 8, 0:fd],
            op=ALU.add), waits=[a2], mark=True)
        a4 = P.emit("vector", lambda e: e.scalar_tensor_tensor(
            qb[0:np_, 0, 0:fd], qb[0:np_, 6, 0:fd], 2.0, qb[0:np_, 5, 0:fd],
            ALU.mult, ALU.add), waits=[cm], mark=True)
        a5 = P.emit("vector", lambda e: e.scalar_tensor_tensor(
            qb[0:np_, 1, 0:fd], qb[0:np_, 4, 0:fd], 2.0, qb[0:np_, 9, 0:fd],
            ALU.mult, ALU.add), waits=[a3], mark=True)
        a6 = P.emit("vector", lambda e: e.scalar_tensor_tensor(
            qb[0:np_, 2, 0:fd], qb[0:np_, 0, 0:fd], 2.0, qb[0:np_, 1, 0:fd],
            ALU.mult, ALU.add), waits=[a5], mark=True)
        sgn = P.emit("scalar", lambda e: e.activation(
            qb[0:np_, 10, 0:fd], src_ap, AF.Sign),
            waits=[src_tok, ws.get("u1")], mark=True)
        u1 = P.emit("vector", lambda e: e.tensor_tensor(
            qb[0:np_, 5, 0:fd], qb[0:np_, 2, 0:fd], qb[0:np_, 10, 0:fd],
            op=ALU.mult), waits=[a6, sgn], mark=True)
        vq = P.emit("vector", lambda e, d=dst_ap: e.tensor_tensor(
            d.rearrange("p (b s) -> p b s", s=16),
            qb[0:np_, 5, 0:fd].rearrange("p (b s) -> p b s", s=16),
            qm[0:np_, 3, 0:nb].unsqueeze(2).broadcast_to([np_, nb, 16]),
            op=ALU.mult), waits=[vt2, u1, dst_tok], mark=True)
        ws.update(cl=cl, u1=u1)
        ws["prev"] = vq
        return vq

    lnst = {"l3": None, "sq": [None, None]}

    def emit_ln_head(rsrc, rt, sl):
        l1 = P.emit("vector", lambda e: e.tensor_reduce(
            lns[:, sl, 0:1], rsrc, axis=AX.X, op=ALU.add), waits=[rt],
            mark=True)
        l2 = P.emit("vector", lambda e: e.tensor_single_scalar(
            lns[:, sl, 1:2], lns[:, sl, 0:1], 1.0 / 1024.0, ALU.mult),
            waits=[l1], mark=True)
        sq = P.emit("scalar", lambda e: e.square(lnw[:], rsrc),
                    waits=[rt, lnst["l3"]], mark=True)
        lnst["sq"][sl] = sq
        l3 = P.emit("vector", lambda e: e.tensor_reduce(
            lns[:, sl, 2:3], lnw[:], axis=AX.X, op=ALU.add), waits=[sq],
            mark=True)
        lnst["l3"] = l3
        l4 = P.emit("vector", lambda e: e.tensor_single_scalar(
            lns[:, sl, 3:4], lns[:, sl, 2:3], 1.0 / 1024.0, ALU.mult),
            waits=[l3], mark=True)
        l5 = P.emit("vector", lambda e: e.tensor_tensor(
            lns[:, sl, 4:5], lns[:, sl, 1:2], lns[:, sl, 1:2], op=ALU.mult),
            waits=[l2], mark=True)
        l6 = P.emit("vector", lambda e: e.tensor_tensor(
            lns[:, sl, 5:6], lns[:, sl, 3:4], lns[:, sl, 4:5],
            op=ALU.subtract), waits=[l4, l5], mark=True)
        l7 = P.emit("vector", lambda e: e.tensor_single_scalar(
            lns[:, sl, 6:7], lns[:, sl, 5:6], EPS, ALU.add), waits=[l6],
            mark=True)
        l8 = P.emit("vector", lambda e: e.tensor_single_scalar(
            lns[:, sl, 7:8], lns[:, sl, 1:2], -1.0, ALU.mult), waits=[l2],
            mark=True)
        l9 = P.emit("scalar", lambda e: e.activation(
            lns[:, sl, 8:9], lns[:, sl, 6:7], AF.Ln), waits=[l7], mark=True)
        l10 = P.emit("scalar", lambda e: e.activation(
            lns[:, sl, 9:10], lns[:, sl, 8:9], AF.Exp, scale=-0.5),
            waits=[l9], mark=True)
        return (rsrc, sl, l10, l8)

    def emit_ln_tail(head, dst_ap, dst_tok=None):
        rsrc, sl, l10, l8 = head
        return P.emit("vector", lambda e, d=dst_ap: e.tensor_scalar(
            d, rsrc, lns[:, sl, 7:8], lns[:, sl, 9:10], ALU.add, ALU.mult),
            waits=[l10, l8, dst_tok], mark=True)

    def emit_ln(rsrc, dst_ap, rt, dst_tok=None, sl=0):
        return emit_ln_tail(emit_ln_head(rsrc, rt, sl), dst_ap, dst_tok)

    P.emit("sync", lambda e: e.dma_start(out=mska[:], in_=d_ma[:, :]), dma=True)
    P.emit("sync", lambda e: e.dma_start(out=mskb[:], in_=d_mb[:, :]), dma=True)
    P.emit("sync", lambda e: e.dma_start(out=idn[:], in_=d_id[:, :]), dma=True)
    P.emit("gpsimd", lambda e: e.memset(onesc[:], 1.0), mark=True)
    P.emit("gpsimd", lambda e: e.memset(onesr[:], 1.0), mark=True)

    with nc.sbuf_tensor("xstg", [128, 2, 4096], BF16) as xstg:
        spills = []
        njob = (NT + OWN) // 4
        for t in range(njob):
            stg = xstg[:, t % 2, :]
            if t < NT // 4:
                srcb = d_x[512 * t:512 * (t + 1), :]
                dstb = s_xb[512 * t:512 * (t + 1), :]
            else:
                tt = t - NT // 4
                srcb = d_xo[512 * tt:512 * (tt + 1), :]
                dstb = s_xob[512 * tt:512 * (tt + 1), :]
            w = [spills[t - 2]] if t >= 2 else []
            P.emit("gpsimd", lambda e, stg=stg, srcb=srcb: e.dma_start(
                out=stg, in_=srcb), waits=w, dma=True)
            ld = P.tok("gpsimd")
            P.emit("sync", lambda e, stg=stg, dstb=dstb: e.dma_start(
                out=dstb, in_=stg), waits=[ld], dma=True)
            spills.append(P.tok("sync"))
        spill_all = P.tok("sync")

    with contextlib.ExitStack() as yctx:
        y16 = yctx.enter_context(nc.sbuf_tensor("y16", [128, 8, 2048], BF16))

        with contextlib.ExitStack() as actx:
            aen = actx.enter_context
            xT = aen(nc.sbuf_tensor("xT", [128, 8, T], BF16))
            xoT = aen(nc.sbuf_tensor("xoT", [128, 8, 2048], BF16))
            waq = aen(nc.sbuf_tensor("waq", [128, 8, 128], BF16))
            wak = aen(nc.sbuf_tensor("wak", [128, 8, 128], BF16))
            wav = aen(nc.sbuf_tensor("wav", [128, 8, 256], BF16))
            v16 = aen(nc.sbuf_tensor("v16", [128, 32, 256], BF16))
            k16 = aen(nc.sbuf_tensor("k16", [128, T], BF16))
            q16 = aen(nc.sbuf_tensor("q16", [128, 2048], BF16))
            dect = aen(nc.sbuf_tensor("dect", [128, 18 * 128], BF16))
            pAB = aen(nc.sbuf_tensor("pAB", [128, 3, 2, 512], BF16))
            pA = pAB[:, :, 0, :]
            pB = pAB[:, :, 1, :]
            pmA = aen(nc.sbuf_tensor("pmA", [128, 3, 128], BF16))
            pmB = aen(nc.sbuf_tensor("pmB", [128, 3, 128], BF16))
            pdA = aen(nc.sbuf_tensor("pdA", [128, 3, 512], BF16))
            pdB = aen(nc.sbuf_tensor("pdB", [128, 3, 512], BF16))
            rz = aen(nc.sbuf_tensor("rz", [128, 512], F32, side="right"))
            rz16 = aen(nc.sbuf_tensor("rz16", [128, 512], BF16, side="right"))
            rzb = aen(nc.sbuf_tensor("rzb", [128, 512], BF16, side="right"))
            wf32 = aen(nc.sbuf_tensor("wf32", [128, 2, 512], F32))

            wjobs = []
            for r0, r1 in ((0, 128), (128, 256), (256, 352)):
                for f0 in (0, 512):
                    wjobs.append((d_wg, s_wgi, r0, r1, f0, f0 + 512))
            for r0, r1 in ((0, 128), (128, 256), (256, 352)):
                for f0 in (0, 512):
                    wjobs.append((d_wu, s_wui, r0, r1, f0, f0 + 512))
            for ci in range(6):
                f0, f1 = 512 * ci, min(FF, 512 * ci + 512)
                wjobs.append((d_wd, s_wdi, 0, 128, f0, f1))
            wlds = []
            wsps = [None] * len(wjobs)
            wvqs = [None] * len(wjobs)

            def emit_wload(j):
                dsrc, ddst, r0, r1, f0, f1 = wjobs[j]
                np_, fd = r1 - r0, f1 - f0
                P.emit("scalar", lambda e, dsrc=dsrc, r0=r0, r1=r1, f0=f0, f1=f1, fd=fd, np_=np_, sl=j % 2: e.dma_start(
                    out=wf32[0:np_, sl, 0:fd], in_=dsrc[r0:r1, f0:f1]),
                    waits=[wvqs[j - 2]] if j >= 2 else (), dma=True)
                wlds.append(P.tok("scalar"))

            emit_wload(0)
            emit_wload(1)
            for wj, (dsrc, ddst, r0, r1, f0, f1) in enumerate(wjobs):
                np_, fd = r1 - r0, f1 - f0
                vq = emit_quant_v(wf32[0:np_, wj % 2, 0:fd],
                                  qout[0:np_, wj % 2, 0:fd], fd,
                                  src_tok=wlds[wj],
                                  dst_tok=wsps[wj - 2] if wj >= 2 else None,
                                  np_=np_)
                wvqs[wj] = vq
                P.emit("scalar", lambda e, ddst=ddst, r0=r0, r1=r1, f0=f0, f1=f1, fd=fd, np_=np_, sl=wj % 2: e.dma_start(
                    out=ddst[r0:r1, f0:f1], in_=qout[0:np_, sl, 0:fd]),
                    waits=[vq], dma=True)
                wsps[wj] = P.tok("scalar")
                if wj + 2 < len(wjobs):
                    emit_wload(wj + 2)
            wq_spill = P.tok("scalar")
            ccd = {}

            def emit_cc():
                for (ci, co) in ((s_wgi, s_wgq), (s_wui, s_wuq),
                                 (s_wdi, s_wdq)):
                    P.emit("gpsimd",
                           lambda e, ci=ci, co=co: e.collective_compute(
                               "AllGather", ALU.bypass, ins=[ci[:, :]],
                               outs=[co[:, :]],
                               replica_groups=[[0, 1, 2, 3, 4, 5, 6, 7]]),
                           waits=[wq_spill], mark=True)
                ccd["tok"] = P.tok("gpsimd")

            P.emit("sync", lambda e: e.dma_start(out=dect[:], in_=d_dec[:, :]),
                   dma=True)
            for cc in range(8):
                P.emit("sync", lambda e, cc=cc: e.dma_start(
                    out=xT[:, cc, :], in_=s_xb[:, 128 * cc:128 * (cc + 1)],
                    transpose=True), waits=[spill_all], dma=True)
            for cc in range(8):
                P.emit("sync", lambda e, cc=cc: e.dma_start(
                    out=xoT[:, cc, :], in_=s_xob[:, 128 * cc:128 * (cc + 1)],
                    transpose=True), dma=True)
            xt_tok = P.tok("sync")


            ROT = ((0, 1), (2, 3), (6, 7))
            PROT = (0, 1, 6)
            pend = [None]

            last_pair_scal = None
            last_pair_expf = None

            def emit_wa(qt, pr, prod_tok):
                co = 256 * qt + 128 * pr
                toks = {}
                if pr == 0:
                    for cc in range(8):
                        P.emit("gpsimd", lambda e, cc=cc, qt=qt: e.dma_start(
                            out=wav[:, cc, :],
                            in_=d_wa[128 * cc:128 * (cc + 1),
                                     2048 + 256 * qt:2048 + 256 * (qt + 1)]),
                            waits=[prod_tok] if cc == 0 else (), dma=True)
                    toks["v"] = P.tok("gpsimd")
                for cc in range(8):
                    P.emit("gpsimd", lambda e, cc=cc, co=co: e.dma_start(
                        out=wak[:, cc, :],
                        in_=d_wa[128 * cc:128 * (cc + 1),
                                 1024 + co:1024 + co + 128]),
                        waits=[prod_tok] if (pr == 1 and cc == 0) else (),
                        dma=True)
                toks["k"] = P.tok("gpsimd")
                for cc in range(8):
                    P.emit("gpsimd", lambda e, cc=cc, co=co: e.dma_start(
                        out=waq[:, cc, :],
                        in_=d_wa[128 * cc:128 * (cc + 1), co:co + 128]),
                        dma=True)
                toks["q"] = P.tok("gpsimd")
                return toks

            wa_next = emit_wa(0, 0, spill_all)
            for qt in range(nqt if phlim >= 1 else 0):
                for pr in range(2):
                    gp = 2 * qt + pr
                    wa_cur = wa_next
                    if qt == nqt - 1 and pr == 1:
                        emit_cc()

                    evac_toks = [None, None, None]
                    jobs = []
                    if pr == 0:
                        for t in range(NT):
                            jobs.append(("v", t))
                    jobs += [("k", ch) for ch in range(8)]
                    jobs += [("q", ch) for ch in range(4)]
                    seen_kind = set()
                    for ji, (kind, idx) in enumerate(jobs):
                        bk = PROT[ji % 3]
                        bw = [evac_toks[ji % 3]]
                        if kind not in seen_kind:
                            seen_kind.add(kind)
                            bw += [wa_cur[kind]]
                        if ji == 0:
                            bw += [xt_tok, last_pair_expf]
                        if ji == 2 and pend[0] is not None:
                            sc, _vec = pend[0]()
                            pend[0] = None
                            bw += [sc]
                        for cc in range(8):
                            if kind == "v":
                                P.emit("tensor", lambda e, t=idx, cc=cc, bk=bk: e.matmul(
                                    ps[:, bk, 0:256],
                                    xT[:, cc, 128 * t:128 * (t + 1)],
                                    wav[:, cc, :], start=(cc == 0),
                                    stop=(cc == 7)),
                                    waits=bw if cc == 0 else (),
                                    mark=(cc == 7))
                            elif kind == "k":
                                P.emit("tensor", lambda e, ch=idx, cc=cc, bk=bk: e.matmul(
                                    ps[:, bk, :], wak[:, cc, :],
                                    xT[:, cc, 512 * ch:512 * (ch + 1)],
                                    start=(cc == 0), stop=(cc == 7)),
                                    waits=bw if cc == 0 else (),
                                    mark=(cc == 7))
                            else:
                                P.emit("tensor", lambda e, ch=idx, cc=cc, bk=bk: e.matmul(
                                    ps[:, bk, :], waq[:, cc, :],
                                    xoT[:, cc, 512 * ch:512 * (ch + 1)],
                                    start=(cc == 0), stop=(cc == 7)),
                                    waits=bw if cc == 0 else (),
                                    mark=(cc == 7))
                        mm = P.tok("tensor")
                        if kind == "v":
                            tk = P.emit("vector", lambda e, t=idx, bk=bk: e.tensor_copy(
                                v16[:, t, :], ps[:, bk, 0:256]), waits=[mm],
                                mark=True)
                        elif kind == "k":
                            tk = P.emit("vector", lambda e, ch=idx, bk=bk: e.tensor_copy(
                                k16[:, 512 * ch:512 * (ch + 1)], ps[:, bk, :]),
                                waits=[mm], mark=True)
                        else:
                            tk = P.emit("vector", lambda e, ch=idx, bk=bk: e.tensor_copy(
                                q16[:, 512 * ch:512 * (ch + 1)], ps[:, bk, :]),
                                waits=[mm], mark=True)
                        evac_toks[ji % 3] = tk
                    kq_tok = P.tok("vector")
                    if not (qt == nqt - 1 and pr == 1):
                        nq2, np2 = (qt, 1) if pr == 0 else (qt + 1, 0)
                        wa_next = emit_wa(nq2, np2, P.tok("tensor"))
                    prev_qc_scal = None
                    prev_qc_vec = None
                    prev_qc_expf = None

                    for qc in range(nqc if attlim >= 2 else 0):
                        nkt = 8 * qc + 8
                        z_started = [False, False]
                        pv_started = [False, False]
                        exp_toks = {}
                        vmarks = {}
                        tmarks = {}
                        kt_geom = {}

                        def geom(kt, qc=qc):
                            jlo = min(3, max(0, -(-(kt - 1 - 8 * qc) // 2)))
                            d0 = kt - 8 * qc
                            jm = None
                            msk = None
                            if 0 <= d0 <= 7:
                                jm = d0 // 2 if d0 % 2 == 0 else (d0 - 1) // 2
                                msk = mska if d0 % 2 == 0 else mskb
                            jwlo = jlo
                            jwhi_t = min(3, (kt + 16 - 8 * qc) // 2)
                            jwhi = 3 if kt == 0 else jwhi_t
                            if attlim < 3:
                                jm, msk, jwhi_t, jwhi = None, None, jwlo - 1, jwlo - 1
                            return jlo, jm, msk, jwlo, jwhi, jwhi_t

                        def emit_st(kt, qc=qc):
                            jlo = geom(kt)[0]
                            w0 = 128 * jlo
                            ba, bb = ROT[kt % 3]
                            w = [exp_toks.get(kt - 3), kq_tok]
                            if kt < 2:
                                w += [prev_qc_expf]
                            elif kt == 2:
                                w += [prev_qc_expf, prev_qc_scal]
                            for h, bx in ((0, ba), (1, bb)):
                                P.emit("tensor",
                                       lambda e, h=h, kt=kt, w0=w0, bx=bx, qc=qc: e.matmul(
                                           ps[:, bx, w0:],
                                           k16[64 * h:64 * (h + 1),
                                               128 * kt:128 * (kt + 1)],
                                           q16[64 * h:64 * (h + 1),
                                               512 * qc + w0:512 * (qc + 1)],
                                           start=True, stop=True,
                                           tile_position=(64 * h, 0)),
                                       waits=w if h == 0 else (),
                                       mark=(h == 1))
                            return P.tok("tensor")

                        def emit_exp(kt, st_tok):
                            jlo = geom(kt)[0]
                            w0 = 128 * jlo
                            ba, bb = ROT[kt % 3]
                            sl = kt % 3
                            w = [st_tok, vmarks.get(kt - 3), tmarks.get(kt - 3)]
                            tok = P.emit("scalar",
                                         lambda e, w0=w0, ba=ba, sl=sl: e.activation(
                                             pAB[:, sl, :, w0:],
                                             ps[:, ba:ba + 2, w0:],
                                             AF.Exp, scale=0.125),
                                         waits=w, mark=True)
                            exp_toks[kt] = tok
                            return tok

                        def emit_vops(kt):
                            jlo, jm, msk, jwlo, jwhi, jwhi_t = geom(kt)
                            sl = kt % 3
                            v_last = None
                            first = [exp_toks[kt]]
                            if jm is not None:
                                P.emit("vector",
                                       lambda e, sl=sl, jm=jm, m=msk: e.tensor_tensor(
                                           pmA[:, sl, :],
                                           pA[:, sl, 128 * jm:128 * (jm + 1)],
                                           m[:], op=ALU.mult), waits=first)
                                first = []
                                v_last = P.emit("vector",
                                                lambda e, sl=sl, jm=jm, m=msk: e.tensor_tensor(
                                                    pmB[:, sl, :],
                                                    pB[:, sl, 128 * jm:128 * (jm + 1)],
                                                    m[:], op=ALU.mult), mark=True)
                            if kt == 0 and jwhi_t < 3:
                                P.emit("vector", lambda e, sl=sl, j1=jwhi_t + 1: e.memset(
                                    pdA[:, sl, 128 * j1:], 0.0), waits=first)
                                first = []
                                v_last = P.emit("vector",
                                                lambda e, sl=sl, j1=jwhi_t + 1: e.memset(
                                                    pdB[:, sl, 128 * j1:], 0.0),
                                                mark=True)
                            if jwhi_t >= jwlo:
                                i0 = 2 * (4 * qc + jwlo) - kt + 1
                                nj = jwhi_t - jwlo + 1
                                dsl = dect[:].rearrange(
                                    "p (i q) -> p i q", q=128)[
                                    :, i0:i0 + 2 * nj - 1:2, :]
                                P.emit("vector",
                                       lambda e, sl=sl, jwlo=jwlo, nj=nj, dsl=dsl: e.tensor_tensor(
                                           pdA[:, sl, 128 * jwlo:128 * (jwlo + nj)]
                                           .rearrange("p (j q) -> p j q", q=128),
                                           pA[:, sl, 128 * jwlo:128 * (jwlo + nj)]
                                           .rearrange("p (j q) -> p j q", q=128),
                                           dsl, op=ALU.mult), waits=first)
                                first = []
                                v_last = P.emit("vector",
                                                lambda e, sl=sl, jwlo=jwlo, nj=nj, dsl=dsl: e.tensor_tensor(
                                                    pdB[:, sl, 128 * jwlo:128 * (jwlo + nj)]
                                                    .rearrange("p (j q) -> p j q", q=128),
                                                    pB[:, sl, 128 * jwlo:128 * (jwlo + nj)]
                                                    .rearrange("p (j q) -> p j q", q=128),
                                                    dsl, op=ALU.mult), mark=True)
                            vmarks[kt] = v_last
                            return v_last

                        def emit_zpv(kt, qc=qc):
                            jlo, jm, msk, jwlo, jwhi, jwhi_t = geom(kt)
                            sl = kt % 3
                            zlo = (jm + 1) if jm is not None else jlo
                            if attlim < 3:
                                zlo = 4
                            last_kt = (kt == nkt - 1)
                            tops = []
                            for h, praw, pmsk in ((0, pA, pmA), (1, pB, pmB)):
                                zops = []
                                if zlo <= 3:
                                    zops.append(("raw", praw, zlo))
                                if jm is not None:
                                    zops.append(("msk", pmsk, jm))
                                for oi, (knd, buf, j0) in enumerate(zops):
                                    st = not z_started[h]
                                    z_started[h] = True
                                    sp = last_kt and (oi == len(zops) - 1)
                                    if knd == "raw":
                                        tops.append(lambda e, h=h, sl=sl, j0=j0, buf=buf, st=st, sp=sp: e.matmul(
                                            ps[64 * h:64 * h + 1, 5, 128 * j0:],
                                            onesc[:], buf[:, sl, 128 * j0:],
                                            start=st, stop=sp,
                                            skip_group_check=True,
                                            tile_position=(0, 64 * h)))
                                    else:
                                        tops.append(lambda e, h=h, sl=sl, j0=j0, buf=buf, st=st, sp=sp: e.matmul(
                                            ps[64 * h:64 * h + 1, 5,
                                               128 * j0:128 * (j0 + 1)],
                                            onesc[:], buf[:, sl, :],
                                            start=st, stop=sp,
                                            skip_group_check=True,
                                            tile_position=(0, 64 * h)))
                            if attlim >= 4 and jwhi >= jwlo:
                                for h, pdec in ((0, pdA), (1, pdB)):
                                    st = not pv_started[h]
                                    pv_started[h] = True
                                    tops.append(lambda e, h=h, sl=sl, kt=kt, pr=pr, jwlo=jwlo, jwhi=jwhi, pdec=pdec, st=st, sp=last_kt: e.matmul(
                                        ps[64 * h:64 * (h + 1), 4,
                                           128 * jwlo:128 * (jwhi + 1)],
                                        v16[:, kt, 128 * pr + 64 * h:
                                            128 * pr + 64 * (h + 1)],
                                        pdec[:, sl, 128 * jwlo:128 * (jwhi + 1)],
                                        start=st, stop=sp,
                                        skip_group_check=True,
                                        tile_position=(0, 64 * h)))
                            w0l = [vmarks.get(kt) or exp_toks[kt]]
                            if kt == 0:
                                w0l += [prev_qc_scal, prev_qc_vec]
                            tk = None
                            for oi, tf in enumerate(tops):
                                tk = P.emit("tensor", tf,
                                            waits=w0l if oi == 0 else (),
                                            mark=(oi == len(tops) - 1))
                            tmarks[kt] = tk
                            return tk

                        for kt in range(nkt):
                            if kt == 2 and pend[0] is not None:
                                prev_qc_scal, prev_qc_vec = pend[0]()
                                pend[0] = None
                            st_tok = emit_st(kt)
                            emit_exp(kt, st_tok)
                            emit_vops(kt)
                            if kt >= 2:
                                emit_zpv(kt - 2)
                        emit_zpv(nkt - 2)
                        emit_zpv(nkt - 1)
                        zpv_tok = P.tok("tensor")
                        prev_qc_expf = exp_toks[nkt - 1]
                        if attlim < 5:
                            continue
                        P.emit("scalar", lambda e: e.activation(
                            rz[0:1, :], ps[0:1, 5, :], AF.Ln),
                            waits=[zpv_tok])
                        lt = P.emit("scalar", lambda e: e.activation(
                            rz[64:65, :], ps[64:65, 5, :], AF.Ln), mark=True)
                        P.emit("scalar", lambda e: e.activation(
                            rz16[0:1, :], rz[0:1, :], AF.Exp, scale=-1.0),
                            waits=[lt])
                        se = P.emit("scalar", lambda e: e.activation(
                            rz16[64:65, :], rz[64:65, :], AF.Exp, scale=-1.0),
                            mark=True)

                        def mk_norm(se=se, gp=gp, qc=qc,
                                    expf=exp_toks[nkt - 1]):
                            def go():
                                for h in range(2):
                                    P.emit("tensor", lambda e, h=h: e.matmul(
                                        psx[64 * h:64 * (h + 1), :],
                                        onesr[64 * h:64 * h + 1, :],
                                        rz16[64 * h:64 * h + 1, :],
                                        start=True, stop=True,
                                        skip_group_check=True,
                                        tile_position=(64 * h, 64 * h)),
                                        waits=([se, expf] if h == 0 else ()),
                                        mark=(h == 1))
                                bc = P.tok("tensor")
                                sbc = P.emit("scalar", lambda e: e.copy(
                                    rzb[:], psx[:]), waits=[bc], mark=True)
                                vec = P.emit(
                                    "vector",
                                    lambda e, gp=gp, qc=qc: e.tensor_tensor(
                                        y16[:, gp, 512 * qc:512 * (qc + 1)],
                                        ps[:, 4, :], rzb[:], op=ALU.mult),
                                    waits=[sbc], mark=True)
                                return sbc, vec
                            return go
                        pend[0] = mk_norm()
                        last_pair_expf = prev_qc_expf

            if pend[0] is not None:
                pend[0]()
                pend[0] = None
            if "tok" not in ccd:
                emit_cc()
            cc_tok = ccd["tok"]
            att_done = {e: P.tok(e) for e in ENGS}
        xjit = en(nc.sbuf_tensor("xjit", [128, 2, 1024], F32, side="right"))
        r1 = en(nc.sbuf_tensor("r1", [128, 2, 1024], F32, side="right"))
        x1t = en(nc.sbuf_tensor("x1t", [128, 1024], F32, side="right"))
        outt = en(nc.sbuf_tensor("outt", [128, 1024], F32, side="right"))
        lnw = en(nc.sbuf_tensor("lnw", [128, 1024], F32, side="right"))
        qout2 = en(nc.sbuf_tensor("qout2", [128, 2, 512], BF16, side="right"))

        wp16 = yctx.enter_context(nc.sbuf_tensor("wp16", [128, 8, 1024], BF16))
        for cc in range(8):
            P.emit("gpsimd", lambda e, cc=cc: e.dma_start(
                out=wp16[:, cc, :], in_=d_wp[128 * cc:128 * (cc + 1), :]),
                waits=([att_done["tensor"], att_done["vector"],
                        att_done["scalar"]] if cc == 0 else ()), dma=True)
        wp_tok = P.tok("gpsimd")
        r1_toks = [None] * OWN
        x1_spills = []
        xq_spills = []
        xq_sp_prev = [None, None]
        out_toks = [None] * OWN
        pend_ln = [None]

        def fin_p2(head, tp):
            x1tok = emit_ln_tail(
                head, x1t[:],
                dst_tok=x1_spills[tp - 1] if tp >= 1 else None)
            P.emit("gpsimd", lambda e, tp=tp: e.dma_start(
                out=s_x1[128 * tp:128 * (tp + 1), :], in_=x1t[:]),
                waits=[x1tok], dma=True)
            x1_spills.append(P.tok("gpsimd"))
            for f0 in (0, 512):
                vq = emit_quant_v(x1t[:, f0:f0 + 512], qout2[:, f0 // 512, :],
                                  512, src_tok=x1tok,
                                  dst_tok=xq_sp_prev[f0 // 512])
                P.emit("gpsimd", lambda e, tp=tp, f0=f0: e.dma_start(
                    out=s_xq[128 * tp:128 * (tp + 1), f0:f0 + 512],
                    in_=qout2[:, f0 // 512, :]), waits=[vq], dma=True)
                xq_sp_prev[f0 // 512] = P.tok("gpsimd")
            xq_spills.append(P.tok("gpsimd"))
        for t in range(OWN if phlim >= 2 else 0):
            P.emit("sync", lambda e, t=t: e.dma_start(
                out=xjit[:, t % 2, :], in_=d_xo[128 * t:128 * (t + 1), :]),
                waits=[r1_toks[t - 2] if t >= 2 else None,
                       att_done["vector"] if t < 2 else None], dma=True)
            xj = P.tok("sync")
            for half in range(2):
                for cc in range(8):
                    P.emit("tensor", lambda e, t=t, half=half, cc=cc: e.matmul(
                        ps[:, half, :], y16[:, cc, 128 * t:128 * (t + 1)],
                        wp16[:, cc, 512 * half:512 * (half + 1)],
                        start=(cc == 0), stop=(cc == 7)),
                        waits=([att_done["scalar"], att_done["vector"],
                                att_done["gpsimd"], wp_tok,
                                r1_toks[t - 1] if t >= 1 else None]
                               if (half == 0 and cc == 0) else ()),
                        mark=(half == 1 and cc == 7))
            pj = P.tok("tensor")
            rt = P.emit("vector", lambda e, t=t: e.tensor_tensor(
                r1[:, t % 2, :], ps[:, 0:2, :].rearrange("p a b -> p (a b)"),
                xjit[:, t % 2, :], op=ALU.add),
                waits=[pj, xj, lnst["sq"][t % 2]], mark=True)
            r1_toks[t] = rt
            head = emit_ln_head(r1[:, t % 2, :], rt, t % 2)
            if pend_ln[0] is not None:
                fin_p2(*pend_ln[0])
            pend_ln[0] = (head, t)
        if pend_ln[0] is not None:
            fin_p2(*pend_ln[0])
            pend_ln[0] = None
        if DEBUG:
            for gpd in range(8):
                P.emit("sync", lambda e, gpd=gpd: e.dma_start(
                    out=s_y[128 * gpd:128 * (gpd + 1), :],
                    in_=y16[:, gpd, :]), waits=[att_done["vector"]], dma=True)
        p2_done = {e: P.tok(e) for e in ENGS}

    with contextlib.ExitStack() as fctx:
        fen = fctx.enter_context
        wgT = fen(nc.sbuf_tensor("wgT", [128, 8, FF], BF16))
        wuT = fen(nc.sbuf_tensor("wuT", [128, 8, FF], BF16))
        wdT = fen(nc.sbuf_tensor("wdT", [128, 22, 1024], BF16))
        xqTt = fen(nc.sbuf_tensor("xqTt", [128, 2, 8, 128], BF16))
        hq = fen(nc.sbuf_tensor("hq", [128, FF], BF16))
        hqT = fen(nc.sbuf_tensor("hqT", [128, 22, 128], BF16))
        hbuf = fen(nc.sbuf_tensor("hbuf", [128, 2, 512], F32))
        sgl = fen(nc.sbuf_tensor("sgl", [128, 2, 512], F32))


        tlw = [p2_done["tensor"], p2_done["vector"], p2_done["scalar"],
               p2_done["gpsimd"], cc_tok]
        for cc in range(8):
            P.emit("sync", lambda e, cc=cc: e.dma_start(
                out=wgT[:, cc, :], in_=s_wgq[:, 128 * cc:128 * (cc + 1)],
                transpose=True), waits=tlw if cc == 0 else (), dma=True)
        for cc in range(8):
            P.emit("sync", lambda e, cc=cc: e.dma_start(
                out=wuT[:, cc, :], in_=s_wuq[:, 128 * cc:128 * (cc + 1)],
                transpose=True), dma=True)
        for ic in range(22):
            P.emit("sync", lambda e, ic=ic: e.dma_start(
                out=wdT[:, ic, :], in_=s_wdq[:, 128 * ic:128 * (ic + 1)],
                transpose=True), dma=True)
        wT_tok = P.tok("sync")

        ICH = [(0, 512), (512, 1024), (1024, 1536), (1536, 2048),
               (2048, 2560), (2560, FF)]
        frt = [None] * OWN

        def fin_ffn(head, tp):
            ot = emit_ln_tail(head, outt[:],
                              dst_tok=out_toks[tp - 1] if tp >= 1 else None)
            P.emit("sync", lambda e, tp=tp: e.dma_start(
                out=d_out[128 * tp:128 * (tp + 1), :], in_=outt[:]),
                waits=[ot], dma=True)
            out_toks[tp] = P.tok("sync")
        xqT_toks = [None] * OWN
        dn_toks = [None] * OWN
        last_silu = [None, None]
        last_h = [None, None]
        prev_cp = [None]
        prev_trts = [None, None, None]
        NTF = OWN if phlim >= 3 else 0

        def emit_xqT(t):
            for cc in range(8):
                P.emit("sync", lambda e, t=t, cc=cc: e.dma_start(
                    out=xqTt[:, t % 2, cc, :],
                    in_=s_xq[128 * t:128 * (t + 1), 128 * cc:128 * (cc + 1)],
                    transpose=True),
                    waits=([xq_spills[t], dn_toks[t - 2] if t >= 2 else None]
                           if cc == 0 else ()), dma=True)
            xqT_toks[t] = P.tok("sync")

        def emit_chunk(st, ici):
            t = st["t"]
            f0, f1 = ICH[ici]
            fw = f1 - f0
            bg = ici % 2
            for cc in range(8):
                P.emit("tensor", lambda e, t=t, cc=cc, f0=f0, f1=f1, bg=bg: e.matmul(
                    ps[:, 2 * bg, 0:f1 - f0], xqTt[:, t % 2, cc, :],
                    wgT[:, cc, f0:f1], start=(cc == 0), stop=(cc == 7)),
                    waits=([xqT_toks[t], wT_tok, last_silu[bg],
                            last_h[bg]] if cc == 0 else ()))
            for cc in range(8):
                P.emit("tensor", lambda e, t=t, cc=cc, f0=f0, f1=f1, bg=bg: e.matmul(
                    ps[:, 2 * bg + 1, 0:f1 - f0], xqTt[:, t % 2, cc, :],
                    wuT[:, cc, f0:f1], start=(cc == 0), stop=(cc == 7)),
                    mark=(cc == 7))
            gu = P.tok("tensor")
            sg_t = P.emit("scalar", lambda e, bg=bg, fw=fw, sv=ici % 2: e.activation(
                sgl[:, sv, 0:fw], ps[:, 2 * bg, 0:fw], AF.Silu),
                waits=[gu, last_h[bg]], mark=True)
            last_silu[bg] = sg_t
            htok = P.emit("vector", lambda e, bg=bg, fw=fw, sv=ici % 2: e.tensor_tensor(
                hbuf[:, sv, 0:fw], sgl[:, sv, 0:fw],
                ps[:, 2 * bg + 1, 0:fw],
                op=ALU.mult), waits=[sg_t], mark=True)
            last_h[bg] = htok
            emit_quant_v(hbuf[:, ici % 2, 0:fw], hq[:, f0:f1], fw,
                         src_tok=htok, dst_tok=prev_trts[ici // 2])
            st["vq"].append(P.tok("vector"))

        def emit_tgrp(st, grp):
            t = st["t"]
            vq_gate = st["vq"][2 * grp + 1]
            ics = list(range(8 * grp, min(22, 8 * grp + 8)))
            for ii, ic in enumerate(ics):
                P.emit("tensor", lambda e, ic=ic, ii=ii: e.transpose(
                    pst[:, 128 * ii:128 * (ii + 1)],
                    hq[:, 128 * ic:128 * (ic + 1)], idn[:]),
                    waits=([prev_cp[0], vq_gate] if ii == 0 else ()),
                    mark=(ii == len(ics) - 1))
                trt = P.tok("tensor")
            prev_trts[grp] = trt
            prev_cp[0] = P.emit(
                "scalar", lambda e, grp=grp, n=len(ics): e.copy(
                    hqT[:, 8 * grp:8 * grp + n, :],
                    pst[:, 0:128 * n].rearrange("p (i q) -> p i q", q=128)),
                waits=[trt], mark=True)
            for half in range(2):
                for ic in ics:
                    P.emit("tensor", lambda e, ic=ic, half=half: e.matmul(
                        ps[:, 4 + half, :], hqT[:, ic, :],
                        wdT[:, ic, 512 * half:512 * (half + 1)],
                        start=(ic == 0), stop=(ic == 21)),
                        waits=([prev_cp[0],
                                wT_tok if (t == 0 and grp == 0)
                                else None] if ic == ics[0] else ()),
                        mark=(grp == 2 and ic == 21))
                if grp == 2:
                    st["dnh"][half] = P.tok("tensor")

        states = {}
        if NTF:
            emit_xqT(0)
            states[0] = {"t": 0, "vq": [], "dnh": [None, None]}
            emit_chunk(states[0], 0)
        for t in range(NTF):
            st = states.pop(t)
            for ici in range(1, 6):
                emit_chunk(st, ici)
                if ici == 1 and pend_ln[0] is not None:
                    fin_ffn(*pend_ln[0])
                    pend_ln[0] = None
                if ici == 3:
                    emit_tgrp(st, 0)
                elif ici == 5:
                    emit_tgrp(st, 1)
            if t + 1 < NTF:
                emit_xqT(t + 1)
                states[t + 1] = {"t": t + 1, "vq": [], "dnh": [None, None]}
                emit_chunk(states[t + 1], 0)
            emit_tgrp(st, 2)
            dn = P.tok("tensor")
            dn_toks[t] = dn
            P.emit("sync", lambda e, t=t: e.dma_start(
                out=xjit[:, t % 2, :], in_=s_x1[128 * t:128 * (t + 1), :]),
                waits=[x1_spills[t], p2_done["vector"] if t < 2 else None,
                       frt[t - 2] if t >= 2 else None], dma=True)
            xj = P.tok("sync")
            P.emit("vector", lambda e, t=t: e.tensor_tensor(
                r1[:, t % 2, 0:512], ps[:, 4, :],
                xjit[:, t % 2, 0:512], op=ALU.add),
                waits=[st["dnh"][0], xj, lnst["sq"][t % 2]], mark=True)
            rt = P.emit("vector", lambda e, t=t: e.tensor_tensor(
                r1[:, t % 2, 512:1024], ps[:, 5, :],
                xjit[:, t % 2, 512:1024], op=ALU.add),
                waits=[st["dnh"][1]], mark=True)
            frt[t] = rt
            head = emit_ln_head(r1[:, t % 2, :], rt, t % 2)
            pend_ln[0] = (head, t)
        if pend_ln[0] is not None:
            fin_ffn(*pend_ln[0])
            pend_ln[0] = None
        if phlim < 3:
            dz = P.emit("vector", lambda e: e.memset(outt[:], 0.0),
                        waits=[P.tok("tensor"), P.tok("scalar"),
                               P.tok("gpsimd"), P.tok("sync")], mark=True)
            P.emit("sync", lambda e: e.dma_start(
                out=d_out[0:128, :], in_=outt[:]), waits=[dz], dma=True)
        fin = P.tok("sync")
        for e in ENGS:
            P.emit(e, lambda en_: en_.nop(), waits=[fin])

    P.finalize()
    blk = en(nc.Block())

    @blk.tensor
    def _(e):
        P.replay("tensor", e)

    @blk.vector
    def _(e):
        P.replay("vector", e)

    @blk.scalar
    def _(e):
        P.replay("scalar", e)

    @blk.gpsimd
    def _(e):
        P.replay("gpsimd", e)

    @blk.sync
    def _(e):
        P.replay("sync", e)

    ctx.close()
    return nc


_NC_CACHE = {}


def _get_nc():
    if "nc" not in _NC_CACHE:
        _NC_CACHE["nc"] = build_program()
    return _NC_CACHE["nc"]


def kernel(L0_attn_x, L0_attn_W_attn, L0_attn_W_proj, L0_ffn_w_gate,
           L0_ffn_w_up, L0_ffn_w_down):
    x = np.ascontiguousarray(np.asarray(L0_attn_x, dtype=np.float32))
    wa = np.ascontiguousarray(np.asarray(L0_attn_W_attn, dtype=np.float32))
    wp = np.ascontiguousarray(np.asarray(L0_attn_W_proj, dtype=np.float32))
    wg = np.ascontiguousarray(np.asarray(L0_ffn_w_gate, dtype=np.float32))
    wu = np.ascontiguousarray(np.asarray(L0_ffn_w_up, dtype=np.float32))
    wd = np.ascontiguousarray(np.asarray(L0_ffn_w_down, dtype=np.float32))
    dec = [_decay_table(p) for p in range(2)]
    idm = np.ascontiguousarray(np.eye(128, dtype=ml_dtypes.bfloat16))
    msk = [_masks(p) for p in range(2)]
    in_maps = []
    for c in range(8):
        b, p = c // 2, c % 2
        xb = np.ascontiguousarray(x[b])
        own = np.ascontiguousarray(
            xb.reshape(NT, 128, C)[p::2].reshape(OWN * 128, C))
        in_maps.append({
            "x_full": xb, "x_own": own, "w_attn": wa, "w_proj": wp,
            "w_gate_sh": np.ascontiguousarray(wg[352 * c:352 * (c + 1)]),
            "w_up_sh": np.ascontiguousarray(wu[352 * c:352 * (c + 1)]),
            "w_down_sh": np.ascontiguousarray(wd[128 * c:128 * (c + 1)]),
            "decay_tab": dec[p], "mask_a": msk[p][0], "mask_b": msk[p][1],
            "ident": idm,
        })
    nc = _get_nc()
    res = run_bass_kernel_spmd(nc, in_maps, core_ids=list(range(8)))
    out = np.empty((B, T, C), dtype=np.float32)
    for c in range(8):
        b, p = c // 2, c % 2
        o = np.asarray(res.results[c]["out"]).reshape(OWN, 128, C)
        out.reshape(B, NT, 128, C)[b, p::2] = o
    return out


# version 27
# speedup vs baseline: 1.0430x; 1.0430x over previous
import contextlib
import numpy as np
import ml_dtypes

import concourse.bass as bass
import concourse.mybir as mybir
from concourse.bass_utils import run_bass_kernel_spmd

F32 = mybir.dt.float32
BF16 = mybir.dt.bfloat16
AF = mybir.ActivationFunctionType
ALU = mybir.AluOpType
AX = mybir.AxisListType

B, T, C = 4, 4096, 1024
NH, HD, FF = 16, 64, 2816
NT = T // 128
OWN = 16
LN6 = float(np.log(np.float64(6.0)))
EPS = 1e-5
_QT = [0.0625, 0.5625, 1.5625, 3.0625, 6.25, 12.25, 25.0]

DEBUG = False
PHLIM = 3
ATTLIM = 5
NQT = 4
NQC = 4
EXPON = True


def _decay_values():
    WM_DECAY_LENGTH, WM_ECHOIC = 2048, 16
    dl = WM_DECAY_LENGTH - WM_ECHOIC + 1
    nums = np.linspace(0.0, 1.0, dl, dtype=np.float64)
    dv = 1.0 - np.power(nums, 1.0 / np.e)
    dv = np.concatenate([np.ones(WM_ECHOIC - 1), dv]).astype(np.float32)
    out = np.zeros(T + 256, dtype=np.float32)
    out[:2048] = dv
    return out


def _decay_table(p):
    dv = _decay_values()
    r = np.arange(128)[:, None]
    c = np.arange(128)[None, :]
    tab = np.zeros((18, 128, 128), dtype=np.float32)
    for idx in range(18):
        d = idx - 1 + p
        dist = d * 128 + (c - r)
        tab[idx] = dv[np.clip(dist, 0, T - 1)] * (dist >= 0)
    tab = tab.transpose(1, 0, 2).reshape(128, 18 * 128)
    return np.ascontiguousarray(tab.astype(ml_dtypes.bfloat16))


def _masks(p):
    r = np.arange(128)[:, None]
    c = np.arange(128)[None, :]
    tri = (c >= r).astype(np.float32)
    if p == 0:
        ma, mb = tri, np.zeros((128, 128), np.float32)
    else:
        ma, mb = np.ones((128, 128), np.float32), tri
    return (np.ascontiguousarray(ma.astype(ml_dtypes.bfloat16)),
            np.ascontiguousarray(mb.astype(ml_dtypes.bfloat16)))


ENGS = ("tensor", "vector", "scalar", "gpsimd", "sync")


class Prog:

    def __init__(self, nc, ctx):
        self.ops = []
        self.ctr = {}
        self.sem = {}
        for e in ENGS:
            for k in (e, e + "_d"):
                self.sem[k] = ctx.enter_context(nc.semaphore("c_" + k))
                self.ctr[k] = 0

    @staticmethod
    def _flat(waits):
        out = []
        for w in waits:
            if w is None:
                continue
            if isinstance(w, list):
                out.extend(x for x in w if x is not None and x[1] > 0)
            elif w[1] > 0:
                out.append(w)
        return out

    def emit(self, eng, fn, waits=(), mark=False, dma=False):
        key = eng + "_d" if dma else eng
        inc = 16 if dma else (1 if mark else 0)
        if inc:
            self.ctr[key] += inc
        self.ops.append([eng, fn, self._flat(waits), inc, key])
        return (key, self.ctr[key])

    def tok(self, eng):
        return [(eng, self.ctr[eng]), (eng + "_d", self.ctr[eng + "_d"])]

    def finalize(self):
        waited = {}
        for op in self.ops:
            for (we, wv) in op[2]:
                if we.endswith("_d"):
                    waited.setdefault(we, set()).add(wv)
        run = {k: 0 for k in self.ctr}
        for op in self.ops:
            eng, fn, waits, inc, key = op
            if inc and key.endswith("_d"):
                pre = run[key]
                fences = [w for w in waited.get(key, ()) if pre >= w > pre - 16
                          or (pre >= w and any(True for _ in ()))]
                run[key] = pre + inc
        last_fenced = {k: 0 for k in self.ctr}
        run = {k: 0 for k in self.ctr}
        for op in self.ops:
            eng, fn, waits, inc, key = op
            if inc and key.endswith("_d"):
                pre = run[key]
                ws = sorted(w for w in waited.get(key, ())
                            if last_fenced[key] < w <= pre)
                if ws:
                    op[2] = [(key, ws[-1])] + op[2]
                    last_fenced[key] = ws[-1]
                run[key] = pre + inc

    def replay(self, name, e):
        for eng, fn, waits, inc, key in self.ops:
            if eng != name:
                continue
            for (we, wv) in waits:
                e.wait_ge(self.sem[we], wv)
            inst = fn(e)
            if inc:
                inst.then_inc(self.sem[key], inc)


def build_program(phlim=None):
    if phlim is None:
        phlim = PHLIM
    attlim = ATTLIM
    nqt, nqc, expon = NQT, NQC, EXPON
    nc = bass.Bass("TRN2", target_bir_lowering=False)
    okind = "ExternalOutput" if DEBUG else "Internal"

    d_x = nc.dram_tensor("x_full", [T, C], F32, kind="ExternalInput").ap()
    d_xo = nc.dram_tensor("x_own", [OWN * 128, C], F32, kind="ExternalInput").ap()
    d_wa = nc.dram_tensor("w_attn", [C, 3 * C], F32, kind="ExternalInput").ap()
    d_wp = nc.dram_tensor("w_proj", [C, C], F32, kind="ExternalInput").ap()
    d_wg = nc.dram_tensor("w_gate_sh", [FF // 8, C], F32,
                          kind="ExternalInput").ap()
    d_wu = nc.dram_tensor("w_up_sh", [FF // 8, C], F32,
                          kind="ExternalInput").ap()
    d_wd = nc.dram_tensor("w_down_sh", [C // 8, FF], F32,
                          kind="ExternalInput").ap()
    d_dec = nc.dram_tensor("decay_tab", [128, 18 * 128], BF16,
                           kind="ExternalInput").ap()
    d_ma = nc.dram_tensor("mask_a", [128, 128], BF16, kind="ExternalInput").ap()
    d_mb = nc.dram_tensor("mask_b", [128, 128], BF16, kind="ExternalInput").ap()
    d_id = nc.dram_tensor("ident", [128, 128], BF16, kind="ExternalInput").ap()
    d_out = nc.dram_tensor("out", [OWN * 128, C], F32, kind="ExternalOutput").ap()
    s_xb = nc.dram_tensor("s_xb", [T, C], BF16, kind="Internal").ap()
    s_xob = nc.dram_tensor("s_xob", [OWN * 128, C], BF16, kind="Internal").ap()
    s_wgq = nc.dram_tensor("s_wgq", [FF, C], BF16, kind="Internal").ap()
    s_wgi = nc.dram_tensor("s_wgi", [FF // 8, C], BF16, kind="Internal").ap()
    s_wui = nc.dram_tensor("s_wui", [FF // 8, C], BF16, kind="Internal").ap()
    s_wdi = nc.dram_tensor("s_wdi", [C // 8, FF], BF16, kind="Internal").ap()
    s_wuq = nc.dram_tensor("s_wuq", [FF, C], BF16, kind="Internal").ap()
    s_wdq = nc.dram_tensor("s_wdq", [C, FF], BF16, kind="Internal").ap()
    s_xq = nc.dram_tensor("s_xq", [OWN * 128, C], BF16, kind=okind).ap()
    s_x1 = nc.dram_tensor("s_x1", [OWN * 128, C], F32, kind=okind).ap()
    s_y = (nc.dram_tensor("s_y", [1024, 2048], BF16, kind="ExternalOutput").ap()
           if DEBUG else None)

    ctx = contextlib.ExitStack()
    en = ctx.enter_context
    P = Prog(nc, ctx)

    mska = en(nc.sbuf_tensor("mska", [128, 128], BF16))
    mskb = en(nc.sbuf_tensor("mskb", [128, 128], BF16))
    onesc = en(nc.sbuf_tensor("onesc", [128, 1], BF16))
    onesr = en(nc.sbuf_tensor("onesr", [128, 64], BF16))
    idn = en(nc.sbuf_tensor("idn", [128, 128], BF16))
    lns = en(nc.sbuf_tensor("lns", [128, 2, 10], F32))
    qw = en(nc.sbuf_tensor("qw", [128, 512], F32))
    qb = en(nc.sbuf_tensor("qb", [128, 11, 512], BF16))
    qm = en(nc.sbuf_tensor("qm", [128, 4, 32], F32))
    qout = en(nc.sbuf_tensor("qout", [128, 2, 512], BF16))
    qws0 = {"qm": qm, "qw": qw, "qb": qb, "prev": None}
    ps = en(nc.psum_tensor("ps", [128, 8, 512], F32))
    psx = ps[:, 6, :]
    pst = ps[:, 7, :].bitcast(BF16)

    

    def emit_quant_v(src_ap, dst_ap, fd, src_tok=None, dst_tok=None,
                     np_=128, ws=None):
        if ws is None:
            ws = qws0
        qm, qw, qb = ws["qm"], ws["qw"], ws["qb"]
        nb = fd // 16
        t0 = P.emit("vector", lambda e: e.tensor_reduce(
            qm[0:np_, 0, 0:nb], src_ap.rearrange("p (b s) -> p b s", s=16),
            axis=AX.X, op=ALU.max, apply_absolute_value=True),
            waits=[src_tok], mark=True)
        vt2 = P.emit("vector", lambda e: e.tensor_single_scalar(
            qm[0:np_, 3, 0:nb], qm[0:np_, 0, 0:nb], 1.0 / 12.0, ALU.mult),
            waits=[t0], mark=True)
        m2 = P.emit("vector", lambda e: e.tensor_tensor(
            qm[0:np_, 1, 0:nb], qm[0:np_, 0, 0:nb], qm[0:np_, 0, 0:nb],
            op=ALU.mult), waits=[t0], mark=True)
        x2 = P.emit("scalar", lambda e: e.square(
            qw[0:np_, 0:fd], src_ap), waits=[src_tok, ws.get("cl")],
            mark=True)
        m2in = qm[0:np_, 1, 0:nb].unsqueeze(2).broadcast_to([np_, nb, 16])
        qw3 = qw[0:np_, 0:fd].rearrange("p (b s) -> p b s", s=16)
        cl = None
        for i, thr in enumerate(_QT):
            cl = P.emit("vector", lambda e, i=i, k=thr / 36.0: e.scalar_tensor_tensor(
                qb[0:np_, i, 0:fd].rearrange("p (b s) -> p b s", s=16),
                m2in, k, qw3, ALU.mult, ALU.is_lt),
                waits=[m2, x2] if i == 0 else (), mark=(i == 6))
        cm = P.emit("vector", lambda e: e.tensor_tensor(
            qb[0:np_, 7, 0:fd], qb[0:np_, 0, 0:fd],
            qb[0:np_, 1, 0:fd], op=ALU.add), waits=[cl], mark=True)
        a2 = P.emit("vector", lambda e: e.tensor_tensor(
            qb[0:np_, 8, 0:fd], qb[0:np_, 2, 0:fd], qb[0:np_, 3, 0:fd],
            op=ALU.add), waits=[cl], mark=True)
        a3 = P.emit("vector", lambda e: e.tensor_tensor(
            qb[0:np_, 9, 0:fd], qb[0:np_, 7, 0:fd], qb[0:np_, 8, 0:fd],
            op=ALU.add), waits=[a2], mark=True)
        a4 = P.emit("vector", lambda e: e.scalar_tensor_tensor(
            qb[0:np_, 0, 0:fd], qb[0:np_, 6, 0:fd], 2.0, qb[0:np_, 5, 0:fd],
            ALU.mult, ALU.add), waits=[cm], mark=True)
        a5 = P.emit("vector", lambda e: e.scalar_tensor_tensor(
            qb[0:np_, 1, 0:fd], qb[0:np_, 4, 0:fd], 2.0, qb[0:np_, 9, 0:fd],
            ALU.mult, ALU.add), waits=[a3], mark=True)
        a6 = P.emit("vector", lambda e: e.scalar_tensor_tensor(
            qb[0:np_, 2, 0:fd], qb[0:np_, 0, 0:fd], 2.0, qb[0:np_, 1, 0:fd],
            ALU.mult, ALU.add), waits=[a5], mark=True)
        sgn = P.emit("scalar", lambda e: e.activation(
            qb[0:np_, 10, 0:fd], src_ap, AF.Sign),
            waits=[src_tok, ws.get("u1")], mark=True)
        u1 = P.emit("vector", lambda e: e.tensor_tensor(
            qb[0:np_, 5, 0:fd], qb[0:np_, 2, 0:fd], qb[0:np_, 10, 0:fd],
            op=ALU.mult), waits=[a6, sgn], mark=True)
        vq = P.emit("vector", lambda e, d=dst_ap: e.tensor_tensor(
            d.rearrange("p (b s) -> p b s", s=16),
            qb[0:np_, 5, 0:fd].rearrange("p (b s) -> p b s", s=16),
            qm[0:np_, 3, 0:nb].unsqueeze(2).broadcast_to([np_, nb, 16]),
            op=ALU.mult), waits=[vt2, u1, dst_tok], mark=True)
        ws.update(cl=cl, u1=u1)
        ws["prev"] = vq
        return vq

    lnst = {"l3": None, "sq": [None, None]}

    def emit_ln_head(rsrc, rt, sl):
        l1 = P.emit("vector", lambda e: e.tensor_reduce(
            lns[:, sl, 0:1], rsrc, axis=AX.X, op=ALU.add), waits=[rt],
            mark=True)
        l2 = P.emit("vector", lambda e: e.tensor_single_scalar(
            lns[:, sl, 1:2], lns[:, sl, 0:1], 1.0 / 1024.0, ALU.mult),
            waits=[l1], mark=True)
        sq = P.emit("scalar", lambda e: e.square(lnw[:], rsrc),
                    waits=[rt, lnst["l3"]], mark=True)
        lnst["sq"][sl] = sq
        l3 = P.emit("vector", lambda e: e.tensor_reduce(
            lns[:, sl, 2:3], lnw[:], axis=AX.X, op=ALU.add), waits=[sq],
            mark=True)
        lnst["l3"] = l3
        l4 = P.emit("vector", lambda e: e.tensor_single_scalar(
            lns[:, sl, 3:4], lns[:, sl, 2:3], 1.0 / 1024.0, ALU.mult),
            waits=[l3], mark=True)
        l5 = P.emit("vector", lambda e: e.tensor_tensor(
            lns[:, sl, 4:5], lns[:, sl, 1:2], lns[:, sl, 1:2], op=ALU.mult),
            waits=[l2], mark=True)
        l6 = P.emit("vector", lambda e: e.tensor_tensor(
            lns[:, sl, 5:6], lns[:, sl, 3:4], lns[:, sl, 4:5],
            op=ALU.subtract), waits=[l4, l5], mark=True)
        l7 = P.emit("vector", lambda e: e.tensor_single_scalar(
            lns[:, sl, 6:7], lns[:, sl, 5:6], EPS, ALU.add), waits=[l6],
            mark=True)
        l8 = P.emit("vector", lambda e: e.tensor_single_scalar(
            lns[:, sl, 7:8], lns[:, sl, 1:2], -1.0, ALU.mult), waits=[l2],
            mark=True)
        l9 = P.emit("scalar", lambda e: e.activation(
            lns[:, sl, 8:9], lns[:, sl, 6:7], AF.Ln), waits=[l7], mark=True)
        l10 = P.emit("scalar", lambda e: e.activation(
            lns[:, sl, 9:10], lns[:, sl, 8:9], AF.Exp, scale=-0.5),
            waits=[l9], mark=True)
        return (rsrc, sl, l10, l8)

    def emit_ln_tail(head, dst_ap, dst_tok=None):
        rsrc, sl, l10, l8 = head
        return P.emit("vector", lambda e, d=dst_ap: e.tensor_scalar(
            d, rsrc, lns[:, sl, 7:8], lns[:, sl, 9:10], ALU.add, ALU.mult),
            waits=[l10, l8, dst_tok], mark=True)

    def emit_ln(rsrc, dst_ap, rt, dst_tok=None, sl=0):
        return emit_ln_tail(emit_ln_head(rsrc, rt, sl), dst_ap, dst_tok)

    P.emit("sync", lambda e: e.dma_start(out=mska[:], in_=d_ma[:, :]), dma=True)
    P.emit("sync", lambda e: e.dma_start(out=mskb[:], in_=d_mb[:, :]), dma=True)
    P.emit("sync", lambda e: e.dma_start(out=idn[:], in_=d_id[:, :]), dma=True)
    P.emit("gpsimd", lambda e: e.memset(onesc[:], 1.0), mark=True)
    P.emit("gpsimd", lambda e: e.memset(onesr[:], 1.0), mark=True)

    with nc.sbuf_tensor("xstg", [128, 2, 4096], BF16) as xstg:
        spills = []
        njob = (NT + OWN) // 4
        for t in range(njob):
            stg = xstg[:, t % 2, :]
            if t < NT // 4:
                srcb = d_x[512 * t:512 * (t + 1), :]
                dstb = s_xb[512 * t:512 * (t + 1), :]
            else:
                tt = t - NT // 4
                srcb = d_xo[512 * tt:512 * (tt + 1), :]
                dstb = s_xob[512 * tt:512 * (tt + 1), :]
            w = [spills[t - 2]] if t >= 2 else []
            P.emit("gpsimd", lambda e, stg=stg, srcb=srcb: e.dma_start(
                out=stg, in_=srcb), waits=w, dma=True)
            ld = P.tok("gpsimd")
            P.emit("sync", lambda e, stg=stg, dstb=dstb: e.dma_start(
                out=dstb, in_=stg), waits=[ld], dma=True)
            spills.append(P.tok("sync"))
        spill_all = P.tok("sync")

    with contextlib.ExitStack() as yctx:
        y16 = yctx.enter_context(nc.sbuf_tensor("y16", [128, 8, 2048], BF16))

        with contextlib.ExitStack() as actx:
            aen = actx.enter_context
            xT = aen(nc.sbuf_tensor("xT", [128, 8, T], BF16))
            xoT = aen(nc.sbuf_tensor("xoT", [128, 8, 2048], BF16))
            waq = aen(nc.sbuf_tensor("waq", [128, 8, 128], BF16))
            wak = aen(nc.sbuf_tensor("wak", [128, 8, 128], BF16))
            wav = aen(nc.sbuf_tensor("wav", [128, 8, 256], BF16))
            v16 = aen(nc.sbuf_tensor("v16", [128, 32, 256], BF16))
            k16 = aen(nc.sbuf_tensor("k16", [128, T], BF16))
            q16 = aen(nc.sbuf_tensor("q16", [128, 2048], BF16))
            dect = aen(nc.sbuf_tensor("dect", [128, 18 * 128], BF16))
            pAB = aen(nc.sbuf_tensor("pAB", [128, 3, 2, 512], BF16))
            pA = pAB[:, :, 0, :]
            pB = pAB[:, :, 1, :]
            pmA = aen(nc.sbuf_tensor("pmA", [128, 3, 128], BF16))
            pmB = aen(nc.sbuf_tensor("pmB", [128, 3, 128], BF16))
            pdA = aen(nc.sbuf_tensor("pdA", [128, 3, 512], BF16))
            pdB = aen(nc.sbuf_tensor("pdB", [128, 3, 512], BF16))
            rz = aen(nc.sbuf_tensor("rz", [128, 512], F32, side="right"))
            rz16 = aen(nc.sbuf_tensor("rz16", [128, 512], BF16, side="right"))
            rzb = aen(nc.sbuf_tensor("rzb", [128, 512], BF16, side="right"))
            wf32 = aen(nc.sbuf_tensor("wf32", [128, 2, 512], F32))

            wjobs = []
            for r0, r1 in ((0, 128), (128, 256), (256, 352)):
                for f0 in (0, 512):
                    wjobs.append((d_wg, s_wgi, r0, r1, f0, f0 + 512))
            for r0, r1 in ((0, 128), (128, 256), (256, 352)):
                for f0 in (0, 512):
                    wjobs.append((d_wu, s_wui, r0, r1, f0, f0 + 512))
            for ci in range(6):
                f0, f1 = 512 * ci, min(FF, 512 * ci + 512)
                wjobs.append((d_wd, s_wdi, 0, 128, f0, f1))
            wlds = []
            wsps = [None] * len(wjobs)
            wvqs = [None] * len(wjobs)

            def emit_wload(j):
                dsrc, ddst, r0, r1, f0, f1 = wjobs[j]
                np_, fd = r1 - r0, f1 - f0
                P.emit("scalar", lambda e, dsrc=dsrc, r0=r0, r1=r1, f0=f0, f1=f1, fd=fd, np_=np_, sl=j % 2: e.dma_start(
                    out=wf32[0:np_, sl, 0:fd], in_=dsrc[r0:r1, f0:f1]),
                    waits=[wvqs[j - 2]] if j >= 2 else (), dma=True)
                wlds.append(P.tok("scalar"))

            emit_wload(0)
            emit_wload(1)
            for wj, (dsrc, ddst, r0, r1, f0, f1) in enumerate(wjobs):
                np_, fd = r1 - r0, f1 - f0
                vq = emit_quant_v(wf32[0:np_, wj % 2, 0:fd],
                                  qout[0:np_, wj % 2, 0:fd], fd,
                                  src_tok=wlds[wj],
                                  dst_tok=wsps[wj - 2] if wj >= 2 else None,
                                  np_=np_)
                wvqs[wj] = vq
                P.emit("scalar", lambda e, ddst=ddst, r0=r0, r1=r1, f0=f0, f1=f1, fd=fd, np_=np_, sl=wj % 2: e.dma_start(
                    out=ddst[r0:r1, f0:f1], in_=qout[0:np_, sl, 0:fd]),
                    waits=[vq], dma=True)
                wsps[wj] = P.tok("scalar")
                if wj + 2 < len(wjobs):
                    emit_wload(wj + 2)
            wq_spill = P.tok("scalar")
            ccd = {}

            def emit_cc():
                for (ci, co) in ((s_wgi, s_wgq), (s_wui, s_wuq),
                                 (s_wdi, s_wdq)):
                    P.emit("gpsimd",
                           lambda e, ci=ci, co=co: e.collective_compute(
                               "AllGather", ALU.bypass, ins=[ci[:, :]],
                               outs=[co[:, :]],
                               replica_groups=[[0, 1, 2, 3, 4, 5, 6, 7]]),
                           waits=[wq_spill], mark=True)
                ccd["tok"] = P.tok("gpsimd")

            P.emit("sync", lambda e: e.dma_start(out=dect[:], in_=d_dec[:, :]),
                   dma=True)
            for cc in range(8):
                P.emit("sync", lambda e, cc=cc: e.dma_start(
                    out=xT[:, cc, :], in_=s_xb[:, 128 * cc:128 * (cc + 1)],
                    transpose=True), waits=[spill_all], dma=True)
            for cc in range(8):
                P.emit("sync", lambda e, cc=cc: e.dma_start(
                    out=xoT[:, cc, :], in_=s_xob[:, 128 * cc:128 * (cc + 1)],
                    transpose=True), dma=True)
            xt_tok = P.tok("sync")


            ROT = ((0, 1), (2, 3), (6, 7))
            PROT = (0, 1, 6)
            pend = [None]

            last_pair_scal = None
            last_pair_expf = None

            def emit_wa(qt, pr, prod_tok):
                co = 256 * qt + 128 * pr
                toks = {}
                if pr == 0:
                    for cc in range(8):
                        P.emit("gpsimd", lambda e, cc=cc, qt=qt: e.dma_start(
                            out=wav[:, cc, :],
                            in_=d_wa[128 * cc:128 * (cc + 1),
                                     2048 + 256 * qt:2048 + 256 * (qt + 1)]),
                            waits=[prod_tok] if cc == 0 else (), dma=True)
                    toks["v"] = P.tok("gpsimd")
                for cc in range(8):
                    P.emit("gpsimd", lambda e, cc=cc, co=co: e.dma_start(
                        out=wak[:, cc, :],
                        in_=d_wa[128 * cc:128 * (cc + 1),
                                 1024 + co:1024 + co + 128]),
                        waits=[prod_tok] if (pr == 1 and cc == 0) else (),
                        dma=True)
                toks["k"] = P.tok("gpsimd")
                for cc in range(8):
                    P.emit("gpsimd", lambda e, cc=cc, co=co: e.dma_start(
                        out=waq[:, cc, :],
                        in_=d_wa[128 * cc:128 * (cc + 1), co:co + 128]),
                        dma=True)
                toks["q"] = P.tok("gpsimd")
                return toks

            wa_next = emit_wa(0, 0, spill_all)
            for qt in range(nqt if phlim >= 1 else 0):
                for pr in range(2):
                    gp = 2 * qt + pr
                    wa_cur = wa_next
                    if qt == nqt - 1 and pr == 1:
                        emit_cc()

                    evac_toks = [None, None, None]
                    jobs = []
                    if pr == 0:
                        for t in range(NT):
                            jobs.append(("v", t))
                    jobs += [("k", ch) for ch in range(8)]
                    jobs += [("q", ch) for ch in range(4)]
                    seen_kind = set()
                    for ji, (kind, idx) in enumerate(jobs):
                        bk = PROT[ji % 3]
                        bw = [evac_toks[ji % 3]]
                        if kind not in seen_kind:
                            seen_kind.add(kind)
                            bw += [wa_cur[kind]]
                        if ji == 0:
                            bw += [xt_tok, last_pair_expf]
                        if ji == 2 and pend[0] is not None:
                            sc, _vec = pend[0]()
                            pend[0] = None
                            bw += [sc]
                        for cc in range(8):
                            if kind == "v":
                                P.emit("tensor", lambda e, t=idx, cc=cc, bk=bk: e.matmul(
                                    ps[:, bk, 0:256],
                                    xT[:, cc, 128 * t:128 * (t + 1)],
                                    wav[:, cc, :], start=(cc == 0),
                                    stop=(cc == 7)),
                                    waits=bw if cc == 0 else (),
                                    mark=(cc == 7))
                            elif kind == "k":
                                P.emit("tensor", lambda e, ch=idx, cc=cc, bk=bk: e.matmul(
                                    ps[:, bk, :], wak[:, cc, :],
                                    xT[:, cc, 512 * ch:512 * (ch + 1)],
                                    start=(cc == 0), stop=(cc == 7)),
                                    waits=bw if cc == 0 else (),
                                    mark=(cc == 7))
                            else:
                                P.emit("tensor", lambda e, ch=idx, cc=cc, bk=bk: e.matmul(
                                    ps[:, bk, :], waq[:, cc, :],
                                    xoT[:, cc, 512 * ch:512 * (ch + 1)],
                                    start=(cc == 0), stop=(cc == 7)),
                                    waits=bw if cc == 0 else (),
                                    mark=(cc == 7))
                        mm = P.tok("tensor")
                        if kind == "v":
                            tk = P.emit("vector", lambda e, t=idx, bk=bk: e.tensor_copy(
                                v16[:, t, :], ps[:, bk, 0:256]), waits=[mm],
                                mark=True)
                        elif kind == "k":
                            tk = P.emit("vector", lambda e, ch=idx, bk=bk: e.tensor_copy(
                                k16[:, 512 * ch:512 * (ch + 1)], ps[:, bk, :]),
                                waits=[mm], mark=True)
                        else:
                            tk = P.emit("vector", lambda e, ch=idx, bk=bk: e.tensor_copy(
                                q16[:, 512 * ch:512 * (ch + 1)], ps[:, bk, :]),
                                waits=[mm], mark=True)
                        evac_toks[ji % 3] = tk
                    kq_tok = P.tok("vector")
                    if not (qt == nqt - 1 and pr == 1):
                        nq2, np2 = (qt, 1) if pr == 0 else (qt + 1, 0)
                        wa_next = emit_wa(nq2, np2, P.tok("tensor"))
                    prev_qc_scal = None
                    prev_qc_vec = None
                    prev_qc_expf = None

                    for qc in range(nqc if attlim >= 2 else 0):
                        nkt = 8 * qc + 8
                        z_started = [False, False]
                        pv_started = [False, False]
                        exp_toks = {}
                        vmarks = {}
                        tmarks = {}
                        kt_geom = {}

                        def geom(kt, qc=qc):
                            jlo = min(3, max(0, -(-(kt - 1 - 8 * qc) // 2)))
                            d0 = kt - 8 * qc
                            jm = None
                            msk = None
                            if 0 <= d0 <= 7:
                                jm = d0 // 2 if d0 % 2 == 0 else (d0 - 1) // 2
                                msk = mska if d0 % 2 == 0 else mskb
                            jwlo = jlo
                            jwhi_t = min(3, (kt + 16 - 8 * qc) // 2)
                            jwhi = 3 if kt == 0 else jwhi_t
                            if attlim < 3:
                                jm, msk, jwhi_t, jwhi = None, None, jwlo - 1, jwlo - 1
                            return jlo, jm, msk, jwlo, jwhi, jwhi_t

                        def emit_st(kt, qc=qc):
                            jlo = geom(kt)[0]
                            w0 = 128 * jlo
                            ba, bb = ROT[kt % 3]
                            w = [exp_toks.get(kt - 3), kq_tok]
                            if kt < 2:
                                w += [prev_qc_expf]
                            elif kt == 2:
                                w += [prev_qc_expf, prev_qc_scal]
                            for h, bx in ((0, ba), (1, bb)):
                                P.emit("tensor",
                                       lambda e, h=h, kt=kt, w0=w0, bx=bx, qc=qc: e.matmul(
                                           ps[:, bx, w0:],
                                           k16[64 * h:64 * (h + 1),
                                               128 * kt:128 * (kt + 1)],
                                           q16[64 * h:64 * (h + 1),
                                               512 * qc + w0:512 * (qc + 1)],
                                           start=True, stop=True,
                                           tile_position=(64 * h, 0)),
                                       waits=w if h == 0 else (),
                                       mark=(h == 1))
                            return P.tok("tensor")

                        def emit_exp(kt, st_tok):
                            jlo = geom(kt)[0]
                            w0 = 128 * jlo
                            ba, bb = ROT[kt % 3]
                            sl = kt % 3
                            w = [st_tok, vmarks.get(kt - 3), tmarks.get(kt - 3)]
                            tok = P.emit("scalar",
                                         lambda e, w0=w0, ba=ba, sl=sl: e.activation(
                                             pAB[:, sl, :, w0:],
                                             ps[:, ba:ba + 2, w0:],
                                             AF.Exp, scale=0.125),
                                         waits=w, mark=True)
                            exp_toks[kt] = tok
                            return tok

                        def emit_vops(kt):
                            jlo, jm, msk, jwlo, jwhi, jwhi_t = geom(kt)
                            sl = kt % 3
                            v_last = None
                            first = [exp_toks[kt]]
                            if jm is not None:
                                P.emit("vector",
                                       lambda e, sl=sl, jm=jm, m=msk: e.tensor_tensor(
                                           pmA[:, sl, :],
                                           pA[:, sl, 128 * jm:128 * (jm + 1)],
                                           m[:], op=ALU.mult), waits=first)
                                first = []
                                v_last = P.emit("vector",
                                                lambda e, sl=sl, jm=jm, m=msk: e.tensor_tensor(
                                                    pmB[:, sl, :],
                                                    pB[:, sl, 128 * jm:128 * (jm + 1)],
                                                    m[:], op=ALU.mult), mark=True)
                            if kt == 0 and jwhi_t < 3:
                                P.emit("vector", lambda e, sl=sl, j1=jwhi_t + 1: e.memset(
                                    pdA[:, sl, 128 * j1:], 0.0), waits=first)
                                first = []
                                v_last = P.emit("vector",
                                                lambda e, sl=sl, j1=jwhi_t + 1: e.memset(
                                                    pdB[:, sl, 128 * j1:], 0.0),
                                                mark=True)
                            if jwhi_t >= jwlo:
                                i0 = 2 * (4 * qc + jwlo) - kt + 1
                                nj = jwhi_t - jwlo + 1
                                dsl = dect[:].rearrange(
                                    "p (i q) -> p i q", q=128)[
                                    :, i0:i0 + 2 * nj - 1:2, :]
                                P.emit("vector",
                                       lambda e, sl=sl, jwlo=jwlo, nj=nj, dsl=dsl: e.tensor_tensor(
                                           pdA[:, sl, 128 * jwlo:128 * (jwlo + nj)]
                                           .rearrange("p (j q) -> p j q", q=128),
                                           pA[:, sl, 128 * jwlo:128 * (jwlo + nj)]
                                           .rearrange("p (j q) -> p j q", q=128),
                                           dsl, op=ALU.mult), waits=first)
                                first = []
                                v_last = P.emit("vector",
                                                lambda e, sl=sl, jwlo=jwlo, nj=nj, dsl=dsl: e.tensor_tensor(
                                                    pdB[:, sl, 128 * jwlo:128 * (jwlo + nj)]
                                                    .rearrange("p (j q) -> p j q", q=128),
                                                    pB[:, sl, 128 * jwlo:128 * (jwlo + nj)]
                                                    .rearrange("p (j q) -> p j q", q=128),
                                                    dsl, op=ALU.mult), mark=True)
                            vmarks[kt] = v_last
                            return v_last

                        def emit_zpv(kt, qc=qc):
                            jlo, jm, msk, jwlo, jwhi, jwhi_t = geom(kt)
                            sl = kt % 3
                            zlo = (jm + 1) if jm is not None else jlo
                            if attlim < 3:
                                zlo = 4
                            last_kt = (kt == nkt - 1)
                            tops = []
                            for h, praw, pmsk in ((0, pA, pmA), (1, pB, pmB)):
                                zops = []
                                if zlo <= 3:
                                    zops.append(("raw", praw, zlo))
                                if jm is not None:
                                    zops.append(("msk", pmsk, jm))
                                for oi, (knd, buf, j0) in enumerate(zops):
                                    st = not z_started[h]
                                    z_started[h] = True
                                    sp = last_kt and (oi == len(zops) - 1)
                                    if knd == "raw":
                                        tops.append(lambda e, h=h, sl=sl, j0=j0, buf=buf, st=st, sp=sp: e.matmul(
                                            ps[64 * h:64 * h + 1, 5, 128 * j0:],
                                            onesc[:], buf[:, sl, 128 * j0:],
                                            start=st, stop=sp,
                                            skip_group_check=True,
                                            tile_position=(0, 64 * h)))
                                    else:
                                        tops.append(lambda e, h=h, sl=sl, j0=j0, buf=buf, st=st, sp=sp: e.matmul(
                                            ps[64 * h:64 * h + 1, 5,
                                               128 * j0:128 * (j0 + 1)],
                                            onesc[:], buf[:, sl, :],
                                            start=st, stop=sp,
                                            skip_group_check=True,
                                            tile_position=(0, 64 * h)))
                            if attlim >= 4 and jwhi >= jwlo:
                                for h, pdec in ((0, pdA), (1, pdB)):
                                    st = not pv_started[h]
                                    pv_started[h] = True
                                    tops.append(lambda e, h=h, sl=sl, kt=kt, pr=pr, jwlo=jwlo, jwhi=jwhi, pdec=pdec, st=st, sp=last_kt: e.matmul(
                                        ps[64 * h:64 * (h + 1), 4,
                                           128 * jwlo:128 * (jwhi + 1)],
                                        v16[:, kt, 128 * pr + 64 * h:
                                            128 * pr + 64 * (h + 1)],
                                        pdec[:, sl, 128 * jwlo:128 * (jwhi + 1)],
                                        start=st, stop=sp,
                                        skip_group_check=True,
                                        tile_position=(0, 64 * h)))
                            w0l = [vmarks.get(kt) or exp_toks[kt]]
                            if kt == 0:
                                w0l += [prev_qc_scal, prev_qc_vec]
                            tk = None
                            for oi, tf in enumerate(tops):
                                tk = P.emit("tensor", tf,
                                            waits=w0l if oi == 0 else (),
                                            mark=(oi == len(tops) - 1))
                            tmarks[kt] = tk
                            return tk

                        for kt in range(nkt):
                            if kt == 2 and pend[0] is not None:
                                prev_qc_scal, prev_qc_vec = pend[0]()
                                pend[0] = None
                            st_tok = emit_st(kt)
                            emit_exp(kt, st_tok)
                            emit_vops(kt)
                            if kt >= 2:
                                emit_zpv(kt - 2)
                        emit_zpv(nkt - 2)
                        emit_zpv(nkt - 1)
                        zpv_tok = P.tok("tensor")
                        prev_qc_expf = exp_toks[nkt - 1]
                        if attlim < 5:
                            continue
                        P.emit("scalar", lambda e: e.activation(
                            rz[0:1, :], ps[0:1, 5, :], AF.Ln),
                            waits=[zpv_tok])
                        lt = P.emit("scalar", lambda e: e.activation(
                            rz[64:65, :], ps[64:65, 5, :], AF.Ln), mark=True)
                        P.emit("scalar", lambda e: e.activation(
                            rz16[0:1, :], rz[0:1, :], AF.Exp, scale=-1.0),
                            waits=[lt])
                        se = P.emit("scalar", lambda e: e.activation(
                            rz16[64:65, :], rz[64:65, :], AF.Exp, scale=-1.0),
                            mark=True)

                        def mk_norm(se=se, gp=gp, qc=qc,
                                    expf=exp_toks[nkt - 1]):
                            def go():
                                for h in range(2):
                                    P.emit("tensor", lambda e, h=h: e.matmul(
                                        psx[64 * h:64 * (h + 1), :],
                                        onesr[64 * h:64 * h + 1, :],
                                        rz16[64 * h:64 * h + 1, :],
                                        start=True, stop=True,
                                        skip_group_check=True,
                                        tile_position=(64 * h, 64 * h)),
                                        waits=([se, expf] if h == 0 else ()),
                                        mark=(h == 1))
                                bc = P.tok("tensor")
                                sbc = P.emit("scalar", lambda e: e.copy(
                                    rzb[:], psx[:]), waits=[bc], mark=True)
                                vec = P.emit(
                                    "vector",
                                    lambda e, gp=gp, qc=qc: e.tensor_tensor(
                                        y16[:, gp, 512 * qc:512 * (qc + 1)],
                                        ps[:, 4, :], rzb[:], op=ALU.mult),
                                    waits=[sbc], mark=True)
                                return sbc, vec
                            return go
                        pend[0] = mk_norm()
                        last_pair_expf = prev_qc_expf

            if pend[0] is not None:
                pend[0]()
                pend[0] = None
            if "tok" not in ccd:
                emit_cc()
            cc_tok = ccd["tok"]
            att_done = {e: P.tok(e) for e in ENGS}
        xjit = en(nc.sbuf_tensor("xjit", [128, 2, 1024], F32, side="right"))
        r1 = en(nc.sbuf_tensor("r1", [128, 2, 1024], F32, side="right"))
        x1t = en(nc.sbuf_tensor("x1t", [128, 1024], F32, side="right"))
        outt = en(nc.sbuf_tensor("outt", [128, 1024], F32, side="right"))
        lnw = en(nc.sbuf_tensor("lnw", [128, 1024], F32, side="right"))
        qout2 = en(nc.sbuf_tensor("qout2", [128, 2, 512], BF16, side="right"))

        wp16 = yctx.enter_context(nc.sbuf_tensor("wp16", [128, 8, 1024], BF16))
        for cc in range(8):
            P.emit("gpsimd", lambda e, cc=cc: e.dma_start(
                out=wp16[:, cc, :], in_=d_wp[128 * cc:128 * (cc + 1), :]),
                waits=([att_done["tensor"], att_done["vector"],
                        att_done["scalar"]] if cc == 0 else ()), dma=True)
        wp_tok = P.tok("gpsimd")
        r1_toks = [None] * OWN
        x1_spills = []
        xq_spills = []
        xq_sp_prev = [None, None]
        out_toks = [None] * OWN
        pend_ln = [None]

        def fin_p2(head, tp):
            x1tok = emit_ln_tail(
                head, x1t[:],
                dst_tok=x1_spills[tp - 1] if tp >= 1 else None)
            P.emit("gpsimd", lambda e, tp=tp: e.dma_start(
                out=s_x1[128 * tp:128 * (tp + 1), :], in_=x1t[:]),
                waits=[x1tok], dma=True)
            x1_spills.append(P.tok("gpsimd"))
            for f0 in (0, 512):
                vq = emit_quant_v(x1t[:, f0:f0 + 512], qout2[:, f0 // 512, :],
                                  512, src_tok=x1tok,
                                  dst_tok=xq_sp_prev[f0 // 512])
                P.emit("gpsimd", lambda e, tp=tp, f0=f0: e.dma_start(
                    out=s_xq[128 * tp:128 * (tp + 1), f0:f0 + 512],
                    in_=qout2[:, f0 // 512, :]), waits=[vq], dma=True)
                xq_sp_prev[f0 // 512] = P.tok("gpsimd")
            xq_spills.append(P.tok("gpsimd"))
        for t in range(OWN if phlim >= 2 else 0):
            P.emit("sync", lambda e, t=t: e.dma_start(
                out=xjit[:, t % 2, :], in_=d_xo[128 * t:128 * (t + 1), :]),
                waits=[r1_toks[t - 2] if t >= 2 else None,
                       att_done["vector"] if t < 2 else None], dma=True)
            xj = P.tok("sync")
            for half in range(2):
                for cc in range(8):
                    P.emit("tensor", lambda e, t=t, half=half, cc=cc: e.matmul(
                        ps[:, half, :], y16[:, cc, 128 * t:128 * (t + 1)],
                        wp16[:, cc, 512 * half:512 * (half + 1)],
                        start=(cc == 0), stop=(cc == 7)),
                        waits=([att_done["scalar"], att_done["vector"],
                                att_done["gpsimd"], wp_tok,
                                r1_toks[t - 1] if t >= 1 else None]
                               if (half == 0 and cc == 0) else ()),
                        mark=(half == 1 and cc == 7))
            pj = P.tok("tensor")
            rt = P.emit("vector", lambda e, t=t: e.tensor_tensor(
                r1[:, t % 2, :], ps[:, 0:2, :].rearrange("p a b -> p (a b)"),
                xjit[:, t % 2, :], op=ALU.add),
                waits=[pj, xj, lnst["sq"][t % 2]], mark=True)
            r1_toks[t] = rt
            head = emit_ln_head(r1[:, t % 2, :], rt, t % 2)
            if pend_ln[0] is not None:
                fin_p2(*pend_ln[0])
            pend_ln[0] = (head, t)
        if pend_ln[0] is not None:
            fin_p2(*pend_ln[0])
            pend_ln[0] = None
        if DEBUG:
            for gpd in range(8):
                P.emit("sync", lambda e, gpd=gpd: e.dma_start(
                    out=s_y[128 * gpd:128 * (gpd + 1), :],
                    in_=y16[:, gpd, :]), waits=[att_done["vector"]], dma=True)
        p2_done = {e: P.tok(e) for e in ENGS}

    with contextlib.ExitStack() as fctx:
        fen = fctx.enter_context
        wgT = fen(nc.sbuf_tensor("wgT", [128, 8, FF], BF16))
        wuT = fen(nc.sbuf_tensor("wuT", [128, 8, FF], BF16))
        wdT = fen(nc.sbuf_tensor("wdT", [128, 22, 1024], BF16))
        xqTt = fen(nc.sbuf_tensor("xqTt", [128, 2, 8, 128], BF16))
        hq = fen(nc.sbuf_tensor("hq", [128, FF], BF16))
        hqT = fen(nc.sbuf_tensor("hqT", [128, 22, 128], BF16))
        hbuf = fen(nc.sbuf_tensor("hbuf", [128, 2, 512], F32))
        sgl = fen(nc.sbuf_tensor("sgl", [128, 2, 512], F32))


        tlw = [p2_done["tensor"], cc_tok]
        for cc in range(8):
            P.emit("sync", lambda e, cc=cc: e.dma_start(
                out=wgT[:, cc, :], in_=s_wgq[:, 128 * cc:128 * (cc + 1)],
                transpose=True), waits=tlw if cc == 0 else (), dma=True)
        for cc in range(8):
            P.emit("sync", lambda e, cc=cc: e.dma_start(
                out=wuT[:, cc, :], in_=s_wuq[:, 128 * cc:128 * (cc + 1)],
                transpose=True), dma=True)
        wgu_tok = P.tok("sync")
        for ic in range(22):
            P.emit("sync", lambda e, ic=ic: e.dma_start(
                out=wdT[:, ic, :], in_=s_wdq[:, 128 * ic:128 * (ic + 1)],
                transpose=True), dma=True)
        wT_tok = P.tok("sync")

        ICH = [(0, 512), (512, 1024), (1024, 1536), (1536, 2048),
               (2048, 2560), (2560, FF)]
        frt = [None] * OWN

        def fin_ffn(head, tp):
            ot = emit_ln_tail(head, outt[:],
                              dst_tok=out_toks[tp - 1] if tp >= 1 else None)
            P.emit("sync", lambda e, tp=tp: e.dma_start(
                out=d_out[128 * tp:128 * (tp + 1), :], in_=outt[:]),
                waits=[ot], dma=True)
            out_toks[tp] = P.tok("sync")
        xqT_toks = [None] * OWN
        dn_toks = [None] * OWN
        last_silu = [None, None]
        last_h = [None, None]
        prev_cp = None
        prev_trts = [None, None, None]
        for t in range(OWN if phlim >= 3 else 0):
            for cc in range(8):
                P.emit("sync", lambda e, t=t, cc=cc: e.dma_start(
                    out=xqTt[:, t % 2, cc, :],
                    in_=s_xq[128 * t:128 * (t + 1), 128 * cc:128 * (cc + 1)],
                    transpose=True),
                    waits=([xq_spills[t], dn_toks[t - 2] if t >= 2 else None]
                           if cc == 0 else ()), dma=True)
            xqT_toks[t] = P.tok("sync")
            vq_toks = []
            dn_half = [None, None]

            def emit_tgrp(grp, t=t, vq_toks=vq_toks, dn_half=dn_half):
                nonlocal prev_cp
                vq_gate = vq_toks[2 * grp + 1]
                ics = list(range(8 * grp, min(22, 8 * grp + 8)))
                for ii, ic in enumerate(ics):
                    P.emit("tensor", lambda e, ic=ic, ii=ii: e.transpose(
                        pst[:, 128 * ii:128 * (ii + 1)],
                        hq[:, 128 * ic:128 * (ic + 1)], idn[:]),
                        waits=([prev_cp, vq_gate] if ii == 0 else ()),
                        mark=(ii == len(ics) - 1))
                trt = P.tok("tensor")
                prev_trts[grp] = trt
                prev_cp = P.emit(
                    "scalar", lambda e, grp=grp, n=len(ics): e.copy(
                        hqT[:, 8 * grp:8 * grp + n, :],
                        pst[:, 0:128 * n].rearrange("p (i q) -> p i q", q=128)),
                    waits=[trt], mark=True)
                for half in range(2):
                    for ic in ics:
                        P.emit("tensor", lambda e, ic=ic, half=half: e.matmul(
                            ps[:, 4 + half, :], hqT[:, ic, :],
                            wdT[:, ic, 512 * half:512 * (half + 1)],
                            start=(ic == 0), stop=(ic == 21)),
                            waits=([prev_cp,
                                    wT_tok if (t == 0 and grp == 0)
                                    else None] if ic == ics[0] else ()),
                            mark=(grp == 2 and ic == 21))
                    if grp == 2:
                        dn_half[half] = P.tok("tensor")
            for ici, (f0, f1) in enumerate(ICH):
                fw = f1 - f0
                bg = ici % 2
                for cc in range(8):
                    P.emit("tensor", lambda e, t=t, cc=cc, f0=f0, f1=f1, bg=bg: e.matmul(
                        ps[:, 2 * bg, 0:f1 - f0], xqTt[:, t % 2, cc, :],
                        wgT[:, cc, f0:f1], start=(cc == 0), stop=(cc == 7)),
                        waits=([xqT_toks[t], wgu_tok, last_silu[bg],
                                last_h[bg]] +
                               ([p2_done["vector"], p2_done["scalar"],
                                 p2_done["gpsimd"]]
                                if (t == 0 and ici == 0) else [])
                               if cc == 0 else ()))
                for cc in range(8):
                    P.emit("tensor", lambda e, t=t, cc=cc, f0=f0, f1=f1, bg=bg: e.matmul(
                        ps[:, 2 * bg + 1, 0:f1 - f0], xqTt[:, t % 2, cc, :],
                        wuT[:, cc, f0:f1], start=(cc == 0), stop=(cc == 7)),
                        mark=(cc == 7))
                gu = P.tok("tensor")
                sg_t = P.emit("scalar", lambda e, bg=bg, fw=fw, sv=ici % 2: e.activation(
                    sgl[:, sv, 0:fw], ps[:, 2 * bg, 0:fw], AF.Silu),
                    waits=[gu, last_h[bg]], mark=True)
                last_silu[bg] = sg_t
                htok = P.emit("vector", lambda e, bg=bg, fw=fw, sv=ici % 2: e.tensor_tensor(
                    hbuf[:, sv, 0:fw], sgl[:, sv, 0:fw],
                    ps[:, 2 * bg + 1, 0:fw],
                    op=ALU.mult), waits=[sg_t], mark=True)
                last_h[bg] = htok
                emit_quant_v(hbuf[:, ici % 2, 0:fw], hq[:, f0:f1], fw,
                             src_tok=htok, dst_tok=prev_trts[ici // 2])
                vq_toks.append(P.tok("vector"))
                if ici == 0 and pend_ln[0] is not None:
                    fin_ffn(*pend_ln[0])
                    pend_ln[0] = None
                if ici in (3, 5):
                    emit_tgrp({3: 0, 5: 1}[ici])
            emit_tgrp(2)
            dn = P.tok("tensor")
            dn_toks[t] = dn
            P.emit("sync", lambda e, t=t: e.dma_start(
                out=xjit[:, t % 2, :], in_=s_x1[128 * t:128 * (t + 1), :]),
                waits=[x1_spills[t], p2_done["vector"] if t < 2 else None,
                       frt[t - 2] if t >= 2 else None], dma=True)
            xj = P.tok("sync")
            P.emit("vector", lambda e, t=t: e.tensor_tensor(
                r1[:, t % 2, 0:512], ps[:, 4, :],
                xjit[:, t % 2, 0:512], op=ALU.add),
                waits=[dn_half[0], xj, lnst["sq"][t % 2]], mark=True)
            rt = P.emit("vector", lambda e, t=t: e.tensor_tensor(
                r1[:, t % 2, 512:1024], ps[:, 5, :],
                xjit[:, t % 2, 512:1024], op=ALU.add),
                waits=[dn_half[1]], mark=True)
            frt[t] = rt
            head = emit_ln_head(r1[:, t % 2, :], rt, t % 2)
            pend_ln[0] = (head, t)
        if pend_ln[0] is not None:
            fin_ffn(*pend_ln[0])
            pend_ln[0] = None
        if phlim < 3:
            dz = P.emit("vector", lambda e: e.memset(outt[:], 0.0),
                        waits=[P.tok("tensor"), P.tok("scalar"),
                               P.tok("gpsimd"), P.tok("sync")], mark=True)
            P.emit("sync", lambda e: e.dma_start(
                out=d_out[0:128, :], in_=outt[:]), waits=[dz], dma=True)
        fin = P.tok("sync")
        for e in ENGS:
            P.emit(e, lambda en_: en_.nop(), waits=[fin])

    P.finalize()
    blk = en(nc.Block())

    @blk.tensor
    def _(e):
        P.replay("tensor", e)

    @blk.vector
    def _(e):
        P.replay("vector", e)

    @blk.scalar
    def _(e):
        P.replay("scalar", e)

    @blk.gpsimd
    def _(e):
        P.replay("gpsimd", e)

    @blk.sync
    def _(e):
        P.replay("sync", e)

    ctx.close()
    return nc


_NC_CACHE = {}


def _get_nc():
    if "nc" not in _NC_CACHE:
        _NC_CACHE["nc"] = build_program()
    return _NC_CACHE["nc"]


def kernel(L0_attn_x, L0_attn_W_attn, L0_attn_W_proj, L0_ffn_w_gate,
           L0_ffn_w_up, L0_ffn_w_down):
    x = np.ascontiguousarray(np.asarray(L0_attn_x, dtype=np.float32))
    wa = np.ascontiguousarray(np.asarray(L0_attn_W_attn, dtype=np.float32))
    wp = np.ascontiguousarray(np.asarray(L0_attn_W_proj, dtype=np.float32))
    wg = np.ascontiguousarray(np.asarray(L0_ffn_w_gate, dtype=np.float32))
    wu = np.ascontiguousarray(np.asarray(L0_ffn_w_up, dtype=np.float32))
    wd = np.ascontiguousarray(np.asarray(L0_ffn_w_down, dtype=np.float32))
    dec = [_decay_table(p) for p in range(2)]
    idm = np.ascontiguousarray(np.eye(128, dtype=ml_dtypes.bfloat16))
    msk = [_masks(p) for p in range(2)]
    in_maps = []
    for c in range(8):
        b, p = c // 2, c % 2
        xb = np.ascontiguousarray(x[b])
        own = np.ascontiguousarray(
            xb.reshape(NT, 128, C)[p::2].reshape(OWN * 128, C))
        in_maps.append({
            "x_full": xb, "x_own": own, "w_attn": wa, "w_proj": wp,
            "w_gate_sh": np.ascontiguousarray(wg[352 * c:352 * (c + 1)]),
            "w_up_sh": np.ascontiguousarray(wu[352 * c:352 * (c + 1)]),
            "w_down_sh": np.ascontiguousarray(wd[128 * c:128 * (c + 1)]),
            "decay_tab": dec[p], "mask_a": msk[p][0], "mask_b": msk[p][1],
            "ident": idm,
        })
    nc = _get_nc()
    res = run_bass_kernel_spmd(nc, in_maps, core_ids=list(range(8)))
    out = np.empty((B, T, C), dtype=np.float32)
    for c in range(8):
        b, p = c // 2, c % 2
        o = np.asarray(res.results[c]["out"]).reshape(OWN, 128, C)
        out.reshape(B, NT, 128, C)[b, p::2] = o
    return out


# version 28
# speedup vs baseline: 1.0445x; 1.0014x over previous
import contextlib
import numpy as np
import ml_dtypes

import concourse.bass as bass
import concourse.mybir as mybir
from concourse.bass_utils import run_bass_kernel_spmd

F32 = mybir.dt.float32
BF16 = mybir.dt.bfloat16
AF = mybir.ActivationFunctionType
ALU = mybir.AluOpType
AX = mybir.AxisListType

B, T, C = 4, 4096, 1024
NH, HD, FF = 16, 64, 2816
NT = T // 128
OWN = 16
LN6 = float(np.log(np.float64(6.0)))
EPS = 1e-5
_QT = [0.0625, 0.5625, 1.5625, 3.0625, 6.25, 12.25, 25.0]

DEBUG = False
PHLIM = 3
ATTLIM = 5
NQT = 4
NQC = 4
EXPON = True


def _decay_values():
    WM_DECAY_LENGTH, WM_ECHOIC = 2048, 16
    dl = WM_DECAY_LENGTH - WM_ECHOIC + 1
    nums = np.linspace(0.0, 1.0, dl, dtype=np.float64)
    dv = 1.0 - np.power(nums, 1.0 / np.e)
    dv = np.concatenate([np.ones(WM_ECHOIC - 1), dv]).astype(np.float32)
    out = np.zeros(T + 256, dtype=np.float32)
    out[:2048] = dv
    return out


def _decay_table(p):
    dv = _decay_values()
    r = np.arange(128)[:, None]
    c = np.arange(128)[None, :]
    tab = np.zeros((18, 128, 128), dtype=np.float32)
    for idx in range(18):
        d = idx - 1 + p
        dist = d * 128 + (c - r)
        tab[idx] = dv[np.clip(dist, 0, T - 1)] * (dist >= 0)
    tab = tab.transpose(1, 0, 2).reshape(128, 18 * 128)
    return np.ascontiguousarray(tab.astype(ml_dtypes.bfloat16))


def _masks(p):
    r = np.arange(128)[:, None]
    c = np.arange(128)[None, :]
    tri = (c >= r).astype(np.float32)
    if p == 0:
        ma, mb = tri, np.zeros((128, 128), np.float32)
    else:
        ma, mb = np.ones((128, 128), np.float32), tri
    return (np.ascontiguousarray(ma.astype(ml_dtypes.bfloat16)),
            np.ascontiguousarray(mb.astype(ml_dtypes.bfloat16)))


ENGS = ("tensor", "vector", "scalar", "gpsimd", "sync")


class Prog:

    def __init__(self, nc, ctx):
        self.ops = []
        self.ctr = {}
        self.sem = {}
        for e in ENGS:
            for k in (e, e + "_d"):
                self.sem[k] = ctx.enter_context(nc.semaphore("c_" + k))
                self.ctr[k] = 0

    @staticmethod
    def _flat(waits):
        out = []
        for w in waits:
            if w is None:
                continue
            if isinstance(w, list):
                out.extend(x for x in w if x is not None and x[1] > 0)
            elif w[1] > 0:
                out.append(w)
        return out

    def emit(self, eng, fn, waits=(), mark=False, dma=False):
        key = eng + "_d" if dma else eng
        inc = 16 if dma else (1 if mark else 0)
        if inc:
            self.ctr[key] += inc
        self.ops.append([eng, fn, self._flat(waits), inc, key])
        return (key, self.ctr[key])

    def tok(self, eng):
        return [(eng, self.ctr[eng]), (eng + "_d", self.ctr[eng + "_d"])]

    def finalize(self):
        waited = {}
        for op in self.ops:
            for (we, wv) in op[2]:
                if we.endswith("_d"):
                    waited.setdefault(we, set()).add(wv)
        run = {k: 0 for k in self.ctr}
        for op in self.ops:
            eng, fn, waits, inc, key = op
            if inc and key.endswith("_d"):
                pre = run[key]
                fences = [w for w in waited.get(key, ()) if pre >= w > pre - 16
                          or (pre >= w and any(True for _ in ()))]
                run[key] = pre + inc
        last_fenced = {k: 0 for k in self.ctr}
        run = {k: 0 for k in self.ctr}
        for op in self.ops:
            eng, fn, waits, inc, key = op
            if inc and key.endswith("_d"):
                pre = run[key]
                ws = sorted(w for w in waited.get(key, ())
                            if last_fenced[key] < w <= pre)
                if ws:
                    op[2] = [(key, ws[-1])] + op[2]
                    last_fenced[key] = ws[-1]
                run[key] = pre + inc

    def replay(self, name, e):
        for eng, fn, waits, inc, key in self.ops:
            if eng != name:
                continue
            for (we, wv) in waits:
                e.wait_ge(self.sem[we], wv)
            inst = fn(e)
            if inc:
                inst.then_inc(self.sem[key], inc)


def build_program(phlim=None):
    if phlim is None:
        phlim = PHLIM
    attlim = ATTLIM
    nqt, nqc, expon = NQT, NQC, EXPON
    nc = bass.Bass("TRN2", target_bir_lowering=False)
    okind = "ExternalOutput" if DEBUG else "Internal"

    d_x = nc.dram_tensor("x_full", [T, C], F32, kind="ExternalInput").ap()
    d_xo = nc.dram_tensor("x_own", [OWN * 128, C], F32, kind="ExternalInput").ap()
    d_wa = nc.dram_tensor("w_attn", [C, 3 * C], F32, kind="ExternalInput").ap()
    d_wp = nc.dram_tensor("w_proj", [C, C], F32, kind="ExternalInput").ap()
    d_wg = nc.dram_tensor("w_gate_sh", [FF // 8, C], F32,
                          kind="ExternalInput").ap()
    d_wu = nc.dram_tensor("w_up_sh", [FF // 8, C], F32,
                          kind="ExternalInput").ap()
    d_wd = nc.dram_tensor("w_down_sh", [C // 8, FF], F32,
                          kind="ExternalInput").ap()
    d_dec = nc.dram_tensor("decay_tab", [128, 18 * 128], BF16,
                           kind="ExternalInput").ap()
    d_ma = nc.dram_tensor("mask_a", [128, 128], BF16, kind="ExternalInput").ap()
    d_mb = nc.dram_tensor("mask_b", [128, 128], BF16, kind="ExternalInput").ap()
    d_id = nc.dram_tensor("ident", [128, 128], BF16, kind="ExternalInput").ap()
    d_out = nc.dram_tensor("out", [OWN * 128, C], F32, kind="ExternalOutput").ap()
    s_xb = nc.dram_tensor("s_xb", [T, C], BF16, kind="Internal").ap()
    s_xob = nc.dram_tensor("s_xob", [OWN * 128, C], BF16, kind="Internal").ap()
    s_wgq = nc.dram_tensor("s_wgq", [FF, C], BF16, kind="Internal").ap()
    s_wgi = nc.dram_tensor("s_wgi", [FF // 8, C], BF16, kind="Internal").ap()
    s_wui = nc.dram_tensor("s_wui", [FF // 8, C], BF16, kind="Internal").ap()
    s_wdi = nc.dram_tensor("s_wdi", [C // 8, FF], BF16, kind="Internal").ap()
    s_wuq = nc.dram_tensor("s_wuq", [FF, C], BF16, kind="Internal").ap()
    s_wdq = nc.dram_tensor("s_wdq", [C, FF], BF16, kind="Internal").ap()
    s_xq = nc.dram_tensor("s_xq", [OWN * 128, C], BF16, kind=okind).ap()
    s_x1 = nc.dram_tensor("s_x1", [OWN * 128, C], F32, kind=okind).ap()
    s_y = (nc.dram_tensor("s_y", [1024, 2048], BF16, kind="ExternalOutput").ap()
           if DEBUG else None)

    ctx = contextlib.ExitStack()
    en = ctx.enter_context
    P = Prog(nc, ctx)

    mska = en(nc.sbuf_tensor("mska", [128, 128], BF16))
    mskb = en(nc.sbuf_tensor("mskb", [128, 128], BF16))
    onesc = en(nc.sbuf_tensor("onesc", [128, 1], BF16))
    onesr = en(nc.sbuf_tensor("onesr", [128, 64], BF16))
    idn = en(nc.sbuf_tensor("idn", [128, 128], BF16))
    lns = en(nc.sbuf_tensor("lns", [128, 2, 10], F32))
    qw = en(nc.sbuf_tensor("qw", [128, 512], F32))
    qb = en(nc.sbuf_tensor("qb", [128, 11, 512], BF16))
    qm = en(nc.sbuf_tensor("qm", [128, 4, 32], F32))
    qout = en(nc.sbuf_tensor("qout", [128, 2, 512], BF16))
    qws0 = {"qm": qm, "qw": qw, "qb": qb, "prev": None}
    ps = en(nc.psum_tensor("ps", [128, 8, 512], F32))
    psx = ps[:, 6, :]
    pst = ps[:, 7, :].bitcast(BF16)

    

    def emit_quant_v(src_ap, dst_ap, fd, src_tok=None, dst_tok=None,
                     np_=128, ws=None):
        if ws is None:
            ws = qws0
        qm, qw, qb = ws["qm"], ws["qw"], ws["qb"]
        nb = fd // 16
        t0 = P.emit("vector", lambda e: e.tensor_reduce(
            qm[0:np_, 0, 0:nb], src_ap.rearrange("p (b s) -> p b s", s=16),
            axis=AX.X, op=ALU.max, apply_absolute_value=True),
            waits=[src_tok], mark=True)
        vt2 = P.emit("vector", lambda e: e.tensor_single_scalar(
            qm[0:np_, 3, 0:nb], qm[0:np_, 0, 0:nb], 1.0 / 12.0, ALU.mult),
            waits=[t0], mark=True)
        m2 = P.emit("vector", lambda e: e.tensor_tensor(
            qm[0:np_, 1, 0:nb], qm[0:np_, 0, 0:nb], qm[0:np_, 0, 0:nb],
            op=ALU.mult), waits=[t0], mark=True)
        x2 = P.emit("scalar", lambda e: e.square(
            qw[0:np_, 0:fd], src_ap), waits=[src_tok, ws.get("cl")],
            mark=True)
        m2in = qm[0:np_, 1, 0:nb].unsqueeze(2).broadcast_to([np_, nb, 16])
        qw3 = qw[0:np_, 0:fd].rearrange("p (b s) -> p b s", s=16)
        cl = None
        for i, thr in enumerate(_QT):
            cl = P.emit("vector", lambda e, i=i, k=thr / 36.0: e.scalar_tensor_tensor(
                qb[0:np_, i, 0:fd].rearrange("p (b s) -> p b s", s=16),
                m2in, k, qw3, ALU.mult, ALU.is_lt),
                waits=[m2, x2] if i == 0 else (), mark=(i == 6))
        cm = P.emit("vector", lambda e: e.tensor_tensor(
            qb[0:np_, 7, 0:fd], qb[0:np_, 0, 0:fd],
            qb[0:np_, 1, 0:fd], op=ALU.add), waits=[cl], mark=True)
        a2 = P.emit("vector", lambda e: e.tensor_tensor(
            qb[0:np_, 8, 0:fd], qb[0:np_, 2, 0:fd], qb[0:np_, 3, 0:fd],
            op=ALU.add), waits=[cl], mark=True)
        a3 = P.emit("vector", lambda e: e.tensor_tensor(
            qb[0:np_, 9, 0:fd], qb[0:np_, 7, 0:fd], qb[0:np_, 8, 0:fd],
            op=ALU.add), waits=[a2], mark=True)
        a4 = P.emit("vector", lambda e: e.scalar_tensor_tensor(
            qb[0:np_, 0, 0:fd], qb[0:np_, 6, 0:fd], 2.0, qb[0:np_, 5, 0:fd],
            ALU.mult, ALU.add), waits=[cm], mark=True)
        a5 = P.emit("vector", lambda e: e.scalar_tensor_tensor(
            qb[0:np_, 1, 0:fd], qb[0:np_, 4, 0:fd], 2.0, qb[0:np_, 9, 0:fd],
            ALU.mult, ALU.add), waits=[a3], mark=True)
        a6 = P.emit("vector", lambda e: e.scalar_tensor_tensor(
            qb[0:np_, 2, 0:fd], qb[0:np_, 0, 0:fd], 2.0, qb[0:np_, 1, 0:fd],
            ALU.mult, ALU.add), waits=[a5], mark=True)
        sgn = P.emit("scalar", lambda e: e.activation(
            qb[0:np_, 10, 0:fd], src_ap, AF.Sign),
            waits=[src_tok, ws.get("u1")], mark=True)
        u1 = P.emit("vector", lambda e: e.tensor_tensor(
            qb[0:np_, 5, 0:fd], qb[0:np_, 2, 0:fd], qb[0:np_, 10, 0:fd],
            op=ALU.mult), waits=[a6, sgn], mark=True)
        vq = P.emit("vector", lambda e, d=dst_ap: e.tensor_tensor(
            d.rearrange("p (b s) -> p b s", s=16),
            qb[0:np_, 5, 0:fd].rearrange("p (b s) -> p b s", s=16),
            qm[0:np_, 3, 0:nb].unsqueeze(2).broadcast_to([np_, nb, 16]),
            op=ALU.mult), waits=[vt2, u1, dst_tok], mark=True)
        ws.update(cl=cl, u1=u1)
        ws["prev"] = vq
        return vq

    lnst = {"l3": None, "sq": [None, None]}

    def emit_ln_head(rsrc, rt, sl):
        l1 = P.emit("vector", lambda e: e.tensor_reduce(
            lns[:, sl, 0:1], rsrc, axis=AX.X, op=ALU.add), waits=[rt],
            mark=True)
        l2 = P.emit("vector", lambda e: e.tensor_single_scalar(
            lns[:, sl, 1:2], lns[:, sl, 0:1], 1.0 / 1024.0, ALU.mult),
            waits=[l1], mark=True)
        sq = P.emit("scalar", lambda e: e.square(lnw[:], rsrc),
                    waits=[rt, lnst["l3"]], mark=True)
        lnst["sq"][sl] = sq
        l3 = P.emit("vector", lambda e: e.tensor_reduce(
            lns[:, sl, 2:3], lnw[:], axis=AX.X, op=ALU.add), waits=[sq],
            mark=True)
        lnst["l3"] = l3
        l4 = P.emit("vector", lambda e: e.tensor_single_scalar(
            lns[:, sl, 3:4], lns[:, sl, 2:3], 1.0 / 1024.0, ALU.mult),
            waits=[l3], mark=True)
        l5 = P.emit("vector", lambda e: e.tensor_tensor(
            lns[:, sl, 4:5], lns[:, sl, 1:2], lns[:, sl, 1:2], op=ALU.mult),
            waits=[l2], mark=True)
        l6 = P.emit("vector", lambda e: e.tensor_tensor(
            lns[:, sl, 5:6], lns[:, sl, 3:4], lns[:, sl, 4:5],
            op=ALU.subtract), waits=[l4, l5], mark=True)
        l7 = P.emit("vector", lambda e: e.tensor_single_scalar(
            lns[:, sl, 6:7], lns[:, sl, 5:6], EPS, ALU.add), waits=[l6],
            mark=True)
        l8 = P.emit("vector", lambda e: e.tensor_single_scalar(
            lns[:, sl, 7:8], lns[:, sl, 1:2], -1.0, ALU.mult), waits=[l2],
            mark=True)
        l9 = P.emit("scalar", lambda e: e.activation(
            lns[:, sl, 8:9], lns[:, sl, 6:7], AF.Ln), waits=[l7], mark=True)
        l10 = P.emit("scalar", lambda e: e.activation(
            lns[:, sl, 9:10], lns[:, sl, 8:9], AF.Exp, scale=-0.5),
            waits=[l9], mark=True)
        return (rsrc, sl, l10, l8)

    def emit_ln_tail(head, dst_ap, dst_tok=None):
        rsrc, sl, l10, l8 = head
        return P.emit("vector", lambda e, d=dst_ap: e.tensor_scalar(
            d, rsrc, lns[:, sl, 7:8], lns[:, sl, 9:10], ALU.add, ALU.mult),
            waits=[l10, l8, dst_tok], mark=True)

    def emit_ln(rsrc, dst_ap, rt, dst_tok=None, sl=0):
        return emit_ln_tail(emit_ln_head(rsrc, rt, sl), dst_ap, dst_tok)

    P.emit("sync", lambda e: e.dma_start(out=mska[:], in_=d_ma[:, :]), dma=True)
    P.emit("sync", lambda e: e.dma_start(out=mskb[:], in_=d_mb[:, :]), dma=True)
    P.emit("sync", lambda e: e.dma_start(out=idn[:], in_=d_id[:, :]), dma=True)
    P.emit("gpsimd", lambda e: e.memset(onesc[:], 1.0), mark=True)
    P.emit("gpsimd", lambda e: e.memset(onesr[:], 1.0), mark=True)

    with nc.sbuf_tensor("xstg", [128, 2, 4096], BF16) as xstg:
        spills = []
        njob = (NT + OWN) // 4
        for t in range(njob):
            stg = xstg[:, t % 2, :]
            if t < NT // 4:
                srcb = d_x[512 * t:512 * (t + 1), :]
                dstb = s_xb[512 * t:512 * (t + 1), :]
            else:
                tt = t - NT // 4
                srcb = d_xo[512 * tt:512 * (tt + 1), :]
                dstb = s_xob[512 * tt:512 * (tt + 1), :]
            w = [spills[t - 2]] if t >= 2 else []
            P.emit("gpsimd", lambda e, stg=stg, srcb=srcb: e.dma_start(
                out=stg, in_=srcb), waits=w, dma=True)
            ld = P.tok("gpsimd")
            P.emit("sync", lambda e, stg=stg, dstb=dstb: e.dma_start(
                out=dstb, in_=stg), waits=[ld], dma=True)
            spills.append(P.tok("sync"))
        spill_all = P.tok("sync")

    with contextlib.ExitStack() as yctx:
        y16 = yctx.enter_context(nc.sbuf_tensor("y16", [128, 8, 2048], BF16))

        with contextlib.ExitStack() as actx:
            aen = actx.enter_context
            xT = aen(nc.sbuf_tensor("xT", [128, 8, T], BF16))
            xoT = aen(nc.sbuf_tensor("xoT", [128, 8, 2048], BF16))
            waq = aen(nc.sbuf_tensor("waq", [128, 8, 128], BF16))
            wak = aen(nc.sbuf_tensor("wak", [128, 8, 128], BF16))
            wav = aen(nc.sbuf_tensor("wav", [128, 8, 256], BF16))
            v16 = aen(nc.sbuf_tensor("v16", [128, 32, 256], BF16))
            k16 = aen(nc.sbuf_tensor("k16", [128, T], BF16))
            q16 = aen(nc.sbuf_tensor("q16", [128, 2048], BF16))
            dect = aen(nc.sbuf_tensor("dect", [128, 18 * 128], BF16))
            pAB = aen(nc.sbuf_tensor("pAB", [128, 3, 2, 512], BF16))
            pA = pAB[:, :, 0, :]
            pB = pAB[:, :, 1, :]
            pmA = aen(nc.sbuf_tensor("pmA", [128, 3, 128], BF16))
            pmB = aen(nc.sbuf_tensor("pmB", [128, 3, 128], BF16))
            pdA = aen(nc.sbuf_tensor("pdA", [128, 3, 512], BF16))
            pdB = aen(nc.sbuf_tensor("pdB", [128, 3, 512], BF16))
            rz = aen(nc.sbuf_tensor("rz", [128, 512], F32, side="right"))
            rz16 = aen(nc.sbuf_tensor("rz16", [128, 512], BF16, side="right"))
            rzb = aen(nc.sbuf_tensor("rzb", [128, 512], BF16, side="right"))
            wf32 = aen(nc.sbuf_tensor("wf32", [128, 2, 512], F32))

            wjobs = []
            for r0, r1 in ((0, 128), (128, 256), (256, 352)):
                for f0 in (0, 512):
                    wjobs.append((d_wg, s_wgi, r0, r1, f0, f0 + 512))
            for r0, r1 in ((0, 128), (128, 256), (256, 352)):
                for f0 in (0, 512):
                    wjobs.append((d_wu, s_wui, r0, r1, f0, f0 + 512))
            for ci in range(6):
                f0, f1 = 512 * ci, min(FF, 512 * ci + 512)
                wjobs.append((d_wd, s_wdi, 0, 128, f0, f1))
            wlds = []
            wsps = [None] * len(wjobs)
            wvqs = [None] * len(wjobs)

            def emit_wload(j):
                dsrc, ddst, r0, r1, f0, f1 = wjobs[j]
                np_, fd = r1 - r0, f1 - f0
                P.emit("scalar", lambda e, dsrc=dsrc, r0=r0, r1=r1, f0=f0, f1=f1, fd=fd, np_=np_, sl=j % 2: e.dma_start(
                    out=wf32[0:np_, sl, 0:fd], in_=dsrc[r0:r1, f0:f1]),
                    waits=[wvqs[j - 2]] if j >= 2 else (), dma=True)
                wlds.append(P.tok("scalar"))

            emit_wload(0)
            emit_wload(1)
            for wj, (dsrc, ddst, r0, r1, f0, f1) in enumerate(wjobs):
                np_, fd = r1 - r0, f1 - f0
                vq = emit_quant_v(wf32[0:np_, wj % 2, 0:fd],
                                  qout[0:np_, wj % 2, 0:fd], fd,
                                  src_tok=wlds[wj],
                                  dst_tok=wsps[wj - 2] if wj >= 2 else None,
                                  np_=np_)
                wvqs[wj] = vq
                P.emit("scalar", lambda e, ddst=ddst, r0=r0, r1=r1, f0=f0, f1=f1, fd=fd, np_=np_, sl=wj % 2: e.dma_start(
                    out=ddst[r0:r1, f0:f1], in_=qout[0:np_, sl, 0:fd]),
                    waits=[vq], dma=True)
                wsps[wj] = P.tok("scalar")
                if wj + 2 < len(wjobs):
                    emit_wload(wj + 2)
            wq_spill = P.tok("scalar")
            ccd = {}

            def emit_cc():
                for (ci, co) in ((s_wgi, s_wgq), (s_wui, s_wuq),
                                 (s_wdi, s_wdq)):
                    P.emit("gpsimd",
                           lambda e, ci=ci, co=co: e.collective_compute(
                               "AllGather", ALU.bypass, ins=[ci[:, :]],
                               outs=[co[:, :]],
                               replica_groups=[[0, 1, 2, 3, 4, 5, 6, 7]]),
                           waits=[wq_spill], mark=True)
                ccd["tok"] = P.tok("gpsimd")

            P.emit("sync", lambda e: e.dma_start(out=dect[:], in_=d_dec[:, :]),
                   dma=True)
            for cc in range(8):
                P.emit("sync", lambda e, cc=cc: e.dma_start(
                    out=xT[:, cc, :], in_=s_xb[:, 128 * cc:128 * (cc + 1)],
                    transpose=True), waits=[spill_all], dma=True)
            for cc in range(8):
                P.emit("sync", lambda e, cc=cc: e.dma_start(
                    out=xoT[:, cc, :], in_=s_xob[:, 128 * cc:128 * (cc + 1)],
                    transpose=True), dma=True)
            xt_tok = P.tok("sync")


            ROT = ((0, 1), (2, 3), (6, 7))
            PROT = (0, 1, 6)
            pend = [None]

            last_pair_scal = None
            last_pair_expf = None

            def emit_wa(qt, pr, prod_tok):
                co = 256 * qt + 128 * pr
                toks = {}
                if pr == 0:
                    for cc in range(8):
                        P.emit("gpsimd", lambda e, cc=cc, qt=qt: e.dma_start(
                            out=wav[:, cc, :],
                            in_=d_wa[128 * cc:128 * (cc + 1),
                                     2048 + 256 * qt:2048 + 256 * (qt + 1)]),
                            waits=[prod_tok] if cc == 0 else (), dma=True)
                    toks["v"] = P.tok("gpsimd")
                for cc in range(8):
                    P.emit("gpsimd", lambda e, cc=cc, co=co: e.dma_start(
                        out=wak[:, cc, :],
                        in_=d_wa[128 * cc:128 * (cc + 1),
                                 1024 + co:1024 + co + 128]),
                        waits=[prod_tok] if (pr == 1 and cc == 0) else (),
                        dma=True)
                toks["k"] = P.tok("gpsimd")
                for cc in range(8):
                    P.emit("gpsimd", lambda e, cc=cc, co=co: e.dma_start(
                        out=waq[:, cc, :],
                        in_=d_wa[128 * cc:128 * (cc + 1), co:co + 128]),
                        dma=True)
                toks["q"] = P.tok("gpsimd")
                return toks

            wa_next = emit_wa(0, 0, spill_all)
            for qt in range(nqt if phlim >= 1 else 0):
                for pr in range(2):
                    gp = 2 * qt + pr
                    wa_cur = wa_next
                    if qt == nqt - 1 and pr == 1:
                        emit_cc()

                    evac_toks = [None, None, None]
                    jobs = []
                    if pr == 0:
                        for t in range(NT):
                            jobs.append(("v", t))
                    jobs += [("k", ch) for ch in range(8)]
                    jobs += [("q", ch) for ch in range(4)]
                    seen_kind = set()
                    for ji, (kind, idx) in enumerate(jobs):
                        bk = PROT[ji % 3]
                        bw = [evac_toks[ji % 3]]
                        if kind not in seen_kind:
                            seen_kind.add(kind)
                            bw += [wa_cur[kind]]
                        if ji == 0:
                            bw += [xt_tok, last_pair_expf]
                        if ji == 2 and pend[0] is not None:
                            sc, _vec = pend[0]()
                            pend[0] = None
                            bw += [sc]
                        for cc in range(8):
                            if kind == "v":
                                P.emit("tensor", lambda e, t=idx, cc=cc, bk=bk: e.matmul(
                                    ps[:, bk, 0:256],
                                    xT[:, cc, 128 * t:128 * (t + 1)],
                                    wav[:, cc, :], start=(cc == 0),
                                    stop=(cc == 7)),
                                    waits=bw if cc == 0 else (),
                                    mark=(cc == 7))
                            elif kind == "k":
                                P.emit("tensor", lambda e, ch=idx, cc=cc, bk=bk: e.matmul(
                                    ps[:, bk, :], wak[:, cc, :],
                                    xT[:, cc, 512 * ch:512 * (ch + 1)],
                                    start=(cc == 0), stop=(cc == 7)),
                                    waits=bw if cc == 0 else (),
                                    mark=(cc == 7))
                            else:
                                P.emit("tensor", lambda e, ch=idx, cc=cc, bk=bk: e.matmul(
                                    ps[:, bk, :], waq[:, cc, :],
                                    xoT[:, cc, 512 * ch:512 * (ch + 1)],
                                    start=(cc == 0), stop=(cc == 7)),
                                    waits=bw if cc == 0 else (),
                                    mark=(cc == 7))
                        mm = P.tok("tensor")
                        if kind == "v":
                            tk = P.emit("vector", lambda e, t=idx, bk=bk: e.tensor_copy(
                                v16[:, t, :], ps[:, bk, 0:256]), waits=[mm],
                                mark=True)
                        elif kind == "k":
                            tk = P.emit("vector", lambda e, ch=idx, bk=bk: e.tensor_copy(
                                k16[:, 512 * ch:512 * (ch + 1)], ps[:, bk, :]),
                                waits=[mm], mark=True)
                        else:
                            tk = P.emit("vector", lambda e, ch=idx, bk=bk: e.tensor_copy(
                                q16[:, 512 * ch:512 * (ch + 1)], ps[:, bk, :]),
                                waits=[mm], mark=True)
                        evac_toks[ji % 3] = tk
                    kq_tok = P.tok("vector")
                    if not (qt == nqt - 1 and pr == 1):
                        nq2, np2 = (qt, 1) if pr == 0 else (qt + 1, 0)
                        wa_next = emit_wa(nq2, np2, P.tok("tensor"))
                    prev_qc_scal = None
                    prev_qc_vec = None
                    prev_qc_expf = None

                    for qc in range(nqc if attlim >= 2 else 0):
                        nkt = 8 * qc + 8
                        z_started = [False, False]
                        pv_started = [False, False]
                        exp_toks = {}
                        vmarks = {}
                        tmarks = {}
                        kt_geom = {}

                        def geom(kt, qc=qc):
                            jlo = min(3, max(0, -(-(kt - 1 - 8 * qc) // 2)))
                            d0 = kt - 8 * qc
                            jm = None
                            msk = None
                            if 0 <= d0 <= 7:
                                jm = d0 // 2 if d0 % 2 == 0 else (d0 - 1) // 2
                                msk = mska if d0 % 2 == 0 else mskb
                            jwlo = jlo
                            jwhi_t = min(3, (kt + 16 - 8 * qc) // 2)
                            jwhi = 3 if kt == 0 else jwhi_t
                            if attlim < 3:
                                jm, msk, jwhi_t, jwhi = None, None, jwlo - 1, jwlo - 1
                            return jlo, jm, msk, jwlo, jwhi, jwhi_t

                        def emit_st(kt, qc=qc):
                            jlo = geom(kt)[0]
                            w0 = 128 * jlo
                            ba, bb = ROT[kt % 3]
                            w = [exp_toks.get(kt - 3), kq_tok]
                            if kt < 2:
                                w += [prev_qc_expf]
                            elif kt == 2:
                                w += [prev_qc_expf, prev_qc_scal]
                            for h, bx in ((0, ba), (1, bb)):
                                P.emit("tensor",
                                       lambda e, h=h, kt=kt, w0=w0, bx=bx, qc=qc: e.matmul(
                                           ps[:, bx, w0:],
                                           k16[64 * h:64 * (h + 1),
                                               128 * kt:128 * (kt + 1)],
                                           q16[64 * h:64 * (h + 1),
                                               512 * qc + w0:512 * (qc + 1)],
                                           start=True, stop=True,
                                           tile_position=(64 * h, 0)),
                                       waits=w if h == 0 else (),
                                       mark=(h == 1))
                            return P.tok("tensor")

                        def emit_exp(kt, st_tok):
                            jlo = geom(kt)[0]
                            w0 = 128 * jlo
                            ba, bb = ROT[kt % 3]
                            sl = kt % 3
                            w = [st_tok, vmarks.get(kt - 3), tmarks.get(kt - 3)]
                            tok = P.emit("scalar",
                                         lambda e, w0=w0, ba=ba, sl=sl: e.activation(
                                             pAB[:, sl, :, w0:],
                                             ps[:, ba:ba + 2, w0:],
                                             AF.Exp, scale=0.125),
                                         waits=w, mark=True)
                            exp_toks[kt] = tok
                            return tok

                        def emit_vops(kt):
                            jlo, jm, msk, jwlo, jwhi, jwhi_t = geom(kt)
                            sl = kt % 3
                            v_last = None
                            first = [exp_toks[kt]]
                            if jm is not None:
                                P.emit("vector",
                                       lambda e, sl=sl, jm=jm, m=msk: e.tensor_tensor(
                                           pmA[:, sl, :],
                                           pA[:, sl, 128 * jm:128 * (jm + 1)],
                                           m[:], op=ALU.mult), waits=first)
                                first = []
                                v_last = P.emit("vector",
                                                lambda e, sl=sl, jm=jm, m=msk: e.tensor_tensor(
                                                    pmB[:, sl, :],
                                                    pB[:, sl, 128 * jm:128 * (jm + 1)],
                                                    m[:], op=ALU.mult), mark=True)
                            if kt == 0 and jwhi_t < 3:
                                P.emit("vector", lambda e, sl=sl, j1=jwhi_t + 1: e.memset(
                                    pdA[:, sl, 128 * j1:], 0.0), waits=first)
                                first = []
                                v_last = P.emit("vector",
                                                lambda e, sl=sl, j1=jwhi_t + 1: e.memset(
                                                    pdB[:, sl, 128 * j1:], 0.0),
                                                mark=True)
                            if jwhi_t >= jwlo:
                                i0 = 2 * (4 * qc + jwlo) - kt + 1
                                nj = jwhi_t - jwlo + 1
                                dsl = dect[:].rearrange(
                                    "p (i q) -> p i q", q=128)[
                                    :, i0:i0 + 2 * nj - 1:2, :]
                                P.emit("vector",
                                       lambda e, sl=sl, jwlo=jwlo, nj=nj, dsl=dsl: e.tensor_tensor(
                                           pdA[:, sl, 128 * jwlo:128 * (jwlo + nj)]
                                           .rearrange("p (j q) -> p j q", q=128),
                                           pA[:, sl, 128 * jwlo:128 * (jwlo + nj)]
                                           .rearrange("p (j q) -> p j q", q=128),
                                           dsl, op=ALU.mult), waits=first)
                                first = []
                                v_last = P.emit("vector",
                                                lambda e, sl=sl, jwlo=jwlo, nj=nj, dsl=dsl: e.tensor_tensor(
                                                    pdB[:, sl, 128 * jwlo:128 * (jwlo + nj)]
                                                    .rearrange("p (j q) -> p j q", q=128),
                                                    pB[:, sl, 128 * jwlo:128 * (jwlo + nj)]
                                                    .rearrange("p (j q) -> p j q", q=128),
                                                    dsl, op=ALU.mult), mark=True)
                            vmarks[kt] = v_last
                            return v_last

                        def emit_zpv(kt, qc=qc):
                            jlo, jm, msk, jwlo, jwhi, jwhi_t = geom(kt)
                            sl = kt % 3
                            zlo = (jm + 1) if jm is not None else jlo
                            if attlim < 3:
                                zlo = 4
                            last_kt = (kt == nkt - 1)
                            tops = []
                            for h, praw, pmsk in ((0, pA, pmA), (1, pB, pmB)):
                                zops = []
                                if zlo <= 3:
                                    zops.append(("raw", praw, zlo))
                                if jm is not None:
                                    zops.append(("msk", pmsk, jm))
                                for oi, (knd, buf, j0) in enumerate(zops):
                                    st = not z_started[h]
                                    z_started[h] = True
                                    sp = last_kt and (oi == len(zops) - 1)
                                    if knd == "raw":
                                        tops.append(lambda e, h=h, sl=sl, j0=j0, buf=buf, st=st, sp=sp: e.matmul(
                                            ps[64 * h:64 * h + 1, 5, 128 * j0:],
                                            onesc[:], buf[:, sl, 128 * j0:],
                                            start=st, stop=sp,
                                            skip_group_check=True,
                                            tile_position=(0, 64 * h)))
                                    else:
                                        tops.append(lambda e, h=h, sl=sl, j0=j0, buf=buf, st=st, sp=sp: e.matmul(
                                            ps[64 * h:64 * h + 1, 5,
                                               128 * j0:128 * (j0 + 1)],
                                            onesc[:], buf[:, sl, :],
                                            start=st, stop=sp,
                                            skip_group_check=True,
                                            tile_position=(0, 64 * h)))
                            if attlim >= 4 and jwhi >= jwlo:
                                for h, pdec in ((0, pdA), (1, pdB)):
                                    st = not pv_started[h]
                                    pv_started[h] = True
                                    tops.append(lambda e, h=h, sl=sl, kt=kt, pr=pr, jwlo=jwlo, jwhi=jwhi, pdec=pdec, st=st, sp=last_kt: e.matmul(
                                        ps[64 * h:64 * (h + 1), 4,
                                           128 * jwlo:128 * (jwhi + 1)],
                                        v16[:, kt, 128 * pr + 64 * h:
                                            128 * pr + 64 * (h + 1)],
                                        pdec[:, sl, 128 * jwlo:128 * (jwhi + 1)],
                                        start=st, stop=sp,
                                        skip_group_check=True,
                                        tile_position=(0, 64 * h)))
                            w0l = [vmarks.get(kt) or exp_toks[kt]]
                            if kt == 0:
                                w0l += [prev_qc_scal, prev_qc_vec]
                            tk = None
                            for oi, tf in enumerate(tops):
                                tk = P.emit("tensor", tf,
                                            waits=w0l if oi == 0 else (),
                                            mark=(oi == len(tops) - 1))
                            tmarks[kt] = tk
                            return tk

                        for kt in range(nkt):
                            if kt == 2 and pend[0] is not None:
                                prev_qc_scal, prev_qc_vec = pend[0]()
                                pend[0] = None
                            st_tok = emit_st(kt)
                            emit_exp(kt, st_tok)
                            emit_vops(kt)
                            if kt >= 2:
                                emit_zpv(kt - 2)
                        emit_zpv(nkt - 2)
                        emit_zpv(nkt - 1)
                        zpv_tok = P.tok("tensor")
                        prev_qc_expf = exp_toks[nkt - 1]
                        if attlim < 5:
                            continue
                        P.emit("scalar", lambda e: e.activation(
                            rz[0:1, :], ps[0:1, 5, :], AF.Ln),
                            waits=[zpv_tok])
                        lt = P.emit("scalar", lambda e: e.activation(
                            rz[64:65, :], ps[64:65, 5, :], AF.Ln), mark=True)
                        P.emit("scalar", lambda e: e.activation(
                            rz16[0:1, :], rz[0:1, :], AF.Exp, scale=-1.0),
                            waits=[lt])
                        se = P.emit("scalar", lambda e: e.activation(
                            rz16[64:65, :], rz[64:65, :], AF.Exp, scale=-1.0),
                            mark=True)

                        def mk_norm(se=se, gp=gp, qc=qc,
                                    expf=exp_toks[nkt - 1]):
                            def go():
                                for h in range(2):
                                    P.emit("tensor", lambda e, h=h: e.matmul(
                                        psx[64 * h:64 * (h + 1), :],
                                        onesr[64 * h:64 * h + 1, :],
                                        rz16[64 * h:64 * h + 1, :],
                                        start=True, stop=True,
                                        skip_group_check=True,
                                        tile_position=(64 * h, 64 * h)),
                                        waits=([se, expf] if h == 0 else ()),
                                        mark=(h == 1))
                                bc = P.tok("tensor")
                                sbc = P.emit("scalar", lambda e: e.copy(
                                    rzb[:], psx[:]), waits=[bc], mark=True)
                                vec = P.emit(
                                    "vector",
                                    lambda e, gp=gp, qc=qc: e.tensor_tensor(
                                        y16[:, gp, 512 * qc:512 * (qc + 1)],
                                        ps[:, 4, :], rzb[:], op=ALU.mult),
                                    waits=[sbc], mark=True)
                                return sbc, vec
                            return go
                        pend[0] = mk_norm()
                        last_pair_expf = prev_qc_expf

            if pend[0] is not None:
                pend[0]()
                pend[0] = None
            if "tok" not in ccd:
                emit_cc()
            cc_tok = ccd["tok"]
            att_done = {e: P.tok(e) for e in ENGS}
        xjit = en(nc.sbuf_tensor("xjit", [128, 2, 1024], F32, side="right"))
        r1 = en(nc.sbuf_tensor("r1", [128, 2, 1024], F32, side="right"))
        x1t = en(nc.sbuf_tensor("x1t", [128, 1024], F32, side="right"))
        outt = en(nc.sbuf_tensor("outt", [128, 1024], F32, side="right"))
        lnw = en(nc.sbuf_tensor("lnw", [128, 1024], F32, side="right"))
        qout2 = en(nc.sbuf_tensor("qout2", [128, 2, 512], BF16, side="right"))

        wp16 = yctx.enter_context(nc.sbuf_tensor("wp16", [128, 8, 1024], BF16))
        for cc in range(8):
            P.emit("gpsimd", lambda e, cc=cc: e.dma_start(
                out=wp16[:, cc, :], in_=d_wp[128 * cc:128 * (cc + 1), :]),
                waits=([att_done["tensor"]] if cc == 0 else ()), dma=True)
        wp_tok = P.tok("gpsimd")
        r1_toks = [None] * OWN
        x1_spills = []
        xq_spills = []
        xq_sp_prev = [None, None]
        out_toks = [None] * OWN
        pend_ln = [None]

        def fin_p2(head, tp):
            x1tok = emit_ln_tail(
                head, x1t[:],
                dst_tok=x1_spills[tp - 1] if tp >= 1 else None)
            P.emit("gpsimd", lambda e, tp=tp: e.dma_start(
                out=s_x1[128 * tp:128 * (tp + 1), :], in_=x1t[:]),
                waits=[x1tok], dma=True)
            x1_spills.append(P.tok("gpsimd"))
            for f0 in (0, 512):
                vq = emit_quant_v(x1t[:, f0:f0 + 512], qout2[:, f0 // 512, :],
                                  512, src_tok=x1tok,
                                  dst_tok=xq_sp_prev[f0 // 512])
                P.emit("gpsimd", lambda e, tp=tp, f0=f0: e.dma_start(
                    out=s_xq[128 * tp:128 * (tp + 1), f0:f0 + 512],
                    in_=qout2[:, f0 // 512, :]), waits=[vq], dma=True)
                xq_sp_prev[f0 // 512] = P.tok("gpsimd")
            xq_spills.append(P.tok("gpsimd"))
        for t in range(OWN if phlim >= 2 else 0):
            P.emit("sync", lambda e, t=t: e.dma_start(
                out=xjit[:, t % 2, :], in_=d_xo[128 * t:128 * (t + 1), :]),
                waits=[r1_toks[t - 2] if t >= 2 else None,
                       att_done["vector"] if t < 2 else None], dma=True)
            xj = P.tok("sync")
            for half in range(2):
                for cc in range(8):
                    P.emit("tensor", lambda e, t=t, half=half, cc=cc: e.matmul(
                        ps[:, half, :], y16[:, cc, 128 * t:128 * (t + 1)],
                        wp16[:, cc, 512 * half:512 * (half + 1)],
                        start=(cc == 0), stop=(cc == 7)),
                        waits=([att_done["scalar"], att_done["vector"],
                                att_done["gpsimd"], wp_tok,
                                r1_toks[t - 1] if t >= 1 else None]
                               if (half == 0 and cc == 0) else ()),
                        mark=(half == 1 and cc == 7))
            pj = P.tok("tensor")
            rt = P.emit("vector", lambda e, t=t: e.tensor_tensor(
                r1[:, t % 2, :], ps[:, 0:2, :].rearrange("p a b -> p (a b)"),
                xjit[:, t % 2, :], op=ALU.add),
                waits=[pj, xj, lnst["sq"][t % 2]], mark=True)
            r1_toks[t] = rt
            head = emit_ln_head(r1[:, t % 2, :], rt, t % 2)
            if pend_ln[0] is not None:
                fin_p2(*pend_ln[0])
            pend_ln[0] = (head, t)
        if pend_ln[0] is not None:
            fin_p2(*pend_ln[0])
            pend_ln[0] = None
        if DEBUG:
            for gpd in range(8):
                P.emit("sync", lambda e, gpd=gpd: e.dma_start(
                    out=s_y[128 * gpd:128 * (gpd + 1), :],
                    in_=y16[:, gpd, :]), waits=[att_done["vector"]], dma=True)
        p2_done = {e: P.tok(e) for e in ENGS}

    with contextlib.ExitStack() as fctx:
        fen = fctx.enter_context
        wgT = fen(nc.sbuf_tensor("wgT", [128, 8, FF], BF16))
        wuT = fen(nc.sbuf_tensor("wuT", [128, 8, FF], BF16))
        wdT = fen(nc.sbuf_tensor("wdT", [128, 22, 1024], BF16))
        xqTt = fen(nc.sbuf_tensor("xqTt", [128, 2, 8, 128], BF16))
        hq = fen(nc.sbuf_tensor("hq", [128, FF], BF16))
        hqT = fen(nc.sbuf_tensor("hqT", [128, 22, 128], BF16))
        hbuf = fen(nc.sbuf_tensor("hbuf", [128, 2, 512], F32))
        sgl = fen(nc.sbuf_tensor("sgl", [128, 2, 512], F32))


        tlw = [p2_done["tensor"], cc_tok]
        for cc in range(8):
            P.emit("sync", lambda e, cc=cc: e.dma_start(
                out=wgT[:, cc, :], in_=s_wgq[:, 128 * cc:128 * (cc + 1)],
                transpose=True), waits=tlw if cc == 0 else (), dma=True)
        for cc in range(8):
            P.emit("sync", lambda e, cc=cc: e.dma_start(
                out=wuT[:, cc, :], in_=s_wuq[:, 128 * cc:128 * (cc + 1)],
                transpose=True), dma=True)
        wgu_tok = P.tok("sync")
        for ic in range(22):
            P.emit("sync", lambda e, ic=ic: e.dma_start(
                out=wdT[:, ic, :], in_=s_wdq[:, 128 * ic:128 * (ic + 1)],
                transpose=True), dma=True)
        wT_tok = P.tok("sync")

        ICH = [(0, 512), (512, 1024), (1024, 1536), (1536, 2048),
               (2048, 2560), (2560, FF)]
        frt = [None] * OWN

        def fin_ffn(head, tp):
            ot = emit_ln_tail(head, outt[:],
                              dst_tok=out_toks[tp - 1] if tp >= 1 else None)
            P.emit("sync", lambda e, tp=tp: e.dma_start(
                out=d_out[128 * tp:128 * (tp + 1), :], in_=outt[:]),
                waits=[ot], dma=True)
            out_toks[tp] = P.tok("sync")
        xqT_toks = [None] * OWN
        dn_toks = [None] * OWN
        last_silu = [None, None]
        last_h = [None, None]
        prev_cp = None
        prev_trts = [None, None, None]
        for t in range(OWN if phlim >= 3 else 0):
            for cc in range(8):
                P.emit("sync", lambda e, t=t, cc=cc: e.dma_start(
                    out=xqTt[:, t % 2, cc, :],
                    in_=s_xq[128 * t:128 * (t + 1), 128 * cc:128 * (cc + 1)],
                    transpose=True),
                    waits=([xq_spills[t], dn_toks[t - 2] if t >= 2 else None]
                           if cc == 0 else ()), dma=True)
            xqT_toks[t] = P.tok("sync")
            vq_toks = []
            dn_half = [None, None]

            def emit_tgrp(grp, t=t, vq_toks=vq_toks, dn_half=dn_half):
                nonlocal prev_cp
                vq_gate = vq_toks[2 * grp + 1]
                ics = list(range(8 * grp, min(22, 8 * grp + 8)))
                for ii, ic in enumerate(ics):
                    P.emit("tensor", lambda e, ic=ic, ii=ii: e.transpose(
                        pst[:, 128 * ii:128 * (ii + 1)],
                        hq[:, 128 * ic:128 * (ic + 1)], idn[:]),
                        waits=([prev_cp, vq_gate] if ii == 0 else ()),
                        mark=(ii == len(ics) - 1))
                trt = P.tok("tensor")
                prev_trts[grp] = trt
                prev_cp = P.emit(
                    "scalar", lambda e, grp=grp, n=len(ics): e.copy(
                        hqT[:, 8 * grp:8 * grp + n, :],
                        pst[:, 0:128 * n].rearrange("p (i q) -> p i q", q=128)),
                    waits=[trt], mark=True)
                for half in range(2):
                    for ic in ics:
                        P.emit("tensor", lambda e, ic=ic, half=half: e.matmul(
                            ps[:, 4 + half, :], hqT[:, ic, :],
                            wdT[:, ic, 512 * half:512 * (half + 1)],
                            start=(ic == 0), stop=(ic == 21)),
                            waits=([prev_cp,
                                    wT_tok if (t == 0 and grp == 0)
                                    else None] if ic == ics[0] else ()),
                            mark=(grp == 2 and ic == 21))
                    if grp == 2:
                        dn_half[half] = P.tok("tensor")
            for ici, (f0, f1) in enumerate(ICH):
                fw = f1 - f0
                bg = ici % 2
                for cc in range(8):
                    P.emit("tensor", lambda e, t=t, cc=cc, f0=f0, f1=f1, bg=bg: e.matmul(
                        ps[:, 2 * bg, 0:f1 - f0], xqTt[:, t % 2, cc, :],
                        wgT[:, cc, f0:f1], start=(cc == 0), stop=(cc == 7)),
                        waits=([xqT_toks[t], wgu_tok, last_silu[bg],
                                last_h[bg]] +
                               ([p2_done["vector"], p2_done["scalar"],
                                 p2_done["gpsimd"]]
                                if (t == 0 and ici == 0) else [])
                               if cc == 0 else ()))
                for cc in range(8):
                    P.emit("tensor", lambda e, t=t, cc=cc, f0=f0, f1=f1, bg=bg: e.matmul(
                        ps[:, 2 * bg + 1, 0:f1 - f0], xqTt[:, t % 2, cc, :],
                        wuT[:, cc, f0:f1], start=(cc == 0), stop=(cc == 7)),
                        mark=(cc == 7))
                gu = P.tok("tensor")
                sg_t = P.emit("scalar", lambda e, bg=bg, fw=fw, sv=ici % 2: e.activation(
                    sgl[:, sv, 0:fw], ps[:, 2 * bg, 0:fw], AF.Silu),
                    waits=[gu, last_h[bg]], mark=True)
                last_silu[bg] = sg_t
                htok = P.emit("vector", lambda e, bg=bg, fw=fw, sv=ici % 2: e.tensor_tensor(
                    hbuf[:, sv, 0:fw], sgl[:, sv, 0:fw],
                    ps[:, 2 * bg + 1, 0:fw],
                    op=ALU.mult), waits=[sg_t], mark=True)
                last_h[bg] = htok
                emit_quant_v(hbuf[:, ici % 2, 0:fw], hq[:, f0:f1], fw,
                             src_tok=htok, dst_tok=prev_trts[ici // 2])
                vq_toks.append(P.tok("vector"))
                if ici == 0 and pend_ln[0] is not None:
                    fin_ffn(*pend_ln[0])
                    pend_ln[0] = None
                if ici in (3, 5):
                    emit_tgrp({3: 0, 5: 1}[ici])
            emit_tgrp(2)
            dn = P.tok("tensor")
            dn_toks[t] = dn
            P.emit("sync", lambda e, t=t: e.dma_start(
                out=xjit[:, t % 2, :], in_=s_x1[128 * t:128 * (t + 1), :]),
                waits=[x1_spills[t], p2_done["vector"] if t < 2 else None,
                       frt[t - 2] if t >= 2 else None], dma=True)
            xj = P.tok("sync")
            P.emit("vector", lambda e, t=t: e.tensor_tensor(
                r1[:, t % 2, 0:512], ps[:, 4, :],
                xjit[:, t % 2, 0:512], op=ALU.add),
                waits=[dn_half[0], xj, lnst["sq"][t % 2]], mark=True)
            rt = P.emit("vector", lambda e, t=t: e.tensor_tensor(
                r1[:, t % 2, 512:1024], ps[:, 5, :],
                xjit[:, t % 2, 512:1024], op=ALU.add),
                waits=[dn_half[1]], mark=True)
            frt[t] = rt
            head = emit_ln_head(r1[:, t % 2, :], rt, t % 2)
            pend_ln[0] = (head, t)
        if pend_ln[0] is not None:
            fin_ffn(*pend_ln[0])
            pend_ln[0] = None
        if phlim < 3:
            dz = P.emit("vector", lambda e: e.memset(outt[:], 0.0),
                        waits=[P.tok("tensor"), P.tok("scalar"),
                               P.tok("gpsimd"), P.tok("sync")], mark=True)
            P.emit("sync", lambda e: e.dma_start(
                out=d_out[0:128, :], in_=outt[:]), waits=[dz], dma=True)
        fin = P.tok("sync")
        for e in ENGS:
            P.emit(e, lambda en_: en_.nop(), waits=[fin])

    P.finalize()
    blk = en(nc.Block())

    @blk.tensor
    def _(e):
        P.replay("tensor", e)

    @blk.vector
    def _(e):
        P.replay("vector", e)

    @blk.scalar
    def _(e):
        P.replay("scalar", e)

    @blk.gpsimd
    def _(e):
        P.replay("gpsimd", e)

    @blk.sync
    def _(e):
        P.replay("sync", e)

    ctx.close()
    return nc


_NC_CACHE = {}


def _get_nc():
    if "nc" not in _NC_CACHE:
        _NC_CACHE["nc"] = build_program()
    return _NC_CACHE["nc"]


def kernel(L0_attn_x, L0_attn_W_attn, L0_attn_W_proj, L0_ffn_w_gate,
           L0_ffn_w_up, L0_ffn_w_down):
    x = np.ascontiguousarray(np.asarray(L0_attn_x, dtype=np.float32))
    wa = np.ascontiguousarray(np.asarray(L0_attn_W_attn, dtype=np.float32))
    wp = np.ascontiguousarray(np.asarray(L0_attn_W_proj, dtype=np.float32))
    wg = np.ascontiguousarray(np.asarray(L0_ffn_w_gate, dtype=np.float32))
    wu = np.ascontiguousarray(np.asarray(L0_ffn_w_up, dtype=np.float32))
    wd = np.ascontiguousarray(np.asarray(L0_ffn_w_down, dtype=np.float32))
    dec = [_decay_table(p) for p in range(2)]
    idm = np.ascontiguousarray(np.eye(128, dtype=ml_dtypes.bfloat16))
    msk = [_masks(p) for p in range(2)]
    in_maps = []
    for c in range(8):
        b, p = c // 2, c % 2
        xb = np.ascontiguousarray(x[b])
        own = np.ascontiguousarray(
            xb.reshape(NT, 128, C)[p::2].reshape(OWN * 128, C))
        in_maps.append({
            "x_full": xb, "x_own": own, "w_attn": wa, "w_proj": wp,
            "w_gate_sh": np.ascontiguousarray(wg[352 * c:352 * (c + 1)]),
            "w_up_sh": np.ascontiguousarray(wu[352 * c:352 * (c + 1)]),
            "w_down_sh": np.ascontiguousarray(wd[128 * c:128 * (c + 1)]),
            "decay_tab": dec[p], "mask_a": msk[p][0], "mask_b": msk[p][1],
            "ident": idm,
        })
    nc = _get_nc()
    res = run_bass_kernel_spmd(nc, in_maps, core_ids=list(range(8)))
    out = np.empty((B, T, C), dtype=np.float32)
    for c in range(8):
        b, p = c // 2, c % 2
        o = np.asarray(res.results[c]["out"]).reshape(OWN, 128, C)
        out.reshape(B, NT, 128, C)[b, p::2] = o
    return out


# version 29 (final)
# speedup vs baseline: 1.0548x; 1.0098x over previous
import contextlib
import numpy as np
import ml_dtypes

import concourse.bass as bass
import concourse.mybir as mybir
from concourse.bass_utils import run_bass_kernel_spmd

F32 = mybir.dt.float32
BF16 = mybir.dt.bfloat16
AF = mybir.ActivationFunctionType
ALU = mybir.AluOpType
AX = mybir.AxisListType

B, T, C = 4, 4096, 1024
NH, HD, FF = 16, 64, 2816
NT = T // 128
OWN = 16
LN6 = float(np.log(np.float64(6.0)))
EPS = 1e-5
_QT = [0.0625, 0.5625, 1.5625, 3.0625, 6.25, 12.25, 25.0]

DEBUG = False
PHLIM = 3
ATTLIM = 5
NQT = 4
NQC = 4
EXPON = True


def _decay_values():
    WM_DECAY_LENGTH, WM_ECHOIC = 2048, 16
    dl = WM_DECAY_LENGTH - WM_ECHOIC + 1
    nums = np.linspace(0.0, 1.0, dl, dtype=np.float64)
    dv = 1.0 - np.power(nums, 1.0 / np.e)
    dv = np.concatenate([np.ones(WM_ECHOIC - 1), dv]).astype(np.float32)
    out = np.zeros(T + 256, dtype=np.float32)
    out[:2048] = dv
    return out


def _decay_table(p):
    dv = _decay_values()
    r = np.arange(128)[:, None]
    c = np.arange(128)[None, :]
    tab = np.zeros((18, 128, 128), dtype=np.float32)
    for idx in range(18):
        d = idx - 1 + p
        dist = d * 128 + (c - r)
        tab[idx] = dv[np.clip(dist, 0, T - 1)] * (dist >= 0)
    tab = tab.transpose(1, 0, 2).reshape(128, 18 * 128)
    return np.ascontiguousarray(tab.astype(ml_dtypes.bfloat16))


def _masks(p):
    r = np.arange(128)[:, None]
    c = np.arange(128)[None, :]
    tri = (c >= r).astype(np.float32)
    if p == 0:
        ma, mb = tri, np.zeros((128, 128), np.float32)
    else:
        ma, mb = np.ones((128, 128), np.float32), tri
    return (np.ascontiguousarray(ma.astype(ml_dtypes.bfloat16)),
            np.ascontiguousarray(mb.astype(ml_dtypes.bfloat16)))


ENGS = ("tensor", "vector", "scalar", "gpsimd", "sync")


class Prog:

    def __init__(self, nc, ctx):
        self.ops = []
        self.ctr = {}
        self.sem = {}
        for e in ENGS:
            for k in (e, e + "_d"):
                self.sem[k] = ctx.enter_context(nc.semaphore("c_" + k))
                self.ctr[k] = 0

    @staticmethod
    def _flat(waits):
        out = []
        for w in waits:
            if w is None:
                continue
            if isinstance(w, list):
                out.extend(x for x in w if x is not None and x[1] > 0)
            elif w[1] > 0:
                out.append(w)
        return out

    def emit(self, eng, fn, waits=(), mark=False, dma=False):
        key = eng + "_d" if dma else eng
        inc = 16 if dma else (1 if mark else 0)
        if inc:
            self.ctr[key] += inc
        self.ops.append([eng, fn, self._flat(waits), inc, key])
        return (key, self.ctr[key])

    def tok(self, eng):
        return [(eng, self.ctr[eng]), (eng + "_d", self.ctr[eng + "_d"])]

    def finalize(self):
        waited = {}
        for op in self.ops:
            for (we, wv) in op[2]:
                if we.endswith("_d"):
                    waited.setdefault(we, set()).add(wv)
        run = {k: 0 for k in self.ctr}
        for op in self.ops:
            eng, fn, waits, inc, key = op
            if inc and key.endswith("_d"):
                pre = run[key]
                fences = [w for w in waited.get(key, ()) if pre >= w > pre - 16
                          or (pre >= w and any(True for _ in ()))]
                run[key] = pre + inc
        last_fenced = {k: 0 for k in self.ctr}
        run = {k: 0 for k in self.ctr}
        for op in self.ops:
            eng, fn, waits, inc, key = op
            if inc and key.endswith("_d"):
                pre = run[key]
                ws = sorted(w for w in waited.get(key, ())
                            if last_fenced[key] < w <= pre)
                if ws:
                    op[2] = [(key, ws[-1])] + op[2]
                    last_fenced[key] = ws[-1]
                run[key] = pre + inc

    def replay(self, name, e):
        for eng, fn, waits, inc, key in self.ops:
            if eng != name:
                continue
            for (we, wv) in waits:
                e.wait_ge(self.sem[we], wv)
            inst = fn(e)
            if inc:
                inst.then_inc(self.sem[key], inc)


def build_program(phlim=None):
    if phlim is None:
        phlim = PHLIM
    attlim = ATTLIM
    nqt, nqc, expon = NQT, NQC, EXPON
    nc = bass.Bass("TRN2", target_bir_lowering=False)
    okind = "ExternalOutput" if DEBUG else "Internal"

    d_x = nc.dram_tensor("x_full", [T, C], F32, kind="ExternalInput").ap()
    d_xo = nc.dram_tensor("x_own", [OWN * 128, C], F32, kind="ExternalInput").ap()
    d_wa = nc.dram_tensor("w_attn", [C, 3 * C], F32, kind="ExternalInput").ap()
    d_wp = nc.dram_tensor("w_proj", [C, C], F32, kind="ExternalInput").ap()
    d_wg = nc.dram_tensor("w_gate_sh", [FF // 8, C], F32,
                          kind="ExternalInput").ap()
    d_wu = nc.dram_tensor("w_up_sh", [FF // 8, C], F32,
                          kind="ExternalInput").ap()
    d_wd = nc.dram_tensor("w_down_sh", [C // 8, FF], F32,
                          kind="ExternalInput").ap()
    d_dec = nc.dram_tensor("decay_tab", [128, 18 * 128], BF16,
                           kind="ExternalInput").ap()
    d_ma = nc.dram_tensor("mask_a", [128, 128], BF16, kind="ExternalInput").ap()
    d_mb = nc.dram_tensor("mask_b", [128, 128], BF16, kind="ExternalInput").ap()
    d_id = nc.dram_tensor("ident", [128, 128], BF16, kind="ExternalInput").ap()
    d_out = nc.dram_tensor("out", [OWN * 128, C], F32, kind="ExternalOutput").ap()
    s_xb = nc.dram_tensor("s_xb", [T, C], BF16, kind="Internal").ap()
    s_xob = nc.dram_tensor("s_xob", [OWN * 128, C], BF16, kind="Internal").ap()
    s_wgq = nc.dram_tensor("s_wgq", [FF, C], BF16, kind="Internal").ap()
    s_wgi = nc.dram_tensor("s_wgi", [FF // 8, C], BF16, kind="Internal").ap()
    s_wui = nc.dram_tensor("s_wui", [FF // 8, C], BF16, kind="Internal").ap()
    s_wdi = nc.dram_tensor("s_wdi", [C // 8, FF], BF16, kind="Internal").ap()
    s_wuq = nc.dram_tensor("s_wuq", [FF, C], BF16, kind="Internal").ap()
    s_wdq = nc.dram_tensor("s_wdq", [C, FF], BF16, kind="Internal").ap()
    s_xq = nc.dram_tensor("s_xq", [OWN * 128, C], BF16, kind=okind).ap()
    s_x1 = nc.dram_tensor("s_x1", [OWN * 128, C], F32, kind=okind).ap()
    s_y = (nc.dram_tensor("s_y", [1024, 2048], BF16, kind="ExternalOutput").ap()
           if DEBUG else None)

    ctx = contextlib.ExitStack()
    en = ctx.enter_context
    P = Prog(nc, ctx)

    mska = en(nc.sbuf_tensor("mska", [128, 128], BF16))
    mskb = en(nc.sbuf_tensor("mskb", [128, 128], BF16))
    onesc = en(nc.sbuf_tensor("onesc", [128, 1], BF16))
    onesr = en(nc.sbuf_tensor("onesr", [128, 64], BF16))
    idn = en(nc.sbuf_tensor("idn", [128, 128], BF16))
    lns = en(nc.sbuf_tensor("lns", [128, 2, 10], F32))
    qw = en(nc.sbuf_tensor("qw", [128, 512], F32))
    qb = en(nc.sbuf_tensor("qb", [128, 11, 512], BF16))
    qm = en(nc.sbuf_tensor("qm", [128, 4, 32], F32))
    qout = en(nc.sbuf_tensor("qout", [128, 2, 512], BF16))
    qws0 = {"qm": qm, "qw": qw, "qb": qb, "prev": None}
    ps = en(nc.psum_tensor("ps", [128, 8, 512], F32))
    psx = ps[:, 6, :]
    pst = ps[:, 7, :].bitcast(BF16)

    

    def emit_quant_v(src_ap, dst_ap, fd, src_tok=None, dst_tok=None,
                     np_=128, ws=None):
        if ws is None:
            ws = qws0
        qm, qw, qb = ws["qm"], ws["qw"], ws["qb"]
        nb = fd // 16
        t0 = P.emit("vector", lambda e: e.tensor_reduce(
            qm[0:np_, 0, 0:nb], src_ap.rearrange("p (b s) -> p b s", s=16),
            axis=AX.X, op=ALU.max, apply_absolute_value=True),
            waits=[src_tok], mark=True)
        vt2 = P.emit("vector", lambda e: e.tensor_single_scalar(
            qm[0:np_, 3, 0:nb], qm[0:np_, 0, 0:nb], 1.0 / 12.0, ALU.mult),
            waits=[t0], mark=True)
        m2 = P.emit("vector", lambda e: e.tensor_tensor(
            qm[0:np_, 1, 0:nb], qm[0:np_, 0, 0:nb], qm[0:np_, 0, 0:nb],
            op=ALU.mult), waits=[t0], mark=True)
        x2 = P.emit("scalar", lambda e: e.square(
            qw[0:np_, 0:fd], src_ap), waits=[src_tok, ws.get("cl")],
            mark=True)
        m2in = qm[0:np_, 1, 0:nb].unsqueeze(2).broadcast_to([np_, nb, 16])
        qw3 = qw[0:np_, 0:fd].rearrange("p (b s) -> p b s", s=16)
        cl = None
        for i, thr in enumerate(_QT):
            cl = P.emit("vector", lambda e, i=i, k=thr / 36.0: e.scalar_tensor_tensor(
                qb[0:np_, i, 0:fd].rearrange("p (b s) -> p b s", s=16),
                m2in, k, qw3, ALU.mult, ALU.is_lt),
                waits=[m2, x2] if i == 0 else (), mark=(i == 6))
        cm = P.emit("vector", lambda e: e.tensor_tensor(
            qb[0:np_, 7, 0:fd], qb[0:np_, 0, 0:fd],
            qb[0:np_, 1, 0:fd], op=ALU.add), waits=[cl], mark=True)
        a2 = P.emit("vector", lambda e: e.tensor_tensor(
            qb[0:np_, 8, 0:fd], qb[0:np_, 2, 0:fd], qb[0:np_, 3, 0:fd],
            op=ALU.add), waits=[cl], mark=True)
        a3 = P.emit("vector", lambda e: e.tensor_tensor(
            qb[0:np_, 9, 0:fd], qb[0:np_, 7, 0:fd], qb[0:np_, 8, 0:fd],
            op=ALU.add), waits=[a2], mark=True)
        a4 = P.emit("vector", lambda e: e.scalar_tensor_tensor(
            qb[0:np_, 0, 0:fd], qb[0:np_, 6, 0:fd], 2.0, qb[0:np_, 5, 0:fd],
            ALU.mult, ALU.add), waits=[cm], mark=True)
        a5 = P.emit("vector", lambda e: e.scalar_tensor_tensor(
            qb[0:np_, 1, 0:fd], qb[0:np_, 4, 0:fd], 2.0, qb[0:np_, 9, 0:fd],
            ALU.mult, ALU.add), waits=[a3], mark=True)
        a6 = P.emit("vector", lambda e: e.scalar_tensor_tensor(
            qb[0:np_, 2, 0:fd], qb[0:np_, 0, 0:fd], 2.0, qb[0:np_, 1, 0:fd],
            ALU.mult, ALU.add), waits=[a5], mark=True)
        sgn = P.emit("scalar", lambda e: e.activation(
            qb[0:np_, 10, 0:fd], src_ap, AF.Sign),
            waits=[src_tok, ws.get("u1")], mark=True)
        u1 = P.emit("vector", lambda e: e.tensor_tensor(
            qb[0:np_, 5, 0:fd], qb[0:np_, 2, 0:fd], qb[0:np_, 10, 0:fd],
            op=ALU.mult), waits=[a6, sgn], mark=True)
        vq = P.emit("vector", lambda e, d=dst_ap: e.tensor_tensor(
            d.rearrange("p (b s) -> p b s", s=16),
            qb[0:np_, 5, 0:fd].rearrange("p (b s) -> p b s", s=16),
            qm[0:np_, 3, 0:nb].unsqueeze(2).broadcast_to([np_, nb, 16]),
            op=ALU.mult), waits=[vt2, u1, dst_tok], mark=True)
        ws.update(cl=cl, u1=u1)
        ws["prev"] = vq
        return vq

    lnst = {"l3": None, "sq": [None, None]}

    def emit_ln_head(rsrc, rt, sl):
        l1 = P.emit("vector", lambda e: e.tensor_reduce(
            lns[:, sl, 0:1], rsrc, axis=AX.X, op=ALU.add), waits=[rt],
            mark=True)
        l2 = P.emit("vector", lambda e: e.tensor_single_scalar(
            lns[:, sl, 1:2], lns[:, sl, 0:1], 1.0 / 1024.0, ALU.mult),
            waits=[l1], mark=True)
        sq = P.emit("scalar", lambda e: e.square(lnw[:], rsrc),
                    waits=[rt, lnst["l3"]], mark=True)
        lnst["sq"][sl] = sq
        l3 = P.emit("vector", lambda e: e.tensor_reduce(
            lns[:, sl, 2:3], lnw[:], axis=AX.X, op=ALU.add), waits=[sq],
            mark=True)
        lnst["l3"] = l3
        l4 = P.emit("vector", lambda e: e.tensor_single_scalar(
            lns[:, sl, 3:4], lns[:, sl, 2:3], 1.0 / 1024.0, ALU.mult),
            waits=[l3], mark=True)
        l5 = P.emit("vector", lambda e: e.tensor_tensor(
            lns[:, sl, 4:5], lns[:, sl, 1:2], lns[:, sl, 1:2], op=ALU.mult),
            waits=[l2], mark=True)
        l6 = P.emit("vector", lambda e: e.tensor_tensor(
            lns[:, sl, 5:6], lns[:, sl, 3:4], lns[:, sl, 4:5],
            op=ALU.subtract), waits=[l4, l5], mark=True)
        l7 = P.emit("vector", lambda e: e.tensor_single_scalar(
            lns[:, sl, 6:7], lns[:, sl, 5:6], EPS, ALU.add), waits=[l6],
            mark=True)
        l8 = P.emit("vector", lambda e: e.tensor_single_scalar(
            lns[:, sl, 7:8], lns[:, sl, 1:2], -1.0, ALU.mult), waits=[l2],
            mark=True)
        l9 = P.emit("scalar", lambda e: e.activation(
            lns[:, sl, 8:9], lns[:, sl, 6:7], AF.Ln), waits=[l7], mark=True)
        l10 = P.emit("scalar", lambda e: e.activation(
            lns[:, sl, 9:10], lns[:, sl, 8:9], AF.Exp, scale=-0.5),
            waits=[l9], mark=True)
        return (rsrc, sl, l10, l8)

    def emit_ln_tail(head, dst_ap, dst_tok=None):
        rsrc, sl, l10, l8 = head
        return P.emit("vector", lambda e, d=dst_ap: e.tensor_scalar(
            d, rsrc, lns[:, sl, 7:8], lns[:, sl, 9:10], ALU.add, ALU.mult),
            waits=[l10, l8, dst_tok], mark=True)

    def emit_ln(rsrc, dst_ap, rt, dst_tok=None, sl=0):
        return emit_ln_tail(emit_ln_head(rsrc, rt, sl), dst_ap, dst_tok)

    P.emit("sync", lambda e: e.dma_start(out=mska[:], in_=d_ma[:, :]), dma=True)
    P.emit("sync", lambda e: e.dma_start(out=mskb[:], in_=d_mb[:, :]), dma=True)
    P.emit("sync", lambda e: e.dma_start(out=idn[:], in_=d_id[:, :]), dma=True)
    P.emit("gpsimd", lambda e: e.memset(onesc[:], 1.0), mark=True)
    P.emit("gpsimd", lambda e: e.memset(onesr[:], 1.0), mark=True)

    with nc.sbuf_tensor("xstg", [128, 2, 4096], BF16) as xstg:
        spills = []
        njob = (NT + OWN) // 4
        for t in range(njob):
            stg = xstg[:, t % 2, :]
            if t < NT // 4:
                srcb = d_x[512 * t:512 * (t + 1), :]
                dstb = s_xb[512 * t:512 * (t + 1), :]
            else:
                tt = t - NT // 4
                srcb = d_xo[512 * tt:512 * (tt + 1), :]
                dstb = s_xob[512 * tt:512 * (tt + 1), :]
            w = [spills[t - 2]] if t >= 2 else []
            P.emit("gpsimd", lambda e, stg=stg, srcb=srcb: e.dma_start(
                out=stg, in_=srcb), waits=w, dma=True)
            ld = P.tok("gpsimd")
            P.emit("sync", lambda e, stg=stg, dstb=dstb: e.dma_start(
                out=dstb, in_=stg), waits=[ld], dma=True)
            spills.append(P.tok("sync"))
        spill_all = P.tok("sync")

    with contextlib.ExitStack() as yctx:
        y16 = yctx.enter_context(nc.sbuf_tensor("y16", [128, 8, 2048], BF16))

        with contextlib.ExitStack() as actx:
            aen = actx.enter_context
            xT = aen(nc.sbuf_tensor("xT", [128, 8, T], BF16))
            xoT = aen(nc.sbuf_tensor("xoT", [128, 8, 2048], BF16))
            waq = aen(nc.sbuf_tensor("waq", [128, 8, 128], BF16))
            wak = aen(nc.sbuf_tensor("wak", [128, 8, 128], BF16))
            wav = aen(nc.sbuf_tensor("wav", [128, 8, 256], BF16))
            v16 = aen(nc.sbuf_tensor("v16", [128, 32, 256], BF16))
            k16 = aen(nc.sbuf_tensor("k16", [128, T], BF16))
            q16 = aen(nc.sbuf_tensor("q16", [128, 2048], BF16))
            dect = aen(nc.sbuf_tensor("dect", [128, 18 * 128], BF16))
            pAB = aen(nc.sbuf_tensor("pAB", [128, 3, 2, 512], BF16))
            pA = pAB[:, :, 0, :]
            pB = pAB[:, :, 1, :]
            pmA = aen(nc.sbuf_tensor("pmA", [128, 3, 128], BF16))
            pmB = aen(nc.sbuf_tensor("pmB", [128, 3, 128], BF16))
            pdA = aen(nc.sbuf_tensor("pdA", [128, 3, 512], BF16))
            pdB = aen(nc.sbuf_tensor("pdB", [128, 3, 512], BF16))
            rz = aen(nc.sbuf_tensor("rz", [128, 512], F32, side="right"))
            rz16 = aen(nc.sbuf_tensor("rz16", [128, 512], BF16, side="right"))
            rzb = aen(nc.sbuf_tensor("rzb", [128, 512], BF16, side="right"))
            wf32 = aen(nc.sbuf_tensor("wf32", [128, 2, 512], F32))

            wjobs = []
            for r0, r1 in ((0, 128), (128, 256), (256, 352)):
                for f0 in (0, 512):
                    wjobs.append((d_wg, s_wgi, r0, r1, f0, f0 + 512))
            for r0, r1 in ((0, 128), (128, 256), (256, 352)):
                for f0 in (0, 512):
                    wjobs.append((d_wu, s_wui, r0, r1, f0, f0 + 512))
            for ci in range(6):
                f0, f1 = 512 * ci, min(FF, 512 * ci + 512)
                wjobs.append((d_wd, s_wdi, 0, 128, f0, f1))
            wlds = []
            wsps = [None] * len(wjobs)
            wvqs = [None] * len(wjobs)

            def emit_wload(j):
                dsrc, ddst, r0, r1, f0, f1 = wjobs[j]
                np_, fd = r1 - r0, f1 - f0
                P.emit("scalar", lambda e, dsrc=dsrc, r0=r0, r1=r1, f0=f0, f1=f1, fd=fd, np_=np_, sl=j % 2: e.dma_start(
                    out=wf32[0:np_, sl, 0:fd], in_=dsrc[r0:r1, f0:f1]),
                    waits=[wvqs[j - 2]] if j >= 2 else (), dma=True)
                wlds.append(P.tok("scalar"))

            emit_wload(0)
            emit_wload(1)
            for wj, (dsrc, ddst, r0, r1, f0, f1) in enumerate(wjobs):
                np_, fd = r1 - r0, f1 - f0
                vq = emit_quant_v(wf32[0:np_, wj % 2, 0:fd],
                                  qout[0:np_, wj % 2, 0:fd], fd,
                                  src_tok=wlds[wj],
                                  dst_tok=wsps[wj - 2] if wj >= 2 else None,
                                  np_=np_)
                wvqs[wj] = vq
                P.emit("scalar", lambda e, ddst=ddst, r0=r0, r1=r1, f0=f0, f1=f1, fd=fd, np_=np_, sl=wj % 2: e.dma_start(
                    out=ddst[r0:r1, f0:f1], in_=qout[0:np_, sl, 0:fd]),
                    waits=[vq], dma=True)
                wsps[wj] = P.tok("scalar")
                if wj + 2 < len(wjobs):
                    emit_wload(wj + 2)
            wq_spill = P.tok("scalar")
            ccd = {}

            def emit_cc():
                for (ci, co) in ((s_wgi, s_wgq), (s_wui, s_wuq),
                                 (s_wdi, s_wdq)):
                    P.emit("gpsimd",
                           lambda e, ci=ci, co=co: e.collective_compute(
                               "AllGather", ALU.bypass, ins=[ci[:, :]],
                               outs=[co[:, :]],
                               replica_groups=[[0, 1, 2, 3, 4, 5, 6, 7]]),
                           waits=[wq_spill], mark=True)
                ccd["tok"] = P.tok("gpsimd")

            P.emit("sync", lambda e: e.dma_start(out=dect[:], in_=d_dec[:, :]),
                   dma=True)
            for cc in range(8):
                P.emit("sync", lambda e, cc=cc: e.dma_start(
                    out=xT[:, cc, :], in_=s_xb[:, 128 * cc:128 * (cc + 1)],
                    transpose=True), waits=[spill_all], dma=True)
            for cc in range(8):
                P.emit("sync", lambda e, cc=cc: e.dma_start(
                    out=xoT[:, cc, :], in_=s_xob[:, 128 * cc:128 * (cc + 1)],
                    transpose=True), dma=True)
            xt_tok = P.tok("sync")


            ROT = ((0, 1), (2, 3), (6, 7))
            PROT = (0, 1, 6)
            pend = [None]

            last_pair_scal = None
            last_pair_expf = None

            def emit_wa(qt, pr, prod_tok):
                co = 256 * qt + 128 * pr
                toks = {}
                if pr == 0:
                    for cc in range(8):
                        P.emit("gpsimd", lambda e, cc=cc, qt=qt: e.dma_start(
                            out=wav[:, cc, :],
                            in_=d_wa[128 * cc:128 * (cc + 1),
                                     2048 + 256 * qt:2048 + 256 * (qt + 1)]),
                            waits=[prod_tok] if cc == 0 else (), dma=True)
                    toks["v"] = P.tok("gpsimd")
                for cc in range(8):
                    P.emit("gpsimd", lambda e, cc=cc, co=co: e.dma_start(
                        out=wak[:, cc, :],
                        in_=d_wa[128 * cc:128 * (cc + 1),
                                 1024 + co:1024 + co + 128]),
                        waits=[prod_tok] if (pr == 1 and cc == 0) else (),
                        dma=True)
                toks["k"] = P.tok("gpsimd")
                for cc in range(8):
                    P.emit("gpsimd", lambda e, cc=cc, co=co: e.dma_start(
                        out=waq[:, cc, :],
                        in_=d_wa[128 * cc:128 * (cc + 1), co:co + 128]),
                        dma=True)
                toks["q"] = P.tok("gpsimd")
                return toks

            wa_next = emit_wa(0, 0, spill_all)
            for qt in range(nqt if phlim >= 1 else 0):
                for pr in range(2):
                    gp = 2 * qt + pr
                    wa_cur = wa_next
                    if qt == nqt - 1 and pr == 1:
                        emit_cc()

                    evac_toks = [None, None, None]
                    jobs = []
                    if pr == 0:
                        for t in range(NT):
                            jobs.append(("v", t))
                    jobs += [("k", ch) for ch in range(8)]
                    jobs += [("q", ch) for ch in range(4)]
                    seen_kind = set()
                    for ji, (kind, idx) in enumerate(jobs):
                        bk = PROT[ji % 3]
                        bw = [evac_toks[ji % 3]]
                        if kind not in seen_kind:
                            seen_kind.add(kind)
                            bw += [wa_cur[kind]]
                        if ji == 0:
                            bw += [xt_tok, last_pair_expf]
                        if ji == 2 and pend[0] is not None:
                            sc, _vec = pend[0]()
                            pend[0] = None
                            bw += [sc]
                        for cc in range(8):
                            if kind == "v":
                                P.emit("tensor", lambda e, t=idx, cc=cc, bk=bk: e.matmul(
                                    ps[:, bk, 0:256],
                                    xT[:, cc, 128 * t:128 * (t + 1)],
                                    wav[:, cc, :], start=(cc == 0),
                                    stop=(cc == 7)),
                                    waits=bw if cc == 0 else (),
                                    mark=(cc == 7))
                            elif kind == "k":
                                P.emit("tensor", lambda e, ch=idx, cc=cc, bk=bk: e.matmul(
                                    ps[:, bk, :], wak[:, cc, :],
                                    xT[:, cc, 512 * ch:512 * (ch + 1)],
                                    start=(cc == 0), stop=(cc == 7)),
                                    waits=bw if cc == 0 else (),
                                    mark=(cc == 7))
                            else:
                                P.emit("tensor", lambda e, ch=idx, cc=cc, bk=bk: e.matmul(
                                    ps[:, bk, :], waq[:, cc, :],
                                    xoT[:, cc, 512 * ch:512 * (ch + 1)],
                                    start=(cc == 0), stop=(cc == 7)),
                                    waits=bw if cc == 0 else (),
                                    mark=(cc == 7))
                        mm = P.tok("tensor")
                        if kind == "v":
                            tk = P.emit("vector", lambda e, t=idx, bk=bk: e.tensor_copy(
                                v16[:, t, :], ps[:, bk, 0:256]), waits=[mm],
                                mark=True)
                        elif kind == "k":
                            tk = P.emit("vector", lambda e, ch=idx, bk=bk: e.tensor_copy(
                                k16[:, 512 * ch:512 * (ch + 1)], ps[:, bk, :]),
                                waits=[mm], mark=True)
                        else:
                            tk = P.emit("vector", lambda e, ch=idx, bk=bk: e.tensor_copy(
                                q16[:, 512 * ch:512 * (ch + 1)], ps[:, bk, :]),
                                waits=[mm], mark=True)
                        evac_toks[ji % 3] = tk
                    kq_tok = P.tok("vector")
                    if not (qt == nqt - 1 and pr == 1):
                        nq2, np2 = (qt, 1) if pr == 0 else (qt + 1, 0)
                        wa_next = emit_wa(nq2, np2, P.tok("tensor"))
                    prev_qc_scal = None
                    prev_qc_vec = None
                    prev_qc_expf = None

                    for qc in range(nqc if attlim >= 2 else 0):
                        nkt = 8 * qc + 8
                        z_started = [False, False]
                        pv_started = [False, False]
                        exp_toks = {}
                        vmarks = {}
                        tmarks = {}
                        kt_geom = {}

                        def geom(kt, qc=qc):
                            jlo = min(3, max(0, -(-(kt - 1 - 8 * qc) // 2)))
                            d0 = kt - 8 * qc
                            jm = None
                            msk = None
                            if 0 <= d0 <= 7:
                                jm = d0 // 2 if d0 % 2 == 0 else (d0 - 1) // 2
                                msk = mska if d0 % 2 == 0 else mskb
                            jwlo = jlo
                            jwhi_t = min(3, (kt + 16 - 8 * qc) // 2)
                            jwhi = 3 if kt == 0 else jwhi_t
                            if attlim < 3:
                                jm, msk, jwhi_t, jwhi = None, None, jwlo - 1, jwlo - 1
                            return jlo, jm, msk, jwlo, jwhi, jwhi_t

                        def emit_st(kt, qc=qc):
                            jlo = geom(kt)[0]
                            w0 = 128 * jlo
                            ba, bb = ROT[kt % 3]
                            w = [exp_toks.get(kt - 3), kq_tok]
                            if kt < 2:
                                w += [prev_qc_expf]
                            elif kt == 2:
                                w += [prev_qc_expf, prev_qc_scal]
                            for h, bx in ((0, ba), (1, bb)):
                                P.emit("tensor",
                                       lambda e, h=h, kt=kt, w0=w0, bx=bx, qc=qc: e.matmul(
                                           ps[:, bx, w0:],
                                           k16[64 * h:64 * (h + 1),
                                               128 * kt:128 * (kt + 1)],
                                           q16[64 * h:64 * (h + 1),
                                               512 * qc + w0:512 * (qc + 1)],
                                           start=True, stop=True,
                                           tile_position=(64 * h, 0)),
                                       waits=w if h == 0 else (),
                                       mark=(h == 1))
                            return P.tok("tensor")

                        def emit_exp(kt, st_tok):
                            jlo = geom(kt)[0]
                            w0 = 128 * jlo
                            ba, bb = ROT[kt % 3]
                            sl = kt % 3
                            w = [st_tok, vmarks.get(kt - 3), tmarks.get(kt - 3)]
                            tok = P.emit("scalar",
                                         lambda e, w0=w0, ba=ba, sl=sl: e.activation(
                                             pAB[:, sl, :, w0:],
                                             ps[:, ba:ba + 2, w0:],
                                             AF.Exp, scale=0.125),
                                         waits=w, mark=True)
                            exp_toks[kt] = tok
                            return tok

                        def emit_vops(kt):
                            jlo, jm, msk, jwlo, jwhi, jwhi_t = geom(kt)
                            sl = kt % 3
                            v_last = None
                            first = [exp_toks[kt]]
                            if jm is not None:
                                P.emit("vector",
                                       lambda e, sl=sl, jm=jm, m=msk: e.tensor_tensor(
                                           pmA[:, sl, :],
                                           pA[:, sl, 128 * jm:128 * (jm + 1)],
                                           m[:], op=ALU.mult), waits=first)
                                first = []
                                v_last = P.emit("vector",
                                                lambda e, sl=sl, jm=jm, m=msk: e.tensor_tensor(
                                                    pmB[:, sl, :],
                                                    pB[:, sl, 128 * jm:128 * (jm + 1)],
                                                    m[:], op=ALU.mult), mark=True)
                            if kt == 0 and jwhi_t < 3:
                                P.emit("vector", lambda e, sl=sl, j1=jwhi_t + 1: e.memset(
                                    pdA[:, sl, 128 * j1:], 0.0), waits=first)
                                first = []
                                v_last = P.emit("vector",
                                                lambda e, sl=sl, j1=jwhi_t + 1: e.memset(
                                                    pdB[:, sl, 128 * j1:], 0.0),
                                                mark=True)
                            if jwhi_t >= jwlo:
                                i0 = 2 * (4 * qc + jwlo) - kt + 1
                                nj = jwhi_t - jwlo + 1
                                dsl = dect[:].rearrange(
                                    "p (i q) -> p i q", q=128)[
                                    :, i0:i0 + 2 * nj - 1:2, :]
                                P.emit("vector",
                                       lambda e, sl=sl, jwlo=jwlo, nj=nj, dsl=dsl: e.tensor_tensor(
                                           pdA[:, sl, 128 * jwlo:128 * (jwlo + nj)]
                                           .rearrange("p (j q) -> p j q", q=128),
                                           pA[:, sl, 128 * jwlo:128 * (jwlo + nj)]
                                           .rearrange("p (j q) -> p j q", q=128),
                                           dsl, op=ALU.mult), waits=first)
                                first = []
                                v_last = P.emit("vector",
                                                lambda e, sl=sl, jwlo=jwlo, nj=nj, dsl=dsl: e.tensor_tensor(
                                                    pdB[:, sl, 128 * jwlo:128 * (jwlo + nj)]
                                                    .rearrange("p (j q) -> p j q", q=128),
                                                    pB[:, sl, 128 * jwlo:128 * (jwlo + nj)]
                                                    .rearrange("p (j q) -> p j q", q=128),
                                                    dsl, op=ALU.mult), mark=True)
                            vmarks[kt] = v_last
                            return v_last

                        def emit_zpv(kt, qc=qc):
                            jlo, jm, msk, jwlo, jwhi, jwhi_t = geom(kt)
                            sl = kt % 3
                            zlo = (jm + 1) if jm is not None else jlo
                            if attlim < 3:
                                zlo = 4
                            last_kt = (kt == nkt - 1)
                            tops = []
                            for h, praw, pmsk in ((0, pA, pmA), (1, pB, pmB)):
                                zops = []
                                if zlo <= 3:
                                    zops.append(("raw", praw, zlo))
                                if jm is not None:
                                    zops.append(("msk", pmsk, jm))
                                for oi, (knd, buf, j0) in enumerate(zops):
                                    st = not z_started[h]
                                    z_started[h] = True
                                    sp = last_kt and (oi == len(zops) - 1)
                                    if knd == "raw":
                                        tops.append(lambda e, h=h, sl=sl, j0=j0, buf=buf, st=st, sp=sp: e.matmul(
                                            ps[64 * h:64 * h + 1, 5, 128 * j0:],
                                            onesc[:], buf[:, sl, 128 * j0:],
                                            start=st, stop=sp,
                                            skip_group_check=True,
                                            tile_position=(0, 64 * h)))
                                    else:
                                        tops.append(lambda e, h=h, sl=sl, j0=j0, buf=buf, st=st, sp=sp: e.matmul(
                                            ps[64 * h:64 * h + 1, 5,
                                               128 * j0:128 * (j0 + 1)],
                                            onesc[:], buf[:, sl, :],
                                            start=st, stop=sp,
                                            skip_group_check=True,
                                            tile_position=(0, 64 * h)))
                            if attlim >= 4 and jwhi >= jwlo:
                                for h, pdec in ((0, pdA), (1, pdB)):
                                    st = not pv_started[h]
                                    pv_started[h] = True
                                    tops.append(lambda e, h=h, sl=sl, kt=kt, pr=pr, jwlo=jwlo, jwhi=jwhi, pdec=pdec, st=st, sp=last_kt: e.matmul(
                                        ps[64 * h:64 * (h + 1), 4,
                                           128 * jwlo:128 * (jwhi + 1)],
                                        v16[:, kt, 128 * pr + 64 * h:
                                            128 * pr + 64 * (h + 1)],
                                        pdec[:, sl, 128 * jwlo:128 * (jwhi + 1)],
                                        start=st, stop=sp,
                                        skip_group_check=True,
                                        tile_position=(0, 64 * h)))
                            w0l = [vmarks.get(kt) or exp_toks[kt]]
                            if kt == 0:
                                w0l += [prev_qc_scal, prev_qc_vec]
                            tk = None
                            for oi, tf in enumerate(tops):
                                tk = P.emit("tensor", tf,
                                            waits=w0l if oi == 0 else (),
                                            mark=(oi == len(tops) - 1))
                            tmarks[kt] = tk
                            return tk

                        for kt in range(nkt):
                            if kt == 2 and pend[0] is not None:
                                prev_qc_scal, prev_qc_vec = pend[0]()
                                pend[0] = None
                            st_tok = emit_st(kt)
                            emit_exp(kt, st_tok)
                            emit_vops(kt)
                            if kt >= 2:
                                emit_zpv(kt - 2)
                        emit_zpv(nkt - 2)
                        emit_zpv(nkt - 1)
                        zpv_tok = P.tok("tensor")
                        prev_qc_expf = exp_toks[nkt - 1]
                        if attlim < 5:
                            continue
                        P.emit("scalar", lambda e: e.activation(
                            rz[0:1, :], ps[0:1, 5, :], AF.Ln),
                            waits=[zpv_tok])
                        lt = P.emit("scalar", lambda e: e.activation(
                            rz[64:65, :], ps[64:65, 5, :], AF.Ln), mark=True)
                        P.emit("scalar", lambda e: e.activation(
                            rz16[0:1, :], rz[0:1, :], AF.Exp, scale=-1.0),
                            waits=[lt])
                        se = P.emit("scalar", lambda e: e.activation(
                            rz16[64:65, :], rz[64:65, :], AF.Exp, scale=-1.0),
                            mark=True)

                        def mk_norm(se=se, gp=gp, qc=qc,
                                    expf=exp_toks[nkt - 1]):
                            def go():
                                for h in range(2):
                                    P.emit("tensor", lambda e, h=h: e.matmul(
                                        psx[64 * h:64 * (h + 1), :],
                                        onesr[64 * h:64 * h + 1, :],
                                        rz16[64 * h:64 * h + 1, :],
                                        start=True, stop=True,
                                        skip_group_check=True,
                                        tile_position=(64 * h, 64 * h)),
                                        waits=([se, expf] if h == 0 else ()),
                                        mark=(h == 1))
                                bc = P.tok("tensor")
                                sbc = P.emit("scalar", lambda e: e.copy(
                                    rzb[:], psx[:]), waits=[bc], mark=True)
                                vec = P.emit(
                                    "vector",
                                    lambda e, gp=gp, qc=qc: e.tensor_tensor(
                                        y16[:, gp, 512 * qc:512 * (qc + 1)],
                                        ps[:, 4, :], rzb[:], op=ALU.mult),
                                    waits=[sbc], mark=True)
                                return sbc, vec
                            return go
                        pend[0] = mk_norm()
                        last_pair_expf = prev_qc_expf

            if pend[0] is not None:
                pend[0]()
                pend[0] = None
            if "tok" not in ccd:
                emit_cc()
            cc_tok = ccd["tok"]
            att_done = {e: P.tok(e) for e in ENGS}
        xjit = en(nc.sbuf_tensor("xjit", [128, 2, 1024], F32, side="right"))
        r1 = en(nc.sbuf_tensor("r1", [128, 2, 1024], F32, side="right"))
        x1t = en(nc.sbuf_tensor("x1t", [128, 1024], F32, side="right"))
        outt = en(nc.sbuf_tensor("outt", [128, 1024], F32, side="right"))
        lnw = en(nc.sbuf_tensor("lnw", [128, 1024], F32, side="right"))
        qout2 = en(nc.sbuf_tensor("qout2", [128, 2, 512], BF16, side="right"))

        wp16 = yctx.enter_context(nc.sbuf_tensor("wp16", [128, 8, 1024], BF16))
        for cc in range(8):
            P.emit("gpsimd", lambda e, cc=cc: e.dma_start(
                out=wp16[:, cc, :], in_=d_wp[128 * cc:128 * (cc + 1), :]),
                waits=([att_done["tensor"]] if cc == 0 else ()), dma=True)
        wp_tok = P.tok("gpsimd")
        r1_toks = [None] * OWN
        x1_spills = []
        xq_spills = []
        xq_sp_prev = [None, None]
        out_toks = [None] * OWN
        pend_ln = [None]

        def fin_p2(head, tp):
            x1tok = emit_ln_tail(
                head, x1t[:],
                dst_tok=x1_spills[tp - 1] if tp >= 1 else None)
            P.emit("gpsimd", lambda e, tp=tp: e.dma_start(
                out=s_x1[128 * tp:128 * (tp + 1), :], in_=x1t[:]),
                waits=[x1tok], dma=True)
            x1_spills.append(P.tok("gpsimd"))
            for f0 in (0, 512):
                vq = emit_quant_v(x1t[:, f0:f0 + 512], qout2[:, f0 // 512, :],
                                  512, src_tok=x1tok,
                                  dst_tok=xq_sp_prev[f0 // 512])
                P.emit("gpsimd", lambda e, tp=tp, f0=f0: e.dma_start(
                    out=s_xq[128 * tp:128 * (tp + 1), f0:f0 + 512],
                    in_=qout2[:, f0 // 512, :]), waits=[vq], dma=True)
                xq_sp_prev[f0 // 512] = P.tok("gpsimd")
            xq_spills.append(P.tok("gpsimd"))
        for t in range(OWN if phlim >= 2 else 0):
            P.emit("sync", lambda e, t=t: e.dma_start(
                out=xjit[:, t % 2, :], in_=d_xo[128 * t:128 * (t + 1), :]),
                waits=[r1_toks[t - 2] if t >= 2 else None,
                       att_done["vector"] if t < 2 else None], dma=True)
            xj = P.tok("sync")
            for half in range(2):
                for cc in range(8):
                    P.emit("tensor", lambda e, t=t, half=half, cc=cc: e.matmul(
                        ps[:, half, :], y16[:, cc, 128 * t:128 * (t + 1)],
                        wp16[:, cc, 512 * half:512 * (half + 1)],
                        start=(cc == 0), stop=(cc == 7)),
                        waits=([att_done["scalar"], att_done["vector"],
                                att_done["gpsimd"], wp_tok,
                                r1_toks[t - 1] if t >= 1 else None]
                               if (half == 0 and cc == 0) else ()),
                        mark=(half == 1 and cc == 7))
            pj = P.tok("tensor")
            rt = P.emit("vector", lambda e, t=t: e.tensor_tensor(
                r1[:, t % 2, :], ps[:, 0:2, :].rearrange("p a b -> p (a b)"),
                xjit[:, t % 2, :], op=ALU.add),
                waits=[pj, xj, lnst["sq"][t % 2]], mark=True)
            r1_toks[t] = rt
            head = emit_ln_head(r1[:, t % 2, :], rt, t % 2)
            if pend_ln[0] is not None:
                fin_p2(*pend_ln[0])
            pend_ln[0] = (head, t)
        if pend_ln[0] is not None:
            fin_p2(*pend_ln[0])
            pend_ln[0] = None
        if DEBUG:
            for gpd in range(8):
                P.emit("sync", lambda e, gpd=gpd: e.dma_start(
                    out=s_y[128 * gpd:128 * (gpd + 1), :],
                    in_=y16[:, gpd, :]), waits=[att_done["vector"]], dma=True)
        p2_done = {e: P.tok(e) for e in ENGS}

    with contextlib.ExitStack() as fctx:
        fen = fctx.enter_context
        wgT = fen(nc.sbuf_tensor("wgT", [128, 8, FF], BF16))
        wuT = fen(nc.sbuf_tensor("wuT", [128, 8, FF], BF16))
        wdT = fen(nc.sbuf_tensor("wdT", [128, 22, 1024], BF16))
        xqTt = fen(nc.sbuf_tensor("xqTt", [128, 2, 8, 128], BF16))
        hq = fen(nc.sbuf_tensor("hq", [128, FF], BF16))
        hqT = fen(nc.sbuf_tensor("hqT", [128, 22, 128], BF16))
        hbuf = fen(nc.sbuf_tensor("hbuf", [128, 2, 512], F32))
        sgl = fen(nc.sbuf_tensor("sgl", [128, 2, 512], F32))


        tlw = [p2_done["tensor"], cc_tok]
        for cc in range(8):
            P.emit("sync", lambda e, cc=cc: e.dma_start(
                out=wgT[:, cc, :], in_=s_wgq[:, 128 * cc:128 * (cc + 1)],
                transpose=True), waits=tlw if cc == 0 else (), dma=True)
        for cc in range(8):
            P.emit("sync", lambda e, cc=cc: e.dma_start(
                out=wuT[:, cc, :], in_=s_wuq[:, 128 * cc:128 * (cc + 1)],
                transpose=True), dma=True)
        wgu_tok = P.tok("sync")
        xqT_toks = [None] * OWN
        dn_toks = [None] * OWN

        def emit_xqT(t):
            for cc in range(8):
                P.emit("sync", lambda e, t=t, cc=cc: e.dma_start(
                    out=xqTt[:, t % 2, cc, :],
                    in_=s_xq[128 * t:128 * (t + 1), 128 * cc:128 * (cc + 1)],
                    transpose=True),
                    waits=([xq_spills[t], dn_toks[t - 2] if t >= 2 else None]
                           if cc == 0 else ()), dma=True)
            xqT_toks[t] = P.tok("sync")

        emit_xqT(0)
        emit_xqT(1)
        for ic in range(22):
            P.emit("sync", lambda e, ic=ic: e.dma_start(
                out=wdT[:, ic, :], in_=s_wdq[:, 128 * ic:128 * (ic + 1)],
                transpose=True), dma=True)
        wT_tok = P.tok("sync")

        ICH = [(0, 512), (512, 1024), (1024, 1536), (1536, 2048),
               (2048, 2560), (2560, FF)]
        frt = [None] * OWN

        def fin_ffn(head, tp):
            ot = emit_ln_tail(head, outt[:],
                              dst_tok=out_toks[tp - 1] if tp >= 1 else None)
            P.emit("sync", lambda e, tp=tp: e.dma_start(
                out=d_out[128 * tp:128 * (tp + 1), :], in_=outt[:]),
                waits=[ot], dma=True)
            out_toks[tp] = P.tok("sync")
        last_silu = [None, None]
        last_h = [None, None]
        prev_cp = None
        prev_trts = [None, None, None]
        for t in range(OWN if phlim >= 3 else 0):
            if t >= 2:
                emit_xqT(t)
            vq_toks = []
            dn_half = [None, None]

            def emit_tgrp(grp, t=t, vq_toks=vq_toks, dn_half=dn_half):
                nonlocal prev_cp
                vq_gate = vq_toks[2 * grp + 1]
                ics = list(range(8 * grp, min(22, 8 * grp + 8)))
                for ii, ic in enumerate(ics):
                    P.emit("tensor", lambda e, ic=ic, ii=ii: e.transpose(
                        pst[:, 128 * ii:128 * (ii + 1)],
                        hq[:, 128 * ic:128 * (ic + 1)], idn[:]),
                        waits=([prev_cp, vq_gate] if ii == 0 else ()),
                        mark=(ii == len(ics) - 1))
                trt = P.tok("tensor")
                prev_trts[grp] = trt
                prev_cp = P.emit(
                    "scalar", lambda e, grp=grp, n=len(ics): e.copy(
                        hqT[:, 8 * grp:8 * grp + n, :],
                        pst[:, 0:128 * n].rearrange("p (i q) -> p i q", q=128)),
                    waits=[trt], mark=True)
                for half in range(2):
                    for ic in ics:
                        P.emit("tensor", lambda e, ic=ic, half=half: e.matmul(
                            ps[:, 4 + half, :], hqT[:, ic, :],
                            wdT[:, ic, 512 * half:512 * (half + 1)],
                            start=(ic == 0), stop=(ic == 21)),
                            waits=([prev_cp,
                                    wT_tok if (t == 0 and grp == 0)
                                    else None] if ic == ics[0] else ()),
                            mark=(grp == 2 and ic == 21))
                    if grp == 2:
                        dn_half[half] = P.tok("tensor")
            for ici, (f0, f1) in enumerate(ICH):
                fw = f1 - f0
                bg = ici % 2
                for cc in range(8):
                    P.emit("tensor", lambda e, t=t, cc=cc, f0=f0, f1=f1, bg=bg: e.matmul(
                        ps[:, 2 * bg, 0:f1 - f0], xqTt[:, t % 2, cc, :],
                        wgT[:, cc, f0:f1], start=(cc == 0), stop=(cc == 7)),
                        waits=([xqT_toks[t], wgu_tok, last_silu[bg],
                                last_h[bg]] +
                               ([p2_done["vector"], p2_done["scalar"],
                                 p2_done["gpsimd"]]
                                if (t == 0 and ici == 0) else [])
                               if cc == 0 else ()))
                for cc in range(8):
                    P.emit("tensor", lambda e, t=t, cc=cc, f0=f0, f1=f1, bg=bg: e.matmul(
                        ps[:, 2 * bg + 1, 0:f1 - f0], xqTt[:, t % 2, cc, :],
                        wuT[:, cc, f0:f1], start=(cc == 0), stop=(cc == 7)),
                        mark=(cc == 7))
                gu = P.tok("tensor")
                sg_t = P.emit("scalar", lambda e, bg=bg, fw=fw, sv=ici % 2: e.activation(
                    sgl[:, sv, 0:fw], ps[:, 2 * bg, 0:fw], AF.Silu),
                    waits=[gu, last_h[bg]], mark=True)
                last_silu[bg] = sg_t
                htok = P.emit("vector", lambda e, bg=bg, fw=fw, sv=ici % 2: e.tensor_tensor(
                    hbuf[:, sv, 0:fw], sgl[:, sv, 0:fw],
                    ps[:, 2 * bg + 1, 0:fw],
                    op=ALU.mult), waits=[sg_t], mark=True)
                last_h[bg] = htok
                emit_quant_v(hbuf[:, ici % 2, 0:fw], hq[:, f0:f1], fw,
                             src_tok=htok, dst_tok=prev_trts[ici // 2])
                vq_toks.append(P.tok("vector"))
                if ici == 0 and pend_ln[0] is not None:
                    fin_ffn(*pend_ln[0])
                    pend_ln[0] = None
                if ici in (3, 5):
                    emit_tgrp({3: 0, 5: 1}[ici])
            emit_tgrp(2)
            dn = P.tok("tensor")
            dn_toks[t] = dn
            P.emit("sync", lambda e, t=t: e.dma_start(
                out=xjit[:, t % 2, :], in_=s_x1[128 * t:128 * (t + 1), :]),
                waits=[x1_spills[t], p2_done["vector"] if t < 2 else None,
                       frt[t - 2] if t >= 2 else None], dma=True)
            xj = P.tok("sync")
            P.emit("vector", lambda e, t=t: e.tensor_tensor(
                r1[:, t % 2, 0:512], ps[:, 4, :],
                xjit[:, t % 2, 0:512], op=ALU.add),
                waits=[dn_half[0], xj, lnst["sq"][t % 2]], mark=True)
            rt = P.emit("vector", lambda e, t=t: e.tensor_tensor(
                r1[:, t % 2, 512:1024], ps[:, 5, :],
                xjit[:, t % 2, 512:1024], op=ALU.add),
                waits=[dn_half[1]], mark=True)
            frt[t] = rt
            head = emit_ln_head(r1[:, t % 2, :], rt, t % 2)
            pend_ln[0] = (head, t)
        if pend_ln[0] is not None:
            fin_ffn(*pend_ln[0])
            pend_ln[0] = None
        if phlim < 3:
            dz = P.emit("vector", lambda e: e.memset(outt[:], 0.0),
                        waits=[P.tok("tensor"), P.tok("scalar"),
                               P.tok("gpsimd"), P.tok("sync")], mark=True)
            P.emit("sync", lambda e: e.dma_start(
                out=d_out[0:128, :], in_=outt[:]), waits=[dz], dma=True)
        fin = P.tok("sync")
        for e in ENGS:
            P.emit(e, lambda en_: en_.nop(), waits=[fin])

    P.finalize()
    blk = en(nc.Block())

    @blk.tensor
    def _(e):
        P.replay("tensor", e)

    @blk.vector
    def _(e):
        P.replay("vector", e)

    @blk.scalar
    def _(e):
        P.replay("scalar", e)

    @blk.gpsimd
    def _(e):
        P.replay("gpsimd", e)

    @blk.sync
    def _(e):
        P.replay("sync", e)

    ctx.close()
    return nc


_NC_CACHE = {}


def _get_nc():
    if "nc" not in _NC_CACHE:
        _NC_CACHE["nc"] = build_program()
    return _NC_CACHE["nc"]


def kernel(L0_attn_x, L0_attn_W_attn, L0_attn_W_proj, L0_ffn_w_gate,
           L0_ffn_w_up, L0_ffn_w_down):
    x = np.ascontiguousarray(np.asarray(L0_attn_x, dtype=np.float32))
    wa = np.ascontiguousarray(np.asarray(L0_attn_W_attn, dtype=np.float32))
    wp = np.ascontiguousarray(np.asarray(L0_attn_W_proj, dtype=np.float32))
    wg = np.ascontiguousarray(np.asarray(L0_ffn_w_gate, dtype=np.float32))
    wu = np.ascontiguousarray(np.asarray(L0_ffn_w_up, dtype=np.float32))
    wd = np.ascontiguousarray(np.asarray(L0_ffn_w_down, dtype=np.float32))
    dec = [_decay_table(p) for p in range(2)]
    idm = np.ascontiguousarray(np.eye(128, dtype=ml_dtypes.bfloat16))
    msk = [_masks(p) for p in range(2)]
    in_maps = []
    for c in range(8):
        b, p = c // 2, c % 2
        xb = np.ascontiguousarray(x[b])
        own = np.ascontiguousarray(
            xb.reshape(NT, 128, C)[p::2].reshape(OWN * 128, C))
        in_maps.append({
            "x_full": xb, "x_own": own, "w_attn": wa, "w_proj": wp,
            "w_gate_sh": np.ascontiguousarray(wg[352 * c:352 * (c + 1)]),
            "w_up_sh": np.ascontiguousarray(wu[352 * c:352 * (c + 1)]),
            "w_down_sh": np.ascontiguousarray(wd[128 * c:128 * (c + 1)]),
            "decay_tab": dec[p], "mask_a": msk[p][0], "mask_b": msk[p][1],
            "ident": idm,
        })
    nc = _get_nc()
    res = run_bass_kernel_spmd(nc, in_maps, core_ids=list(range(8)))
    out = np.empty((B, T, C), dtype=np.float32)
    for c in range(8):
        b, p = c // 2, c % 2
        o = np.asarray(res.results[c]["out"]).reshape(OWN, 128, C)
        out.reshape(B, NT, 128, C)[b, p::2] = o
    return out
